# version 02
# baseline (speedup 1.0000x reference)
import math
import numpy as np
from contextlib import ExitStack

import concourse.bass as bass
import concourse.mybir as mybir
from concourse.expressions import RuntimeValue
from concourse.bass_utils import run_bass_kernel_spmd

F32 = mybir.dt.float32
BF16 = mybir.dt.bfloat16
AF = mybir.ActivationFunctionType
ALU = mybir.AluOpType
AX = mybir.AxisListType.X

NCORE = 8
BATCH = 2
SEQ = 4096
DM = 4096
NH = 32
HD = 128
HID = 11008
RK = 16
TOK = BATCH * SEQ
TPC = TOK // NCORE
HPC = NH // NCORE
NIC = DM // 128
NHC = HID // 128
SCALE = 1.0 / math.sqrt(HD)
EPS = 1e-5
NEGBIG = -3.0e38
SEGS = [(0, 2048, 4), (2048, 3072, 6), (3072, 4096, 8)]
DBG = ()


class Dep:
    __slots__ = ("w", "readers")

    def __init__(self):
        self.w = None
        self.readers = []


class Tr:
    def __init__(self, nc, ctx):
        self.nc = nc
        self.eng = {"te": nc.tensor, "ve": nc.vector, "se": nc.scalar,
                    "pe": nc.gpsimd, "sy": nc.sync}
        self.sems = {}
        self.cnt = {}
        for name in ("te", "ve", "se", "pe", "qs", "qa", "qg", "cc"):
            self.sems[name] = ctx.enter_context(nc.semaphore("sem_" + name))
            self.cnt[name] = 0
        self.wm = {}

    def _wait(self, en, need):
        e = self.eng[en]
        for s, v in sorted(need.items()):
            if v > 0 and self.wm.get((en, s), 0) < v:
                e.wait_ge(self.sems[s], v)
                self.wm[(en, s)] = v

    def op(self, en, factory, reads=(), writes=(), sem=None, step=1):
        need = {}
        for d in reads:
            if d.w:
                s, v = d.w
                if need.get(s, 0) < v:
                    need[s] = v
        for d in writes:
            if d.w:
                s, v = d.w
                if need.get(s, 0) < v:
                    need[s] = v
            for s, v in d.readers:
                if need.get(s, 0) < v:
                    need[s] = v
        self._wait(en, need)
        inst = factory(self.eng[en])
        s = sem if sem is not None else en
        inst.then_inc(self.sems[s], step)
        self.cnt[s] += step
        stamp = (s, self.cnt[s])
        for d in writes:
            d.w = stamp
            d.readers = []
        for d in reads:
            d.readers.append(stamp)
        return inst

    def op_multi(self, en, factories, reads=(), writes=(), sem=None,
                 step=1):
        need = {}
        for d in reads:
            if d.w:
                s, v = d.w
                if need.get(s, 0) < v:
                    need[s] = v
        for d in writes:
            if d.w:
                s, v = d.w
                if need.get(s, 0) < v:
                    need[s] = v
            for s, v in d.readers:
                if need.get(s, 0) < v:
                    need[s] = v
        self._wait(en, need)
        e = self.eng[en]
        for f in factories[:-1]:
            f(e)
        inst = factories[-1](e)
        s = sem if sem is not None else en
        inst.then_inc(self.sems[s], step)
        self.cnt[s] += step
        stamp = (s, self.cnt[s])
        for d in writes:
            d.w = stamp
            d.readers = []
        for d in reads:
            d.readers.append(stamp)
        return inst

    def dma(self, queue, out, in_, reads=(), writes=(), transpose=False):
        en = {"qs": "sy", "qa": "se", "qg": "pe"}[queue]
        if transpose:
            fac = lambda e: e.dma_start(out=out, in_=in_, transpose=True)
        else:
            fac = lambda e: e.dma_start(out=out, in_=in_)
        return self.op(en, fac, reads=reads, writes=writes, sem=queue,
                       step=16)

    def wait_all(self, en):
        self._wait(en, dict(self.cnt))

    def fence(self):
        for en in ("te", "ve", "se", "pe", "sy"):
            self._wait(en, dict(self.cnt))


def nc_sb(nc, ctx, name, shape, dtype=F32):
    return ctx.enter_context(nc.sbuf_tensor(name, shape, dtype))


def nc_ps(nc, ctx, name, shape, dtype=F32):
    return ctx.enter_context(nc.psum_tensor(name, shape, dtype))


def emit_split(tr, en, src_ap, d_src, hi_ap, d_hi, lo_ap, d_lo):
    tr.op(en, lambda e: e.tensor_copy(hi_ap, src_ap),
          reads=[d_src], writes=[d_hi])
    tr.op(en, lambda e: e.tensor_tensor(lo_ap, src_ap, hi_ap, ALU.subtract),
          reads=[d_src, d_hi], writes=[d_lo])


class QuantKit:
    def __init__(self, nc, ctx):
        self.amax = nc_sb(nc, ctx, "qk_amax", [128, 128])
        self.rs = nc_sb(nc, ctx, "qk_rs", [128, 128])
        self.sneg = nc_sb(nc, ctx, "qk_sneg", [128, 128])
        self.u = nc_sb(nc, ctx, "qk_u", [128, 2048])
        self.c = nc_sb(nc, ctx, "qk_c", [128, 2048])
        self.stats = nc_sb(nc, ctx, "qk_stats", [128, 64])
        self.mv = nc_sb(nc, ctx, "qk_mv", [128, 8])
        (self.d_amax, self.d_rs, self.d_sneg, self.d_u, self.d_c,
         self.d_stats, self.d_mv) = (Dep() for _ in range(7))


def emit_quant(tr, qk, t_in, d_in, t_dq, d_dq):
    for (s0, s1, bits) in SEGS:
        w = s1 - s0
        G = w // 32
        maxq = float(2 ** (bits - 1) - 1)
        shift = float(2 ** (bits - 1))
        src3 = t_in.ap()[:, s0:s1].rearrange("p (g k) -> p g k", k=32)
        tr.op("ve", lambda e, _s=src3, _G=G: e.tensor_reduce(
            qk.amax[:, :_G], _s, AX, ALU.max, apply_absolute_value=True),
            reads=[d_in], writes=[qk.d_amax])
        tr.op("ve", lambda e, _G=G: e.tensor_single_scalar(
            qk.amax[:, :_G], qk.amax[:, :_G], 1e-12, ALU.max),
            reads=[qk.d_amax], writes=[qk.d_amax])
        tr.op("ve", lambda e, _G=G: e.reciprocal(
            qk.rs[:, :_G], qk.amax[:, :_G]),
            reads=[qk.d_amax], writes=[qk.d_rs])
        rs_b = qk.rs.ap()[:, :G].unsqueeze(2).broadcast_to([128, G, 32])
        u3 = qk.u.ap()[:, :w].rearrange("p (g k) -> p g k", k=32)
        tr.op("ve", lambda e, _s=src3, _m=maxq, _r=rs_b, _u=u3:
              e.scalar_tensor_tensor(_u, _s, _m, _r, ALU.mult, ALU.mult),
              reads=[d_in, qk.d_rs], writes=[qk.d_u])
        tr.op("ve", lambda e, _a=s0, _b=s1, _w=w: e.tensor_copy(
            t_dq[:, _a:_b], qk.u[:, :_w]),
            reads=[qk.d_u], writes=[d_dq])
        for i in range(bits):
            bbit = float(2 ** (bits - 1 - i))
            thr = bbit - 0.5 - shift
            tr.op("ve", lambda e, _t=thr, _w=w: e.tensor_single_scalar(
                qk.c[:, :_w], qk.u[:, :_w], _t, ALU.is_ge),
                reads=[qk.d_u], writes=[qk.d_c])
            tr.op("ve", lambda e, _b=bbit, _w=w: e.scalar_tensor_tensor(
                qk.u[:, :_w], qk.c[:, :_w], -_b, qk.u[:, :_w],
                ALU.mult, ALU.add),
                reads=[qk.d_c, qk.d_u], writes=[qk.d_u])
        tr.op("ve", lambda e, _s=shift, _w=w, _a=s0, _b=s1:
              e.scalar_tensor_tensor(qk.c[:, :_w], qk.u[:, :_w], _s,
                                     t_dq[:, _a:_b], ALU.add, ALU.subtract),
              reads=[qk.d_u, d_dq], writes=[qk.d_c])
        tr.op("ve", lambda e, _G=G, _m=maxq: e.tensor_single_scalar(
            qk.sneg[:, :_G], qk.amax[:, :_G], -1.0 / _m, ALU.mult),
            reads=[qk.d_amax], writes=[qk.d_sneg])
        sneg_b = qk.sneg.ap()[:, :G].unsqueeze(2).broadcast_to([128, G, 32])
        dq3 = t_dq.ap()[:, s0:s1].rearrange("p (g k) -> p g k", k=32)
        c3 = qk.c.ap()[:, :w].rearrange("p (g k) -> p g k", k=32)
        tr.op("ve", lambda e, _d=dq3, _c=c3, _sb=sneg_b: e.tensor_tensor(
            _d, _c, _sb, ALU.mult),
            reads=[qk.d_c, qk.d_sneg], writes=[d_dq])


def emit_ln(tr, qk, t_x, d_x, t_out, d_out):
    nch = DM // 512
    for c in range(nch):
        tr.op("ve", lambda e, _c=c: e.bn_stats(
            qk.stats[:, 6 * _c:6 * _c + 6],
            t_x[:, 512 * _c:512 * (_c + 1)]),
            reads=[d_x], writes=[qk.d_stats])
    tr.op("ve", lambda e, _n=nch: e.bn_aggr(
        qk.mv[:, 0:2], qk.stats[:, 0:6 * _n]),
        reads=[qk.d_stats], writes=[qk.d_mv])
    tr.op("ve", lambda e: e.tensor_single_scalar(
        qk.mv[:, 2:3], qk.mv[:, 1:2], EPS, ALU.add),
        reads=[qk.d_mv], writes=[qk.d_mv])
    tr.op("ve", lambda e: e.reciprocal(qk.mv[:, 3:4], qk.mv[:, 2:3]),
          reads=[qk.d_mv], writes=[qk.d_mv])
    tr.op("se", lambda e: e.activation(qk.mv[:, 4:5], qk.mv[:, 3:4], AF.Sqrt),
          reads=[qk.d_mv], writes=[qk.d_mv])
    tr.op("ve", lambda e: e.tensor_scalar(
        t_out[:], t_x[:], qk.mv.ap()[:, 0:1], qk.mv.ap()[:, 4:5],
        ALU.subtract, ALU.mult),
        reads=[d_x, qk.d_mv], writes=[d_out])


def mm3f(pst, lh, ll, rh, rl, first, last):
    out = []
    combos = ((lh, rh), (lh, rl), (ll, rh))
    for li, (lw, rx) in enumerate(combos):
        out.append(lambda e, _l=lw, _r=rx, _s=(first and li == 0),
                   _e=(last and li == 2), _p=pst:
                   e.matmul(_p, _l, _r, start=_s, stop=_e))
    return out


def build_nc():
    nc = bass.Bass("TRN2", target_bir_lowering=False, debug=False,
                   num_devices=NCORE)

    def din(name, shape, dtype=F32):
        return nc.dram_tensor(name, shape, dtype, kind="ExternalInput").ap()

    def dint(name, shape, dtype):
        return nc.dram_tensor(name, shape, dtype, kind="Internal").ap()

    io = {}
    io["x_my"] = din("x_my", [TPC, DM])
    for nm in ("q", "k", "v", "o"):
        io["w%s_my" % nm] = din("w%s_my" % nm, [HPC * HD, DM])
    io["w1"] = din("w1", [HID, DM])
    io["w3"] = din("w3", [HID, DM])
    io["w2"] = din("w2", [DM, HID])
    for nm in ("a1", "b1", "a3", "b3", "a2", "b2"):
        shp = {"a1": [RK, DM], "a3": [RK, DM], "a2": [RK, HID],
               "b1": [HID, RK], "b3": [HID, RK], "b2": [DM, RK]}[nm]
        io[nm] = din(nm, shp)
    io["cosF"] = din("cosF", [128, SEQ])
    io["sinF2"] = din("sinF2", [128, SEQ])
    io["maskS"] = din("maskS", [128, 896])
    io["identf"] = din("identf", [128, 128])
    io["permF"] = din("permF", [128, 128])
    io["out"] = nc.dram_tensor("out", [TPC, DM], F32,
                               kind="ExternalOutput").ap()

    sc = {}
    sc["contrib1"] = dint("contrib1", [DM, 3072], BF16)
    sc["gath1"] = nc.dram_tensor("gath1", [NCORE * DM, 3072], BF16, kind="Internal", addr_space="Shared").ap()
    sc["contrib2"] = dint("contrib2", [TOK, HPC * HD], F32)
    sc["gath2"] = nc.dram_tensor("gath2", [NCORE * TOK, HPC * HD], F32, kind="Internal", addr_space="Shared").ap()
    sc["xq_hi_n"] = dint("xq_hi_n", [TPC, DM], BF16)
    sc["xq_lo_n"] = dint("xq_lo_n", [TPC, DM], BF16)
    for nm in ("q", "k", "v", "o"):
        for h in ("hi", "lo"):
            sc["wn_%s_%s" % (nm, h)] = dint("wn_%s_%s" % (nm, h),
                                            [HPC * HD, DM], BF16)
    for nm in ("q", "k", "v"):
        for h in ("hi", "lo"):
            sc["wT_%s_%s" % (nm, h)] = dint("wT_%s_%s" % (nm, h),
                                            [HPC, 128, DM], BF16)
    for t in ("q", "k"):
        for h in ("hi", "lo"):
            sc["%ss_%s" % (t, h)] = dint("%ss_%s" % (t, h),
                                         [HPC * HD, TOK], BF16)
    sc["vs"] = dint("vs", [TOK, HPC * HD], F32)
    for wnm, rows, cols in (("w1", HID, DM), ("w3", HID, DM),
                            ("w2", DM, HID)):
        for h in ("hi", "lo"):
            sc["%sn_%s" % (wnm, h)] = dint("%sn_%s" % (wnm, h),
                                           [rows, cols], BF16)
    sc["x1_s"] = dint("x1_s", [TPC, DM], F32)
    for h in ("hi", "lo"):
        sc["xn2n_%s" % h] = dint("xn2n_%s" % h, [TPC, DM], BF16)
        sc["xn2T_%s" % h] = dint("xn2T_%s" % h, [128, NIC * TPC], BF16)
        sc["odqn_%s" % h] = dint("odqn_%s" % h, [TPC, DM], BF16)
        sc["odqT_%s" % h] = dint("odqT_%s" % h, [128, NIC * TPC], BF16)
        sc["gn_%s" % h] = dint("gn_%s" % h, [TPC, HID], BF16)
        sc["gT_%s" % h] = dint("gT_%s" % h, [128, NHC * TPC], BF16)
        sc["b1T_%s" % h] = dint("b1T_%s" % h, [RK, HID], BF16)
        sc["b3T_%s" % h] = dint("b3T_%s" % h, [RK, HID], BF16)
        sc["b2T_%s" % h] = dint("b2T_%s" % h, [RK, DM], BF16)

    dbg = {}
    for nm in DBG:
        src = sc[nm]
        dbg[nm] = nc.dram_tensor("dbg_" + nm, list(src.shape), src.dtype,
                                 kind="ExternalOutput").ap()

    with ExitStack() as ctx:
        tr = Tr(nc, ctx)
        dep = {k: Dep() for k in (
            "xqh", "xql", "c1", "g1", "qk", "vs", "c2", "g2", "wT",
            "w1n", "w3n", "w2n", "x1", "xn2n", "xn2T", "odqn", "odqT",
            "gn", "gT", "out", "b1T", "b3T", "b2T")}
        for nm in ("q", "k", "v", "o"):
            dep["wn_" + nm] = Dep()
        qk = QuantKit(nc, ctx)

        s_ident = nc_sb(nc, ctx, "s_ident", [128, 128])
        d_ident = Dep()
        tr.dma("qs", s_ident[:], io["identf"], writes=[d_ident])
        s_perm = nc_sb(nc, ctx, "s_perm", [128, 128])
        d_perm = Dep()
        tr.dma("qs", s_perm[:], io["permF"], writes=[d_perm])
        s_mask = nc_sb(nc, ctx, "s_mask", [128, 896])
        d_mask = Dep()
        tr.dma("qs", s_mask[:], io["maskS"], writes=[d_mask])

        phase0(nc, tr, io, sc, dep, qk)
        tr.fence()
        tr.op("pe", lambda e: e.collective_compute(
            "AllGather", ALU.bypass, replica_groups=[list(range(NCORE))],
            ins=[sc["contrib1"]], outs=[sc["gath1"]]),
            reads=[dep["c1"]], writes=[dep["g1"]], sem="cc")
        phase_ffn_split(nc, tr, io, sc, dep)
        tr.fence()
        phase_qkv(nc, tr, io, sc, dep, s_perm, d_perm, s_ident, d_ident)
        tr.fence()
        phase_attn(nc, tr, io, sc, dep, s_ident, d_ident, s_mask, d_mask)
        tr.fence()
        tr.op("pe", lambda e: e.collective_compute(
            "AllGather", ALU.bypass, replica_groups=[list(range(NCORE))],
            ins=[sc["contrib2"]], outs=[sc["gath2"]]),
            reads=[dep["c2"]], writes=[dep["g2"]], sem="cc")
        tr.fence()
        phase_tail(nc, tr, io, sc, dep, qk, s_ident, d_ident)

        for nm in DBG:
            tr.wait_all("pe")
            tr.dma("qs", dbg[nm], sc[nm])
        tr.wait_all("pe")
        tr.wait_all("sy")
        tr.wait_all("se")
    return nc


def phase0(nc, tr, io, sc, dep, qk):
    with ExitStack() as c0:
        s_xt = nc_sb(nc, c0, "p0_xt", [128, DM])
        s_xn = nc_sb(nc, c0, "p0_xn", [128, DM])
        s_dq = nc_sb(nc, c0, "p0_dq", [128, DM])
        s_hi = nc_sb(nc, c0, "p0_hi", [128, DM], BF16)
        s_lo = nc_sb(nc, c0, "p0_lo", [128, DM], BF16)
        d_xt, d_xn, d_dq, d_hi, d_lo = (Dep() for _ in range(5))
        for tb in range(TPC // 128):
            tr.dma("qs", s_xt[:], io["x_my"][tb * 128:(tb + 1) * 128, :],
                   writes=[d_xt])
            emit_ln(tr, qk, s_xt, d_xt, s_xn, d_xn)
            emit_quant(tr, qk, s_xn, d_xn, s_dq, d_dq)
            emit_split(tr, "ve", s_dq[:], d_dq, s_hi[:], d_hi, s_lo[:], d_lo)
            tr.dma("qs", sc["xq_hi_n"][tb * 128:(tb + 1) * 128, :], s_hi[:],
                   reads=[d_hi], writes=[dep["xqh"]])
            tr.dma("qs", sc["xq_lo_n"][tb * 128:(tb + 1) * 128, :], s_lo[:],
                   reads=[d_lo], writes=[dep["xql"]])
        for nm in ("q", "k", "v", "o"):
            dn = dep["wn_" + nm]
            for r in range(HPC * HD // 128):
                tr.dma("qs", s_xt[:],
                       io["w%s_my" % nm][r * 128:(r + 1) * 128, :],
                       writes=[d_xt])
                emit_quant(tr, qk, s_xt, d_xt, s_dq, d_dq)
                emit_split(tr, "ve", s_dq[:], d_dq, s_hi[:], d_hi,
                           s_lo[:], d_lo)
                tr.dma("qs", sc["wn_%s_hi" % nm][r * 128:(r + 1) * 128, :],
                       s_hi[:], reads=[d_hi], writes=[dn])
                tr.dma("qs", sc["wn_%s_lo" % nm][r * 128:(r + 1) * 128, :],
                       s_lo[:], reads=[d_lo], writes=[dn])

    tr.fence()
    with ExitStack() as c1:
        s_tx = nc_sb(nc, c1, "p0_tx", [128, TPC], BF16)
        d_tx = Dep()
        for half, coff in (("hi", 0), ("lo", TPC)):
            src = sc["xq_%s_n" % half]
            rdep = dep["xqh"] if half == "hi" else dep["xql"]
            for ic in range(NIC):
                tr.dma("qa", s_tx[:], src[:, ic * 128:(ic + 1) * 128],
                       reads=[rdep], writes=[d_tx], transpose=True)
                tr.dma("qs",
                       sc["contrib1"][ic * 128:(ic + 1) * 128,
                                      coff:coff + TPC],
                       s_tx[:], reads=[d_tx], writes=[dep["c1"]])
        s_tw = nc_sb(nc, c1, "p0_tw", [128, HD], BF16)
        d_tw = Dep()
        for nm in ("q", "k", "v"):
            for half in ("hi", "lo"):
                for h in range(HPC):
                    for ic in range(NIC):
                        tr.dma("qa", s_tw[:],
                               sc["wn_%s_%s" % (nm, half)][
                                   h * HD:(h + 1) * HD,
                                   ic * 128:(ic + 1) * 128],
                               reads=[dep["wn_" + nm]], writes=[d_tw],
                               transpose=True)
                        tr.dma("qs",
                               sc["wT_%s_%s" % (nm, half)][
                                   h, :, ic * 128:(ic + 1) * 128],
                               s_tw[:], reads=[d_tw], writes=[dep["wT"]])
        s_to = nc_sb(nc, c1, "p0_to", [128, HPC * HD], BF16)
        d_to = Dep()
        for hx, half in enumerate(("hi", "lo")):
            coff = 2048 + hx * 512
            for ic in range(NIC):
                tr.dma("qa", s_to[:],
                       sc["wn_o_%s" % half][:, ic * 128:(ic + 1) * 128],
                       reads=[dep["wn_o"]], writes=[d_to], transpose=True)
                tr.dma("qs",
                       sc["contrib1"][ic * 128:(ic + 1) * 128,
                                      coff:coff + 512],
                       s_to[:], reads=[d_to], writes=[dep["c1"]])


def phase_ffn_split(nc, tr, io, sc, dep):
    with ExitStack() as cf:
        bufs = []
        for i in range(2):
            bufs.append((nc_sb(nc, cf, "fs_f%d" % i, [128, HID]),
                         nc_sb(nc, cf, "fs_h%d" % i, [128, HID], BF16),
                         nc_sb(nc, cf, "fs_l%d" % i, [128, HID], BF16),
                         Dep(), Dep(), Dep()))
        pi = 0
        for wnm, rows, cols in (("w1", HID, DM), ("w3", HID, DM),
                                ("w2", DM, HID)):
            for r in range(rows // 128):
                sf, sh, sl, df, dh, dl = bufs[pi % 2]
                pi += 1
                tr.dma("qa", sf[:, :cols],
                       io[wnm][r * 128:(r + 1) * 128, :], writes=[df])
                emit_split(tr, "pe", sf[:, :cols], df, sh[:, :cols], dh,
                           sl[:, :cols], dl)
                tr.dma("qs", sc["%sn_hi" % wnm][r * 128:(r + 1) * 128, :],
                       sh[:, :cols], reads=[dh], writes=[dep["%sn" % wnm]])
                tr.dma("qs", sc["%sn_lo" % wnm][r * 128:(r + 1) * 128, :],
                       sl[:, :cols], reads=[dl], writes=[dep["%sn" % wnm]])


def phase_qkv(nc, tr, io, sc, dep, s_perm, d_perm, s_ident, d_ident):
    WIN = 512
    NWIN = TOK // WIN
    with ExitStack() as c1:
        s_w = {}
        d_w = {}
        for nm in ("q", "k", "v"):
            for half in ("hi", "lo"):
                s_w[nm, half] = nc_sb(nc, c1, "q_w%s%s" % (nm, half),
                                      [128, DM], BF16)
                d_w[nm, half] = Dep()
        s_x = {half: nc_sb(nc, c1, "q_x" + half, [128, NIC * WIN], BF16)
               for half in ("hi", "lo")}
        d_x = {half: Dep() for half in ("hi", "lo")}
        s_cos = nc_sb(nc, c1, "q_cos", [128, WIN])
        s_sin = nc_sb(nc, c1, "q_sin", [128, WIN])
        d_cos, d_sin = Dep(), Dep()
        ps_q = nc_ps(nc, c1, "q_psq", [128, WIN])
        ps_k = nc_ps(nc, c1, "q_psk", [128, WIN])
        ps_v = nc_ps(nc, c1, "q_psv", [128, WIN])
        ps_sw = nc_ps(nc, c1, "q_pssw", [128, WIN])
        ps_t = nc_ps(nc, c1, "q_pst", [128, 128])
        d_psq, d_psk, d_psv, d_pssw, d_pst = (Dep() for _ in range(5))
        s_f = nc_sb(nc, c1, "q_f", [128, WIN])
        d_f = Dep()
        s_a = nc_sb(nc, c1, "q_a", [128, WIN])
        d_a = Dep()
        s_b = nc_sb(nc, c1, "q_b", [128, WIN])
        d_b = Dep()
        s_bh = nc_sb(nc, c1, "q_bh", [128, WIN], BF16)
        s_bl = nc_sb(nc, c1, "q_bl", [128, WIN], BF16)
        d_bh, d_bl = Dep(), Dep()
        s_ve = nc_sb(nc, c1, "q_ve", [128, WIN])
        d_ve = Dep()
        s_vt = nc_sb(nc, c1, "q_vt", [128, 128])
        d_vt = Dep()

        for h in range(HPC):
            for nm in ("q", "k", "v"):
                for half in ("hi", "lo"):
                    tr.dma("qs", s_w[nm, half][:],
                           sc["wT_%s_%s" % (nm, half)][h],
                           reads=[dep["wT"]], writes=[d_w[nm, half]])
            for w in range(NWIN):
                slot, coff = w // 2, (w % 2) * WIN
                seqpos = (w % (SEQ // WIN)) * WIN
                for half, dd in (("hi", 0), ("lo", TPC)):
                    for ic in range(NIC):
                        tr.dma("qs",
                               s_x[half][:, ic * WIN:(ic + 1) * WIN],
                               sc["gath1"][slot * DM + ic * 128:
                                           slot * DM + (ic + 1) * 128,
                                           dd + coff:dd + coff + WIN],
                               reads=[dep["g1"]], writes=[d_x[half]])
                tr.dma("qs", s_cos[:], io["cosF"][:, seqpos:seqpos + WIN],
                       writes=[d_cos])
                tr.dma("qs", s_sin[:], io["sinF2"][:, seqpos:seqpos + WIN],
                       writes=[d_sin])
                for nm, pst, dps in (("q", ps_q, d_psq), ("k", ps_k, d_psk),
                                     ("v", ps_v, d_psv)):
                    facs = []
                    for ic in range(NIC):
                        facs += mm3f(pst[:],
                            s_w[nm, "hi"].ap()[:, ic * 128:(ic + 1) * 128],
                            s_w[nm, "lo"].ap()[:, ic * 128:(ic + 1) * 128],
                            s_x["hi"].ap()[:, ic * WIN:(ic + 1) * WIN],
                            s_x["lo"].ap()[:, ic * WIN:(ic + 1) * WIN],
                            ic == 0, ic == NIC - 1)
                    tr.op_multi("te", facs,
                                reads=[d_w[nm, "hi"], d_w[nm, "lo"],
                                       d_x["hi"], d_x["lo"]],
                                writes=[dps])
                for nm, pst, dps, pre in (("q", ps_q, d_psq, "qs"),
                                          ("k", ps_k, d_psk, "ks")):
                    tr.op("ve", lambda e, _p=pst: e.tensor_copy(
                        s_f[:], _p[:]), reads=[dps], writes=[d_f])
                    tr.op("te", lambda e: e.matmul(
                        ps_sw[:], s_perm[:], s_f[:], start=True, stop=True),
                        reads=[d_perm, d_f], writes=[d_pssw])
                    tr.op("ve", lambda e: e.tensor_tensor(
                        s_a[:], s_f.ap()[:, :], s_cos.ap()[:, :], ALU.mult),
                        reads=[d_f, d_cos], writes=[d_a])
                    tr.op("ve", lambda e: e.tensor_tensor(
                        s_b[:], ps_sw.ap()[:, :], s_sin.ap()[:, :],
                        ALU.mult),
                        reads=[d_pssw, d_sin], writes=[d_b])
                    tr.op("ve", lambda e: e.tensor_tensor(
                        s_f[:], s_a.ap()[:, :], s_b.ap()[:, :], ALU.add),
                        reads=[d_a, d_b], writes=[d_f])
                    emit_split(tr, "ve", s_f[:], d_f, s_bh[:], d_bh,
                               s_bl[:], d_bl)
                    tr.dma("qs", sc[pre + "_hi"][h * HD:(h + 1) * HD,
                                                 w * WIN:(w + 1) * WIN],
                           s_bh[:], reads=[d_bh], writes=[dep["qk"]])
                    tr.dma("qs", sc[pre + "_lo"][h * HD:(h + 1) * HD,
                                                 w * WIN:(w + 1) * WIN],
                           s_bl[:], reads=[d_bl], writes=[dep["qk"]])
                tr.op("ve", lambda e: e.tensor_copy(s_ve[:], ps_v[:]),
                      reads=[d_psv], writes=[d_ve])
                for j in range(WIN // 128):
                    tr.op("te", lambda e, _j=j: e.transpose(
                        ps_t[:], s_ve.ap()[:, _j * 128:(_j + 1) * 128],
                        s_ident[:]),
                        reads=[d_ve, d_ident], writes=[d_pst])
                    tr.op("se", lambda e: e.copy(s_vt[:], ps_t[:]),
                          reads=[d_pst], writes=[d_vt])
                    tr.dma("qs",
                           sc["vs"][w * WIN + j * 128:
                                    w * WIN + (j + 1) * 128,
                                    h * HD:(h + 1) * HD],
                           s_vt[:], reads=[d_vt], writes=[dep["vs"]])


def phase_attn(nc, tr, io, sc, dep, s_ident, d_ident, s_mask, d_mask):
    KW = 512
    with ExitStack() as ca:
        s_k = {}
        d_k = {}
        for h in range(HPC):
            for half in ("hi", "lo"):
                s_k[h, half] = nc_sb(nc, ca, "a_k%d%s" % (h, half),
                                     [128, SEQ], BF16)
                d_k[h, half] = Dep()
        s_v = nc_sb(nc, ca, "a_v", [128, (SEQ // 128) * HPC * HD])
        d_v = Dep()
        s_q = {half: nc_sb(nc, ca, "a_q" + half, [128, HPC * HD], BF16)
               for half in ("hi", "lo")}
        d_q = {half: Dep() for half in ("hi", "lo")}
        s_S = nc_sb(nc, ca, "a_S", [128, SEQ])
        d_S = Dep()
        s_P = nc_sb(nc, ca, "a_P", [128, SEQ])
        d_P = Dep()
        s_m = nc_sb(nc, ca, "a_m", [128, 4])
        d_m = Dep()
        s_PT = nc_sb(nc, ca, "a_PT", [128, 256])
        d_PT = [Dep(), Dep()]
        s_stage = nc_sb(nc, ca, "a_stage", [128, HPC * HD])
        d_stage = Dep()
        ps_S = nc_ps(nc, ca, "a_psS", [128, KW])
        d_psS = Dep()
        ps_T = [nc_ps(nc, ca, "a_psT%d" % i, [128, 128]) for i in range(2)]
        d_psT = [Dep(), Dep()]
        ps_o = nc_ps(nc, ca, "a_pso", [128, HD])
        d_pso = Dep()

        for s in range(BATCH):
            for h in range(HPC):
                for half in ("hi", "lo"):
                    tr.dma("qs", s_k[h, half][:],
                           sc["ks_%s" % half][h * HD:(h + 1) * HD,
                                              s * SEQ:(s + 1) * SEQ],
                           reads=[dep["qk"]], writes=[d_k[h, half]])
            for j in range(SEQ // 128):
                tr.dma("qs", s_v[:, j * HPC * HD:(j + 1) * HPC * HD],
                       sc["vs"][s * SEQ + j * 128:s * SEQ + (j + 1) * 128, :],
                       reads=[dep["vs"]], writes=[d_v])
            for qb in range(SEQ // 128):
                nk = qb // 4 + 1
                nkc = nk * 4
                for half in ("hi", "lo"):
                    for h in range(HPC):
                        tr.dma("qs", s_q[half][:, h * HD:(h + 1) * HD],
                               sc["qs_%s" % half][
                                   h * HD:(h + 1) * HD,
                                   s * SEQ + qb * 128:
                                   s * SEQ + qb * 128 + 128],
                               reads=[dep["qk"]], writes=[d_q[half]])
                for h in range(HPC):
                    for kw in range(nk):
                        tr.op_multi("te", mm3f(ps_S[:],
                            s_q["hi"].ap()[:, h * HD:(h + 1) * HD],
                            s_q["lo"].ap()[:, h * HD:(h + 1) * HD],
                            s_k[h, "hi"].ap()[:, kw * KW:(kw + 1) * KW],
                            s_k[h, "lo"].ap()[:, kw * KW:(kw + 1) * KW],
                            True, True),
                            reads=[d_q["hi"], d_q["lo"],
                                   d_k[h, "hi"], d_k[h, "lo"]],
                            writes=[d_psS])
                        if kw == nk - 1:
                            moff = 384 - (qb % 4) * 128
                            tr.op("ve", lambda e, _k=kw, _m=moff:
                                  e.tensor_tensor(
                                      s_S[:, _k * KW:(_k + 1) * KW],
                                      ps_S.ap()[:, :],
                                      s_mask.ap()[:, _m:_m + KW], ALU.add),
                                  reads=[d_psS, d_mask], writes=[d_S])
                        else:
                            tr.op("se", lambda e, _k=kw: e.copy(
                                s_S[:, _k * KW:(_k + 1) * KW], ps_S[:]),
                                reads=[d_psS], writes=[d_S])
                    tr.op("ve", lambda e, _n=nk: e.reduce_max(
                        s_m[:, 0:1], s_S.ap()[:, :_n * KW], AX),
                        reads=[d_S], writes=[d_m])
                    tr.op("ve", lambda e: e.tensor_single_scalar(
                        s_m[:, 1:2], s_m[:, 0:1], -SCALE, ALU.mult),
                        reads=[d_m], writes=[d_m])
                    tr.op("se", lambda e, _n=nk: e.activation(
                        s_P[:, :_n * KW], s_S[:, :_n * KW], AF.Exp,
                        bias=s_m.ap()[:, 1:2], scale=SCALE,
                        accum_out=s_m[:, 2:3]),
                        reads=[d_S, d_m], writes=[d_P, d_m])
                    tr.op("ve", lambda e: e.reciprocal(
                        s_m[:, 3:4], s_m[:, 2:3]),
                        reads=[d_m], writes=[d_m])
                    for kc in range(nkc):
                        pi = kc % 2
                        tr.op("te", lambda e, _k=kc, _p=pi: e.transpose(
                            ps_T[_p][:],
                            s_P.ap()[:, _k * 128:(_k + 1) * 128],
                            s_ident[:]),
                            reads=[d_P, d_ident], writes=[d_psT[pi]])
                        if pi == 0:
                            tr.op("se", lambda e, _p=pi: e.copy(
                                s_PT[:, _p * 128:(_p + 1) * 128],
                                ps_T[_p][:]),
                                reads=[d_psT[pi]], writes=[d_PT[pi]])
                        else:
                            tr.op("ve", lambda e, _p=pi: e.tensor_copy(
                                s_PT[:, _p * 128:(_p + 1) * 128],
                                ps_T[_p][:]),
                                reads=[d_psT[pi]], writes=[d_PT[pi]])
                        tr.op("te", lambda e, _k=kc, _p=pi, _h=h: e.matmul(
                            ps_o[:],
                            s_PT.ap()[:, _p * 128:(_p + 1) * 128],
                            s_v.ap()[:, _k * HPC * HD + _h * HD:
                                     _k * HPC * HD + (_h + 1) * HD],
                            start=(_k == 0), stop=(_k == nkc - 1)),
                            reads=[d_PT[pi], d_v], writes=[d_pso])
                    tr.op("se", lambda e, _h=h: e.activation(
                        s_stage[:, _h * HD:(_h + 1) * HD], ps_o[:], AF.Copy,
                        scale=s_m.ap()[:, 3:4]),
                        reads=[d_pso, d_m], writes=[d_stage])
                tr.dma("qs",
                       sc["contrib2"][s * SEQ + qb * 128:
                                      s * SEQ + qb * 128 + 128, :],
                       s_stage[:], reads=[d_stage], writes=[dep["c2"]])


def phase_lora_prep(nc, tr, ctx, io, sc, dep, s_ident, d_ident):
    lora = {}
    s_tmp = nc_sb(nc, ctx, "lp_tmp", [128, 128])
    d_tmp = Dep()
    t_ps = nc_ps(nc, ctx, "lp_ps", [128, 128])
    d_ps = Dep()
    s_asrc = nc_sb(nc, ctx, "lp_asrc", [RK, 128])
    d_asrc = Dep()
    for nm, dim in (("a1", DM), ("a3", DM), ("a2", HID)):
        nch = dim // 128
        hi = nc_sb(nc, ctx, "lp_%s_hi" % nm, [128, nch * RK], BF16)
        lo = nc_sb(nc, ctx, "lp_%s_lo" % nm, [128, nch * RK], BF16)
        dhi, dlo = Dep(), Dep()
        for c in range(nch):
            tr.dma("qs", s_asrc[:], io[nm][:, c * 128:(c + 1) * 128],
                   writes=[d_asrc])
            tr.op("te", lambda e: e.transpose(
                t_ps[:, :RK], s_asrc.ap()[:, :],
                s_ident.ap()[0:RK, 0:RK]),
                reads=[d_asrc, d_ident], writes=[d_ps])
            tr.op("ve", lambda e: e.tensor_copy(s_tmp[:, :RK], t_ps[:, :RK]),
                  reads=[d_ps], writes=[d_tmp])
            emit_split(tr, "ve", s_tmp[:, :RK], d_tmp,
                       hi[:, c * RK:(c + 1) * RK], dhi,
                       lo[:, c * RK:(c + 1) * RK], dlo)
        lora[nm] = (hi, lo, dhi, dlo)
    t_ps16 = nc_ps(nc, ctx, "lp_ps16", [RK, 128])
    d_ps16 = Dep()
    s_t2 = nc_sb(nc, ctx, "lp_t2", [RK, 128])
    s_t2h = nc_sb(nc, ctx, "lp_t2h", [RK, 128], BF16)
    s_t2l = nc_sb(nc, ctx, "lp_t2l", [RK, 128], BF16)
    d_t2, d_t2h, d_t2l = Dep(), Dep(), Dep()
    s_s2 = nc_sb(nc, ctx, "lp_s2", [128, RK])
    d_s2 = Dep()
    for nm, rows in (("b1", HID), ("b3", HID), ("b2", DM)):
        for c in range(rows // 128):
            tr.dma("qs", s_s2[:], io[nm][c * 128:(c + 1) * 128, :],
                   writes=[d_s2])
            tr.op("te", lambda e: e.transpose(
                t_ps16[:, :], s_s2.ap()[:, :], s_ident[:]),
                reads=[d_s2, d_ident], writes=[d_ps16])
            tr.op("ve", lambda e: e.tensor_copy(s_t2[:, :], t_ps16[:, :]),
                  reads=[d_ps16], writes=[d_t2])
            emit_split(tr, "ve", s_t2[:], d_t2, s_t2h[:], d_t2h,
                       s_t2l[:], d_t2l)
            tr.dma("qs", sc["%sT_hi" % nm][:, c * 128:(c + 1) * 128],
                   s_t2h[:], reads=[d_t2h], writes=[dep["%sT" % nm]])
            tr.dma("qs", sc["%sT_lo" % nm][:, c * 128:(c + 1) * 128],
                   s_t2l[:], reads=[d_t2l], writes=[dep["%sT" % nm]])
    return lora


def phase_tail(nc, tr, io, sc, dep, qk, s_ident, d_ident):
    pid = nc.partition_id()
    base = nc.sync.snap(pid * TPC)
    with ExitStack() as c0:
        s_o = nc_sb(nc, c0, "t_o", [128, DM])
        s_dq = nc_sb(nc, c0, "t_dq", [128, DM])
        s_hi = nc_sb(nc, c0, "t_hi", [128, DM], BF16)
        s_lo = nc_sb(nc, c0, "t_lo", [128, DM], BF16)
        d_o, d_dq, d_hi, d_lo = (Dep() for _ in range(4))
        for tb in range(TPC // 128):
            off = nc.sync.snap(base + tb * 128, donate=True)
            g2v = sc["gath2"].rearrange("(j t) f -> j t f", j=NCORE)
            tr.dma("qs",
                   s_o.ap().rearrange("p (j f) -> p j f", j=NCORE),
                   g2v[:, bass.ds(off, 128), :].transpose([1, 0, 2]),
                   reads=[dep["g2"]], writes=[d_o])
            emit_quant(tr, qk, s_o, d_o, s_dq, d_dq)
            emit_split(tr, "ve", s_dq[:], d_dq, s_hi[:], d_hi, s_lo[:], d_lo)
            tr.dma("qs", sc["odqn_hi"][tb * 128:(tb + 1) * 128, :], s_hi[:],
                   reads=[d_hi], writes=[dep["odqn"]])
            tr.dma("qs", sc["odqn_lo"][tb * 128:(tb + 1) * 128, :], s_lo[:],
                   reads=[d_lo], writes=[dep["odqn"]])
        tr.fence()
        s_tx = nc_sb(nc, c0, "t_tx", [128, TPC], BF16)
        d_tx = Dep()
        for half in ("hi", "lo"):
            for ic in range(NIC):
                tr.dma("qa", s_tx[:],
                       sc["odqn_%s" % half][:, ic * 128:(ic + 1) * 128],
                       reads=[dep["odqn"]], writes=[d_tx], transpose=True)
                for tb in range(TPC // 128):
                    tr.dma("qs",
                           sc["odqT_%s" % half][
                               :, tb * (NIC * 128) + ic * 128:
                               tb * (NIC * 128) + (ic + 1) * 128],
                           s_tx[:, tb * 128:(tb + 1) * 128],
                           reads=[d_tx], writes=[dep["odqT"]])

    with ExitStack() as c1:
        s_oT = {half: nc_sb(nc, c1, "t_oT" + half, [128, 2 * NIC * 128],
                            BF16) for half in ("hi", "lo")}
        d_oT = {half: Dep() for half in ("hi", "lo")}
        s_wo = {half: nc_sb(nc, c1, "t_wo" + half, [128, (NIC // 2) * 512],
                            BF16) for half in ("hi", "lo")}
        d_wo = {half: Dep() for half in ("hi", "lo")}
        s_xr = nc_sb(nc, c1, "t_xr", [128, 512])
        d_xr = Dep()
        s_x1 = [nc_sb(nc, c1, "t_x1_%d" % i, [128, DM]) for i in range(2)]
        d_x1 = [Dep(), Dep()]
        s_xn2 = nc_sb(nc, c1, "t_xn2", [128, DM])
        d_xn2 = Dep()
        s_nh = nc_sb(nc, c1, "t_nh", [128, DM], BF16)
        s_nl = nc_sb(nc, c1, "t_nl", [128, DM], BF16)
        d_nh, d_nl = Dep(), Dep()
        ps_a = [nc_ps(nc, c1, "t_psa%d" % i, [128, 512]) for i in range(2)]
        d_psa = [Dep(), Dep()]
        for tp in range(TPC // 256):
            for half in ("hi", "lo"):
                for tbl in range(2):
                    tb = tp * 2 + tbl
                    tr.dma("qs",
                           s_oT[half][:, tbl * NIC * 128:
                                      (tbl + 1) * NIC * 128],
                           sc["odqT_%s" % half][:, tb * NIC * 128:
                                                (tb + 1) * NIC * 128],
                           reads=[dep["odqT"]], writes=[d_oT[half]])
            for ow in range(NCORE):
                for ih in range(2):
                    for half in ("hi", "lo"):
                        for icc in range(NIC // 2):
                            ic = ih * (NIC // 2) + icc
                            c0_ = 2048 if half == "hi" else 2560
                            tr.dma("qs",
                                   s_wo[half][:, icc * 512:(icc + 1) * 512],
                                   sc["gath1"][ow * DM + ic * 128:
                                               ow * DM + (ic + 1) * 128,
                                               c0_:c0_ + 512],
                                   reads=[dep["g1"]], writes=[d_wo[half]])
                    for tbl in range(2):
                        facs = []
                        for icc in range(NIC // 2):
                            ic = ih * (NIC // 2) + icc
                            facs += mm3f(ps_a[tbl][:],
                                s_oT["hi"].ap()[
                                    :, (tbl * NIC + ic) * 128:
                                    (tbl * NIC + ic + 1) * 128],
                                s_oT["lo"].ap()[
                                    :, (tbl * NIC + ic) * 128:
                                    (tbl * NIC + ic + 1) * 128],
                                s_wo["hi"].ap()[:, icc * 512:
                                                (icc + 1) * 512],
                                s_wo["lo"].ap()[:, icc * 512:
                                                (icc + 1) * 512],
                                ih == 0 and icc == 0,
                                ih == 1 and icc == NIC // 2 - 1)
                        tr.op_multi("te", facs,
                                    reads=[d_oT["hi"], d_oT["lo"],
                                           d_wo["hi"], d_wo["lo"]],
                                    writes=[d_psa[tbl]])
                        if ih == 1:
                            tb = tp * 2 + tbl
                            tr.dma("qs", s_xr[:],
                                   io["x_my"][tb * 128:(tb + 1) * 128,
                                              ow * 512:(ow + 1) * 512],
                                   writes=[d_xr])
                            tr.op("ve", lambda e, _t=tbl, _o=ow:
                                  e.tensor_tensor(
                                      s_x1[_t][:, _o * 512:(_o + 1) * 512],
                                      ps_a[_t].ap()[:, :], s_xr.ap()[:, :],
                                      ALU.add),
                                  reads=[d_psa[tbl], d_xr],
                                  writes=[d_x1[tbl]])
            for tbl in range(2):
                tb = tp * 2 + tbl
                tr.dma("qs", sc["x1_s"][tb * 128:(tb + 1) * 128, :],
                       s_x1[tbl][:], reads=[d_x1[tbl]], writes=[dep["x1"]])
                emit_ln(tr, qk, s_x1[tbl], d_x1[tbl], s_xn2, d_xn2)
                emit_split(tr, "ve", s_xn2[:], d_xn2, s_nh[:], d_nh,
                           s_nl[:], d_nl)
                tr.dma("qs", sc["xn2n_hi"][tb * 128:(tb + 1) * 128, :],
                       s_nh[:], reads=[d_nh], writes=[dep["xn2n"]])
                tr.dma("qs", sc["xn2n_lo"][tb * 128:(tb + 1) * 128, :],
                       s_nl[:], reads=[d_nl], writes=[dep["xn2n"]])
        tr.fence()
        s_tx2 = nc_sb(nc, c1, "t_tx2", [128, TPC], BF16)
        d_tx2 = Dep()
        for half in ("hi", "lo"):
            for ic in range(NIC):
                tr.dma("qa", s_tx2[:],
                       sc["xn2n_%s" % half][:, ic * 128:(ic + 1) * 128],
                       reads=[dep["xn2n"]], writes=[d_tx2], transpose=True)
                for tb in range(TPC // 128):
                    tr.dma("qs",
                           sc["xn2T_%s" % half][
                               :, tb * (NIC * 128) + ic * 128:
                               tb * (NIC * 128) + (ic + 1) * 128],
                           s_tx2[:, tb * 128:(tb + 1) * 128],
                           reads=[d_tx2], writes=[dep["xn2T"]])

    with ExitStack() as cl:
        lora = phase_lora_prep(nc, tr, cl, io, sc, dep, s_ident, d_ident)
        phase_ffn_a(nc, tr, cl, io, sc, dep, lora, s_ident, d_ident)
    with ExitStack() as cw:
        phase_ffn_b(nc, tr, cw, io, sc, dep, s_ident, d_ident)


def phase_ffn_a(nc, tr, cl, io, sc, dep, lora, s_ident, d_ident):
    NHW = (HID + 511) // 512
    s_xT = {half: nc_sb(nc, cl, "f_xT" + half, [128, NIC * 128], BF16)
            for half in ("hi", "lo")}
    d_xT = {half: Dep() for half in ("hi", "lo")}
    a1h, a1l, da1h, da1l = lora["a1"]
    a3h, a3l, da3h, da3l = lora["a3"]
    ps_xa = nc_ps(nc, cl, "f_psxa", [128, 2 * RK])
    d_psxa = Dep()
    s_xa = nc_sb(nc, cl, "f_xa", [128, 2 * RK])
    d_xa = Dep()
    ps_xat = nc_ps(nc, cl, "f_psxat", [RK, 128])
    d_psxat = Dep()
    s_xat = {i: nc_sb(nc, cl, "f_xat%d" % i, [RK, TPC]) for i in (0, 1)}
    d_xat = {i: Dep() for i in (0, 1)}
    s_xath = {i: nc_sb(nc, cl, "f_xath%d" % i, [RK, TPC], BF16)
              for i in (0, 1)}
    s_xatl = {i: nc_sb(nc, cl, "f_xatl%d" % i, [RK, TPC], BF16)
              for i in (0, 1)}
    d_xath = {i: Dep() for i in (0, 1)}
    d_xatl = {i: Dep() for i in (0, 1)}
    for tb in range(TPC // 128):
        for half in ("hi", "lo"):
            tr.dma("qs", s_xT[half][:],
                   sc["xn2T_%s" % half][:, tb * NIC * 128:
                                        (tb + 1) * NIC * 128],
                   reads=[dep["xn2T"]], writes=[d_xT[half]])
        facs = []
        for ic in range(NIC):
            for ri, (rh, rl, cr) in enumerate(
                    ((a1h, a1l, 0), (a3h, a3l, RK))):
                combos = ((s_xT["hi"], rh), (s_xT["hi"], rl),
                          (s_xT["lo"], rh))
                for li, (lw, rx) in enumerate(combos):
                    first = (ic == 0 and ri == 0 and li == 0)
                    last = (ic == NIC - 1 and ri == 1 and li == 2)
                    facs.append(lambda e, _lw=lw, _rx=rx, _ic=ic, _cr=cr,
                                _f=first, _e=last:
                                e.matmul(ps_xa[:, _cr:_cr + RK],
                                         _lw.ap()[:, _ic * 128:
                                                  (_ic + 1) * 128],
                                         _rx.ap()[:, _ic * RK:
                                                  (_ic + 1) * RK],
                                         start=_f, stop=_e))
        tr.op_multi("te", facs,
                    reads=[d_xT["hi"], d_xT["lo"], da1h, da1l, da3h, da3l],
                    writes=[d_psxa])
        tr.op("se", lambda e: e.activation(
            s_xa[:], ps_xa[:], AF.Copy, scale=2.0),
            reads=[d_psxa], writes=[d_xa])
        for i in (0, 1):
            tr.op("te", lambda e, _i=i: e.transpose(
                ps_xat[:, :], s_xa.ap()[:, _i * RK:(_i + 1) * RK],
                s_ident[:]),
                reads=[d_xa, d_ident], writes=[d_psxat])
            tr.op("ve", lambda e, _t=tb, _i=i: e.tensor_copy(
                s_xat[_i][:, _t * 128:(_t + 1) * 128], ps_xat[:, :]),
                reads=[d_psxat], writes=[d_xat[i]])
    for i in (0, 1):
        emit_split(tr, "ve", s_xat[i][:], d_xat[i], s_xath[i][:],
                   d_xath[i], s_xatl[i][:], d_xatl[i])

    s_wf = {half: nc_sb(nc, cl, "f_wf" + half, [128, NIC * 512], BF16)
            for half in ("hi", "lo")}
    d_wf = {half: Dep() for half in ("hi", "lo")}
    s_bT = {half: nc_sb(nc, cl, "f_bT" + half, [RK, 512], BF16)
            for half in ("hi", "lo")}
    d_bT = {half: Dep() for half in ("hi", "lo")}
    s_h1 = nc_sb(nc, cl, "f_h1", [128, 8 * 512])
    d_h1 = Dep()
    s_g = nc_sb(nc, cl, "f_g", [128, 512])
    d_g = Dep()
    s_gh = nc_sb(nc, cl, "f_gh", [128, 512], BF16)
    s_gl = nc_sb(nc, cl, "f_gl", [128, 512], BF16)
    d_gh, d_gl = Dep(), Dep()
    ps_h = nc_ps(nc, cl, "f_psh", [128, 512])
    d_psh = Dep()
    for hw in range(NHW):
        W = min(512, HID - 512 * hw)
        for sub, wsrc, bnm in ((0, "w1n", "b1T"), (1, "w3n", "b3T")):
            for half in ("hi", "lo"):
                for ic in range(NIC):
                    tr.dma("qa", s_wf[half][:, ic * 512:ic * 512 + W],
                           sc["%s_%s" % (wsrc, half)][
                               hw * 512:hw * 512 + W,
                               ic * 128:(ic + 1) * 128],
                           reads=[dep[wsrc]], writes=[d_wf[half]],
                           transpose=True)
                tr.dma("qs", s_bT[half][:, :W],
                       sc["%s_%s" % (bnm, half)][:, hw * 512:hw * 512 + W],
                       reads=[dep[bnm]], writes=[d_bT[half]])
            for tb in range(TPC // 128):
                for half in ("hi", "lo"):
                    tr.dma("qs", s_xT[half][:],
                           sc["xn2T_%s" % half][:, tb * NIC * 128:
                                                (tb + 1) * NIC * 128],
                           reads=[dep["xn2T"]], writes=[d_xT[half]])
                facs = []
                for ic in range(NIC):
                    facs += mm3f(ps_h[:, :W],
                        s_xT["hi"].ap()[:, ic * 128:(ic + 1) * 128],
                        s_xT["lo"].ap()[:, ic * 128:(ic + 1) * 128],
                        s_wf["hi"].ap()[:, ic * 512:ic * 512 + W],
                        s_wf["lo"].ap()[:, ic * 512:ic * 512 + W],
                        ic == 0, False)
                combos = ((s_xath[sub], s_bT["hi"]),
                          (s_xath[sub], s_bT["lo"]),
                          (s_xatl[sub], s_bT["hi"]))
                for li, (lw, rx) in enumerate(combos):
                    facs.append(lambda e, _lw=lw, _rx=rx, _tb=tb,
                                _W=W, _e=(li == 2):
                                e.matmul(ps_h[:, :_W],
                                         _lw.ap()[:, _tb * 128:
                                                  (_tb + 1) * 128],
                                         _rx.ap()[:, :_W],
                                         start=False, stop=_e))
                tr.op_multi("te", facs,
                            reads=[d_xT["hi"], d_xT["lo"], d_wf["hi"],
                                   d_wf["lo"], d_xath[sub], d_xatl[sub],
                                   d_bT["hi"], d_bT["lo"]],
                            writes=[d_psh])
                if sub == 0:
                    tr.op("se", lambda e, _t=tb, _W=W: e.activation(
                        s_h1[:, _t * 512:_t * 512 + _W], ps_h[:, :_W],
                        AF.Silu), reads=[d_psh], writes=[d_h1])
                else:
                    tr.op("ve", lambda e, _t=tb, _W=W: e.tensor_tensor(
                        s_g[:, :_W], s_h1.ap()[:, _t * 512:_t * 512 + _W],
                        ps_h.ap()[:, :_W], ALU.mult),
                        reads=[d_h1, d_psh], writes=[d_g])
                    emit_split(tr, "ve", s_g[:, :W], d_g, s_gh[:, :W],
                               d_gh, s_gl[:, :W], d_gl)
                    tr.dma("qs", sc["gn_hi"][tb * 128:(tb + 1) * 128,
                                             hw * 512:hw * 512 + W],
                           s_gh[:, :W], reads=[d_gh], writes=[dep["gn"]])
                    tr.dma("qs", sc["gn_lo"][tb * 128:(tb + 1) * 128,
                                             hw * 512:hw * 512 + W],
                           s_gl[:, :W], reads=[d_gl], writes=[dep["gn"]])


def phase_ffn_b(nc, tr, cw, io, sc, dep, s_ident, d_ident):
    tr.fence()
    s_tg = nc_sb(nc, cw, "f_tg", [128, TPC], BF16)
    d_tg = Dep()
    for half in ("hi", "lo"):
        for hc in range(NHC):
            tr.dma("qa", s_tg[:],
                   sc["gn_%s" % half][:, hc * 128:(hc + 1) * 128],
                   reads=[dep["gn"]], writes=[d_tg], transpose=True)
            for tb in range(TPC // 128):
                tr.dma("qs",
                       sc["gT_%s" % half][:, tb * (NHC * 128) + hc * 128:
                                          tb * (NHC * 128) + (hc + 1) * 128],
                       s_tg[:, tb * 128:(tb + 1) * 128],
                       reads=[d_tg], writes=[dep["gT"]])

    s_tmp = nc_sb(nc, cw, "fb_tmp", [128, 128])
    d_tmp = Dep()
    t_ps = nc_ps(nc, cw, "fb_ps", [128, 128])
    d_ps = Dep()
    s_a2src = nc_sb(nc, cw, "fb_a2src", [RK, 128])
    d_a2src = Dep()
    a2h = nc_sb(nc, cw, "fb_a2h", [128, NHC * RK], BF16)
    a2l = nc_sb(nc, cw, "fb_a2l", [128, NHC * RK], BF16)
    da2h, da2l = Dep(), Dep()
    for c in range(NHC):
        tr.dma("qs", s_a2src[:], io["a2"][:, c * 128:(c + 1) * 128],
               writes=[d_a2src])
        tr.op("te", lambda e: e.transpose(
            t_ps[:, :RK], s_a2src.ap()[:, :],
            s_ident.ap()[0:RK, 0:RK]),
            reads=[d_a2src, d_ident], writes=[d_ps])
        tr.op("ve", lambda e: e.tensor_copy(s_tmp[:, :RK], t_ps[:, :RK]),
              reads=[d_ps], writes=[d_tmp])
        emit_split(tr, "ve", s_tmp[:, :RK], d_tmp,
                   a2h[:, c * RK:(c + 1) * RK], da2h,
                   a2l[:, c * RK:(c + 1) * RK], da2l)

    s_gT = {half: nc_sb(nc, cw, "fb_gT" + half, [128, NHC * 128], BF16)
            for half in ("hi", "lo")}
    d_gT = {half: Dep() for half in ("hi", "lo")}
    ps_ga = nc_ps(nc, cw, "fb_psga", [128, RK])
    d_psga = Dep()
    s_ga = nc_sb(nc, cw, "fb_ga", [128, RK])
    d_ga = Dep()
    ps_gat = nc_ps(nc, cw, "fb_psgat", [RK, 128])
    d_psgat = Dep()
    s_gat = nc_sb(nc, cw, "fb_gat", [RK, TPC])
    d_gat = Dep()
    s_gath = nc_sb(nc, cw, "fb_gath", [RK, TPC], BF16)
    s_gatl = nc_sb(nc, cw, "fb_gatl", [RK, TPC], BF16)
    d_gath, d_gatl = Dep(), Dep()
    for tb in range(TPC // 128):
        for half in ("hi", "lo"):
            tr.dma("qs", s_gT[half][:],
                   sc["gT_%s" % half][:, tb * NHC * 128:
                                      (tb + 1) * NHC * 128],
                   reads=[dep["gT"]], writes=[d_gT[half]])
        facs = []
        for hc in range(NHC):
            combos = ((s_gT["hi"], a2h), (s_gT["hi"], a2l),
                      (s_gT["lo"], a2h))
            for li, (lw, rx) in enumerate(combos):
                facs.append(lambda e, _lw=lw, _rx=rx, _hc=hc,
                            _f=(hc == 0 and li == 0),
                            _e=(hc == NHC - 1 and li == 2):
                            e.matmul(ps_ga[:],
                                     _lw.ap()[:, _hc * 128:(_hc + 1) * 128],
                                     _rx.ap()[:, _hc * RK:(_hc + 1) * RK],
                                     start=_f, stop=_e))
        tr.op_multi("te", facs,
                    reads=[d_gT["hi"], d_gT["lo"], da2h, da2l],
                    writes=[d_psga])
        tr.op("se", lambda e: e.activation(
            s_ga[:], ps_ga[:], AF.Copy, scale=2.0),
            reads=[d_psga], writes=[d_ga])
        tr.op("te", lambda e: e.transpose(
            ps_gat[:, :], s_ga.ap()[:, :], s_ident[:]),
            reads=[d_ga, d_ident], writes=[d_psgat])
        tr.op("ve", lambda e, _t=tb: e.tensor_copy(
            s_gat[:, _t * 128:(_t + 1) * 128], ps_gat[:, :]),
            reads=[d_psgat], writes=[d_gat])
    emit_split(tr, "ve", s_gat[:], d_gat, s_gath[:], d_gath,
               s_gatl[:], d_gatl)

    OW = 256
    NOW = DM // OW
    s_w2 = {half: nc_sb(nc, cw, "fb_w2" + half, [128, NHC * OW],
                        BF16) for half in ("hi", "lo")}
    d_w2 = {half: Dep() for half in ("hi", "lo")}
    s_b2 = {half: nc_sb(nc, cw, "fb_b2" + half, [RK, OW], BF16)
            for half in ("hi", "lo")}
    d_b2 = {half: Dep() for half in ("hi", "lo")}
    s_x1t = nc_sb(nc, cw, "fb_x1t", [128, OW])
    d_x1t = Dep()
    s_out = nc_sb(nc, cw, "fb_out", [128, OW])
    d_out = Dep()
    ps_o2 = nc_ps(nc, cw, "fb_pso2", [128, OW])
    d_pso2 = Dep()
    for ow in range(NOW):
        for half in ("hi", "lo"):
            tr.dma("qs", s_b2[half][:, :],
                   sc["b2T_%s" % half][:, ow * OW:(ow + 1) * OW],
                   reads=[dep["b2T"]], writes=[d_b2[half]])
            for hc in range(NHC):
                tr.dma("qa", s_w2[half][:, hc * OW:(hc + 1) * OW],
                       sc["w2n_%s" % half][ow * OW:(ow + 1) * OW,
                                           hc * 128:(hc + 1) * 128],
                       reads=[dep["w2n"]], writes=[d_w2[half]],
                       transpose=True)
        for tb in range(TPC // 128):
            for half in ("hi", "lo"):
                tr.dma("qs", s_gT[half][:],
                       sc["gT_%s" % half][:, tb * NHC * 128:
                                          (tb + 1) * NHC * 128],
                       reads=[dep["gT"]], writes=[d_gT[half]])
            facs = []
            for hc in range(NHC):
                facs += mm3f(ps_o2[:],
                    s_gT["hi"].ap()[:, hc * 128:(hc + 1) * 128],
                    s_gT["lo"].ap()[:, hc * 128:(hc + 1) * 128],
                    s_w2["hi"].ap()[:, hc * OW:(hc + 1) * OW],
                    s_w2["lo"].ap()[:, hc * OW:(hc + 1) * OW],
                    hc == 0, False)
            combos = ((s_gath, s_b2["hi"]), (s_gath, s_b2["lo"]),
                      (s_gatl, s_b2["hi"]))
            for li, (lw, rx) in enumerate(combos):
                facs.append(lambda e, _lw=lw, _rx=rx, _tb=tb,
                            _e=(li == 2):
                            e.matmul(ps_o2[:],
                                     _lw.ap()[:, _tb * 128:(_tb + 1) * 128],
                                     _rx.ap()[:, :],
                                     start=False, stop=_e))
            tr.op_multi("te", facs,
                        reads=[d_gT["hi"], d_gT["lo"], d_w2["hi"],
                               d_w2["lo"], d_gath, d_gatl,
                               d_b2["hi"], d_b2["lo"]],
                        writes=[d_pso2])
            tr.dma("qs", s_x1t[:],
                   sc["x1_s"][tb * 128:(tb + 1) * 128,
                              ow * OW:(ow + 1) * OW],
                   reads=[dep["x1"]], writes=[d_x1t])
            tr.op("ve", lambda e: e.tensor_tensor(
                s_out[:], ps_o2.ap()[:, :], s_x1t.ap()[:, :], ALU.add),
                reads=[d_pso2, d_x1t], writes=[d_out])
            tr.dma("qs",
                   io["out"][tb * 128:(tb + 1) * 128, ow * OW:(ow + 1) * OW],
                   s_out[:], reads=[d_out], writes=[dep["out"]])


_NC_CACHE = [None]
_LAST_RES = [None]


def _host_consts():
    import jax
    import jax.numpy as jnp
    cpu = jax.devices("cpu")[0]
    with jax.default_device(cpu):
        inv = 1.0 / (10000.0 ** (jnp.arange(0, HD, 2, dtype=jnp.float32)
                                 / HD))
        ang = jnp.arange(SEQ, dtype=jnp.float32)[:, None] * inv[None, :]
        cos = np.ascontiguousarray(np.asarray(jnp.cos(ang)).T)
        sin = np.ascontiguousarray(np.asarray(jnp.sin(ang)).T)
    cosF = np.ascontiguousarray(np.vstack([cos, cos]))
    sinF2 = np.ascontiguousarray(np.vstack([-sin, sin]))
    mask = np.where(np.arange(896)[None, :] <= np.arange(128)[:, None] + 384,
                    np.float32(0.0), np.float32(NEGBIG)).astype(np.float32)
    ident = np.eye(128, dtype=np.float32)
    perm = np.zeros((128, 128), dtype=np.float32)
    for m in range(128):
        perm[(m + 64) % 128, m] = 1.0
    return cosF, sinF2, mask, ident, perm


def kernel(L0_attn_hidden_states, L0_attn_Wq, L0_attn_Wk, L0_attn_Wv,
           L0_attn_Wo, L0_ffn_w1, L0_ffn_w3, L0_ffn_w2, L0_ffn_lora_A1,
           L0_ffn_lora_B1, L0_ffn_lora_A3, L0_ffn_lora_B3, L0_ffn_lora_A2,
           L0_ffn_lora_B2):
    f32 = lambda a: np.ascontiguousarray(np.asarray(a, dtype=np.float32))
    x = f32(L0_attn_hidden_states).reshape(TOK, DM)
    wq, wk, wv, wo = (f32(L0_attn_Wq), f32(L0_attn_Wk), f32(L0_attn_Wv),
                      f32(L0_attn_Wo))
    w1, w3, w2 = f32(L0_ffn_w1), f32(L0_ffn_w3), f32(L0_ffn_w2)
    a1, b1 = f32(L0_ffn_lora_A1), f32(L0_ffn_lora_B1)
    a3, b3 = f32(L0_ffn_lora_A3), f32(L0_ffn_lora_B3)
    a2, b2 = f32(L0_ffn_lora_A2), f32(L0_ffn_lora_B2)
    cosF, sinF2, mask, ident, perm = _host_consts()

    if _NC_CACHE[0] is None:
        _NC_CACHE[0] = build_nc()
    nc = _NC_CACHE[0]

    in_maps = []
    for c in range(NCORE):
        hs = slice(c * HPC * HD, (c + 1) * HPC * HD)
        in_maps.append({
            "x_my": x[c * TPC:(c + 1) * TPC],
            "wq_my": wq[hs], "wk_my": wk[hs], "wv_my": wv[hs],
            "wo_my": wo[hs],
            "w1": w1, "w3": w3, "w2": w2,
            "a1": a1, "b1": b1, "a3": a3, "b3": b3, "a2": a2, "b2": b2,
            "cosF": cosF, "sinF2": sinF2, "maskS": mask, "identf": ident,
            "permF": perm,
        })
    res = run_bass_kernel_spmd(nc, in_maps, core_ids=list(range(NCORE)))
    _LAST_RES[0] = res.results
    outs = np.concatenate([np.asarray(res.results[c]["out"])
                           for c in range(NCORE)], axis=0)
    return outs.reshape(BATCH, SEQ, DM)


# version 03
# speedup vs baseline: 1.1596x; 1.1596x over previous
import math
import numpy as np
from contextlib import ExitStack

import concourse.bass as bass
import concourse.mybir as mybir
from concourse.expressions import RuntimeValue
from concourse.bass_utils import run_bass_kernel_spmd

F32 = mybir.dt.float32
BF16 = mybir.dt.bfloat16
AF = mybir.ActivationFunctionType
ALU = mybir.AluOpType
AX = mybir.AxisListType.X

NCORE = 8
BATCH = 2
SEQ = 4096
DM = 4096
NH = 32
HD = 128
HID = 11008
RK = 16
TOK = BATCH * SEQ
TPC = TOK // NCORE
HPC = NH // NCORE
NIC = DM // 128
NHC = HID // 128
SCALE = 1.0 / math.sqrt(HD)
EPS = 1e-5
NEGBIG = -3.0e38
SEGS = [(0, 2048, 4), (2048, 3072, 6), (3072, 4096, 8)]
DBG = ()


class Dep:
    __slots__ = ("w", "readers")

    def __init__(self):
        self.w = None
        self.readers = []


class Tr:
    def __init__(self, nc, ctx):
        self.nc = nc
        self.eng = {"te": nc.tensor, "ve": nc.vector, "se": nc.scalar,
                    "pe": nc.gpsimd, "sy": nc.sync}
        self.sems = {}
        self.cnt = {}
        for name in ("te", "ve", "se", "pe", "qs", "qa", "qg", "cc"):
            self.sems[name] = ctx.enter_context(nc.semaphore("sem_" + name))
            self.cnt[name] = 0
        self.wm = {}

    def _wait(self, en, need):
        e = self.eng[en]
        for s, v in sorted(need.items()):
            if v > 0 and self.wm.get((en, s), 0) < v:
                e.wait_ge(self.sems[s], v)
                self.wm[(en, s)] = v

    def op(self, en, factory, reads=(), writes=(), sem=None, step=1):
        need = {}
        for d in reads:
            if d.w:
                s, v = d.w
                if need.get(s, 0) < v:
                    need[s] = v
        for d in writes:
            if d.w:
                s, v = d.w
                if need.get(s, 0) < v:
                    need[s] = v
            for s, v in d.readers:
                if need.get(s, 0) < v:
                    need[s] = v
        self._wait(en, need)
        inst = factory(self.eng[en])
        s = sem if sem is not None else en
        inst.then_inc(self.sems[s], step)
        self.cnt[s] += step
        stamp = (s, self.cnt[s])
        for d in writes:
            d.w = stamp
            d.readers = []
        for d in reads:
            d.readers.append(stamp)
        return inst

    def op_multi(self, en, factories, reads=(), writes=(), sem=None,
                 step=1):
        need = {}
        for d in reads:
            if d.w:
                s, v = d.w
                if need.get(s, 0) < v:
                    need[s] = v
        for d in writes:
            if d.w:
                s, v = d.w
                if need.get(s, 0) < v:
                    need[s] = v
            for s, v in d.readers:
                if need.get(s, 0) < v:
                    need[s] = v
        self._wait(en, need)
        e = self.eng[en]
        for f in factories[:-1]:
            f(e)
        inst = factories[-1](e)
        s = sem if sem is not None else en
        inst.then_inc(self.sems[s], step)
        self.cnt[s] += step
        stamp = (s, self.cnt[s])
        for d in writes:
            d.w = stamp
            d.readers = []
        for d in reads:
            d.readers.append(stamp)
        return inst

    def dma(self, queue, out, in_, reads=(), writes=(), transpose=False):
        en = {"qs": "sy", "qa": "se", "qg": "pe"}[queue]
        if transpose:
            fac = lambda e: e.dma_start(out=out, in_=in_, transpose=True)
        else:
            fac = lambda e: e.dma_start(out=out, in_=in_)
        return self.op(en, fac, reads=reads, writes=writes, sem=queue,
                       step=16)

    def wait_all(self, en):
        self._wait(en, dict(self.cnt))

    def fence(self):
        for en in ("te", "ve", "se", "pe", "sy"):
            self._wait(en, dict(self.cnt))


def nc_sb(nc, ctx, name, shape, dtype=F32):
    return ctx.enter_context(nc.sbuf_tensor(name, shape, dtype))


def nc_ps(nc, ctx, name, shape, dtype=F32):
    return ctx.enter_context(nc.psum_tensor(name, shape, dtype))


def emit_split(tr, en, src_ap, d_src, hi_ap, d_hi, lo_ap, d_lo):
    tr.op(en, lambda e: e.tensor_copy(hi_ap, src_ap),
          reads=[d_src], writes=[d_hi])
    tr.op(en, lambda e: e.tensor_tensor(lo_ap, src_ap, hi_ap, ALU.subtract),
          reads=[d_src, d_hi], writes=[d_lo])


_QK_SEQ = [0]


class QuantKit:
    def __init__(self, nc, ctx):
        _QK_SEQ[0] += 1
        p = "qk%d_" % _QK_SEQ[0]
        self.amax = nc_sb(nc, ctx, p + "amax", [128, 128])
        self.rs = nc_sb(nc, ctx, p + "rs", [128, 128])
        self.sneg = nc_sb(nc, ctx, p + "sneg", [128, 128])
        self.u = nc_sb(nc, ctx, p + "u", [128, 2048])
        self.c = nc_sb(nc, ctx, p + "c", [128, 2048])
        self.stats = nc_sb(nc, ctx, p + "stats", [128, 64])
        self.mv = nc_sb(nc, ctx, p + "mv", [128, 8])
        (self.d_amax, self.d_rs, self.d_sneg, self.d_u, self.d_c,
         self.d_stats, self.d_mv) = (Dep() for _ in range(7))


def emit_quant(tr, qk, t_in, d_in, t_dq, d_dq):
    for (s0, s1, bits) in SEGS:
        w = s1 - s0
        G = w // 32
        maxq = float(2 ** (bits - 1) - 1)
        shift = float(2 ** (bits - 1))
        src3 = t_in.ap()[:, s0:s1].rearrange("p (g k) -> p g k", k=32)
        tr.op("ve", lambda e, _s=src3, _G=G: e.tensor_reduce(
            qk.amax[:, :_G], _s, AX, ALU.max, apply_absolute_value=True),
            reads=[d_in], writes=[qk.d_amax])
        tr.op("ve", lambda e, _G=G: e.tensor_single_scalar(
            qk.amax[:, :_G], qk.amax[:, :_G], 1e-12, ALU.max),
            reads=[qk.d_amax], writes=[qk.d_amax])
        tr.op("ve", lambda e, _G=G: e.reciprocal(
            qk.rs[:, :_G], qk.amax[:, :_G]),
            reads=[qk.d_amax], writes=[qk.d_rs])
        rs_b = qk.rs.ap()[:, :G].unsqueeze(2).broadcast_to([128, G, 32])
        u3 = qk.u.ap()[:, :w].rearrange("p (g k) -> p g k", k=32)
        tr.op("ve", lambda e, _s=src3, _m=maxq, _r=rs_b, _u=u3:
              e.scalar_tensor_tensor(_u, _s, _m, _r, ALU.mult, ALU.mult),
              reads=[d_in, qk.d_rs], writes=[qk.d_u])
        tr.op("ve", lambda e, _a=s0, _b=s1, _w=w: e.tensor_copy(
            t_dq[:, _a:_b], qk.u[:, :_w]),
            reads=[qk.d_u], writes=[d_dq])
        for i in range(bits):
            bbit = float(2 ** (bits - 1 - i))
            thr = bbit - 0.5 - shift
            tr.op("ve", lambda e, _t=thr, _w=w: e.tensor_single_scalar(
                qk.c[:, :_w], qk.u[:, :_w], _t, ALU.is_ge),
                reads=[qk.d_u], writes=[qk.d_c])
            tr.op("ve", lambda e, _b=bbit, _w=w: e.scalar_tensor_tensor(
                qk.u[:, :_w], qk.c[:, :_w], -_b, qk.u[:, :_w],
                ALU.mult, ALU.add),
                reads=[qk.d_c, qk.d_u], writes=[qk.d_u])
        tr.op("ve", lambda e, _s=shift, _w=w, _a=s0, _b=s1:
              e.scalar_tensor_tensor(qk.c[:, :_w], qk.u[:, :_w], _s,
                                     t_dq[:, _a:_b], ALU.add, ALU.subtract),
              reads=[qk.d_u, d_dq], writes=[qk.d_c])
        tr.op("ve", lambda e, _G=G, _m=maxq: e.tensor_single_scalar(
            qk.sneg[:, :_G], qk.amax[:, :_G], -1.0 / _m, ALU.mult),
            reads=[qk.d_amax], writes=[qk.d_sneg])
        sneg_b = qk.sneg.ap()[:, :G].unsqueeze(2).broadcast_to([128, G, 32])
        dq3 = t_dq.ap()[:, s0:s1].rearrange("p (g k) -> p g k", k=32)
        c3 = qk.c.ap()[:, :w].rearrange("p (g k) -> p g k", k=32)
        tr.op("ve", lambda e, _d=dq3, _c=c3, _sb=sneg_b: e.tensor_tensor(
            _d, _c, _sb, ALU.mult),
            reads=[qk.d_c, qk.d_sneg], writes=[d_dq])


def emit_ln(tr, qk, t_x, d_x, t_out, d_out):
    nch = DM // 512
    for c in range(nch):
        tr.op("ve", lambda e, _c=c: e.bn_stats(
            qk.stats[:, 6 * _c:6 * _c + 6],
            t_x[:, 512 * _c:512 * (_c + 1)]),
            reads=[d_x], writes=[qk.d_stats])
    tr.op("ve", lambda e, _n=nch: e.bn_aggr(
        qk.mv[:, 0:2], qk.stats[:, 0:6 * _n]),
        reads=[qk.d_stats], writes=[qk.d_mv])
    tr.op("ve", lambda e: e.tensor_single_scalar(
        qk.mv[:, 2:3], qk.mv[:, 1:2], EPS, ALU.add),
        reads=[qk.d_mv], writes=[qk.d_mv])
    tr.op("ve", lambda e: e.reciprocal(qk.mv[:, 3:4], qk.mv[:, 2:3]),
          reads=[qk.d_mv], writes=[qk.d_mv])
    tr.op("se", lambda e: e.activation(qk.mv[:, 4:5], qk.mv[:, 3:4], AF.Sqrt),
          reads=[qk.d_mv], writes=[qk.d_mv])
    tr.op("ve", lambda e: e.tensor_scalar(
        t_out[:], t_x[:], qk.mv.ap()[:, 0:1], qk.mv.ap()[:, 4:5],
        ALU.subtract, ALU.mult),
        reads=[d_x, qk.d_mv], writes=[d_out])


def mm3f(pst, lh, ll, rh, rl, first, last):
    out = []
    combos = ((lh, rh), (lh, rl), (ll, rh))
    for li, (lw, rx) in enumerate(combos):
        out.append(lambda e, _l=lw, _r=rx, _s=(first and li == 0),
                   _e=(last and li == 2), _p=pst:
                   e.matmul(_p, _l, _r, start=_s, stop=_e))
    return out


def build_nc():
    nc = bass.Bass("TRN2", target_bir_lowering=False, debug=False,
                   num_devices=NCORE)

    def din(name, shape, dtype=F32):
        return nc.dram_tensor(name, shape, dtype, kind="ExternalInput").ap()

    def dint(name, shape, dtype):
        return nc.dram_tensor(name, shape, dtype, kind="Internal").ap()

    io = {}
    io["x_my"] = din("x_my", [TPC, DM])
    for nm in ("q", "k", "v", "o"):
        io["w%s_my" % nm] = din("w%s_my" % nm, [HPC * HD, DM])
    io["w1"] = din("w1", [HID, DM])
    io["w3"] = din("w3", [HID, DM])
    io["w2"] = din("w2", [DM, HID])
    for nm in ("a1", "b1", "a3", "b3", "a2", "b2"):
        shp = {"a1": [RK, DM], "a3": [RK, DM], "a2": [RK, HID],
               "b1": [HID, RK], "b3": [HID, RK], "b2": [DM, RK]}[nm]
        io[nm] = din(nm, shp)
    io["cosF"] = din("cosF", [128, SEQ])
    io["sinF2"] = din("sinF2", [128, SEQ])
    io["maskS"] = din("maskS", [128, 896])
    io["identf"] = din("identf", [128, 128])
    io["permF"] = din("permF", [128, 128])
    io["out"] = nc.dram_tensor("out", [TPC, DM], F32,
                               kind="ExternalOutput").ap()

    sc = {}
    sc["contrib1"] = dint("contrib1", [DM, 3072], BF16)
    sc["gath1"] = nc.dram_tensor("gath1", [NCORE * DM, 3072], BF16, kind="Internal", addr_space="Shared").ap()
    sc["contrib2"] = dint("contrib2", [TOK, HPC * HD], F32)
    sc["gath2"] = nc.dram_tensor("gath2", [NCORE * TOK, HPC * HD], F32, kind="Internal", addr_space="Shared").ap()
    sc["xq_hi_n"] = dint("xq_hi_n", [TPC, DM], BF16)
    sc["xq_lo_n"] = dint("xq_lo_n", [TPC, DM], BF16)
    for nm in ("q", "k", "v", "o"):
        for h in ("hi", "lo"):
            sc["wn_%s_%s" % (nm, h)] = dint("wn_%s_%s" % (nm, h),
                                            [HPC * HD, DM], BF16)
    for nm in ("q", "k", "v"):
        for h in ("hi", "lo"):
            sc["wT_%s_%s" % (nm, h)] = dint("wT_%s_%s" % (nm, h),
                                            [HPC, 128, DM], BF16)
    for t in ("q", "k"):
        for h in ("hi", "lo"):
            sc["%ss_%s" % (t, h)] = dint("%ss_%s" % (t, h),
                                         [HPC * HD, TOK], BF16)
    sc["vs"] = dint("vs", [TOK, HPC * HD], F32)
    for wnm, rows, cols in (("w1", HID, DM), ("w3", HID, DM),
                            ("w2", DM, HID)):
        for h in ("hi", "lo"):
            sc["%sn_%s" % (wnm, h)] = dint("%sn_%s" % (wnm, h),
                                           [rows, cols], BF16)
    sc["x1_s"] = dint("x1_s", [TPC, DM], F32)
    for h in ("hi", "lo"):
        sc["xn2n_%s" % h] = dint("xn2n_%s" % h, [TPC, DM], BF16)
        sc["xn2T_%s" % h] = dint("xn2T_%s" % h, [128, NIC * TPC], BF16)
        sc["odqn_%s" % h] = dint("odqn_%s" % h, [TPC, DM], BF16)
        sc["odqT_%s" % h] = dint("odqT_%s" % h, [128, NIC * TPC], BF16)
        sc["gn_%s" % h] = dint("gn_%s" % h, [TPC, HID], BF16)
        sc["gT_%s" % h] = dint("gT_%s" % h, [128, NHC * TPC], BF16)
        sc["b1T_%s" % h] = dint("b1T_%s" % h, [RK, HID], BF16)
        sc["b3T_%s" % h] = dint("b3T_%s" % h, [RK, HID], BF16)
        sc["b2T_%s" % h] = dint("b2T_%s" % h, [RK, DM], BF16)

    dbg = {}
    for nm in DBG:
        src = sc[nm]
        dbg[nm] = nc.dram_tensor("dbg_" + nm, list(src.shape), src.dtype,
                                 kind="ExternalOutput").ap()

    with ExitStack() as ctx:
        tr = Tr(nc, ctx)
        dep = {k: Dep() for k in (
            "xqh", "xql", "c1", "g1", "qk", "vs", "c2", "g2", "wT",
            "w1n", "w3n", "w2n", "x1", "xn2n", "xn2T", "odqn", "odqT",
            "gn", "gT", "out", "b1T", "b3T", "b2T")}
        for nm in ("q", "k", "v", "o"):
            dep["wn_" + nm] = Dep()

        s_ident = nc_sb(nc, ctx, "s_ident", [128, 128])
        d_ident = Dep()
        tr.dma("qs", s_ident[:], io["identf"], writes=[d_ident])
        s_perm = nc_sb(nc, ctx, "s_perm", [128, 128])
        d_perm = Dep()
        tr.dma("qs", s_perm[:], io["permF"], writes=[d_perm])
        s_mask = nc_sb(nc, ctx, "s_mask", [128, 896])
        d_mask = Dep()
        tr.dma("qs", s_mask[:], io["maskS"], writes=[d_mask])

        with ExitStack() as cp0:
            qk0 = QuantKit(nc, cp0)
            phase0(nc, tr, io, sc, dep, qk0)
        tr.fence()
        tr.op("pe", lambda e: e.collective_compute(
            "AllGather", ALU.bypass, replica_groups=[list(range(NCORE))],
            ins=[sc["contrib1"]], outs=[sc["gath1"]]),
            reads=[dep["c1"]], writes=[dep["g1"]], sem="cc")
        phase_ffn_split(nc, tr, io, sc, dep)
        phase_qkv(nc, tr, io, sc, dep, s_perm, d_perm, s_ident, d_ident)
        tr.fence()
        phase_attn(nc, tr, io, sc, dep, s_ident, d_ident, s_mask, d_mask)
        tr.fence()
        tr.op("pe", lambda e: e.collective_compute(
            "AllGather", ALU.bypass, replica_groups=[list(range(NCORE))],
            ins=[sc["contrib2"]], outs=[sc["gath2"]]),
            reads=[dep["c2"]], writes=[dep["g2"]], sem="cc")
        tr.fence()
        with ExitStack() as ctl:
            qk2 = QuantKit(nc, ctl)
            phase_tail(nc, tr, io, sc, dep, qk2, s_ident, d_ident)

        for nm in DBG:
            tr.wait_all("pe")
            tr.dma("qs", dbg[nm], sc[nm])
        tr.wait_all("pe")
        tr.wait_all("sy")
        tr.wait_all("se")
    return nc


def phase0(nc, tr, io, sc, dep, qk):
    with ExitStack() as c0:
        s_xt = [nc_sb(nc, c0, "p0_xt%d" % i, [128, DM]) for i in range(2)]
        s_xn = nc_sb(nc, c0, "p0_xn", [128, DM])
        s_dq = nc_sb(nc, c0, "p0_dq", [128, DM])
        s_hi = nc_sb(nc, c0, "p0_hi", [128, DM], BF16)
        s_lo = nc_sb(nc, c0, "p0_lo", [128, DM], BF16)
        d_xt = [Dep(), Dep()]
        d_xn, d_dq, d_hi, d_lo = (Dep() for _ in range(4))
        for tb in range(TPC // 128):
            bi = tb % 2
            tr.dma("qs", s_xt[bi][:],
                   io["x_my"][tb * 128:(tb + 1) * 128, :],
                   writes=[d_xt[bi]])
            emit_ln(tr, qk, s_xt[bi], d_xt[bi], s_xn, d_xn)
            emit_quant(tr, qk, s_xn, d_xn, s_dq, d_dq)
            emit_split(tr, "ve", s_dq[:], d_dq, s_hi[:], d_hi, s_lo[:], d_lo)
            tr.dma("qs", sc["xq_hi_n"][tb * 128:(tb + 1) * 128, :], s_hi[:],
                   reads=[d_hi], writes=[dep["xqh"]])
            tr.dma("qs", sc["xq_lo_n"][tb * 128:(tb + 1) * 128, :], s_lo[:],
                   reads=[d_lo], writes=[dep["xql"]])
        for nm in ("q", "k", "v", "o"):
            dn = dep["wn_" + nm]
            for r in range(HPC * HD // 128):
                bi = r % 2
                tr.dma("qs", s_xt[bi][:],
                       io["w%s_my" % nm][r * 128:(r + 1) * 128, :],
                       writes=[d_xt[bi]])
                emit_quant(tr, qk, s_xt[bi], d_xt[bi], s_dq, d_dq)
                emit_split(tr, "ve", s_dq[:], d_dq, s_hi[:], d_hi,
                           s_lo[:], d_lo)
                tr.dma("qs", sc["wn_%s_hi" % nm][r * 128:(r + 1) * 128, :],
                       s_hi[:], reads=[d_hi], writes=[dn])
                tr.dma("qs", sc["wn_%s_lo" % nm][r * 128:(r + 1) * 128, :],
                       s_lo[:], reads=[d_lo], writes=[dn])

    tr.fence()
    with ExitStack() as c1:
        s_tx = [nc_sb(nc, c1, "p0_tx%d" % i, [128, TPC], BF16)
                for i in range(2)]
        d_tx = [Dep(), Dep()]
        pi = 0
        for half, coff in (("hi", 0), ("lo", TPC)):
            src = sc["xq_%s_n" % half]
            rdep = dep["xqh"] if half == "hi" else dep["xql"]
            for ic in range(NIC):
                i = pi % 2
                pi += 1
                tr.dma("qa", s_tx[i][:], src[:, ic * 128:(ic + 1) * 128],
                       reads=[rdep], writes=[d_tx[i]], transpose=True)
                tr.dma("qs",
                       sc["contrib1"][ic * 128:(ic + 1) * 128,
                                      coff:coff + TPC],
                       s_tx[i][:], reads=[d_tx[i]], writes=[dep["c1"]])
        s_tw = [nc_sb(nc, c1, "p0_tw%d" % i, [128, HD], BF16)
                for i in range(2)]
        d_tw = [Dep(), Dep()]
        for nm in ("q", "k", "v"):
            for half in ("hi", "lo"):
                for h in range(HPC):
                    for ic in range(NIC):
                        i = pi % 2
                        pi += 1
                        tr.dma("qa", s_tw[i][:],
                               sc["wn_%s_%s" % (nm, half)][
                                   h * HD:(h + 1) * HD,
                                   ic * 128:(ic + 1) * 128],
                               reads=[dep["wn_" + nm]], writes=[d_tw[i]],
                               transpose=True)
                        tr.dma("qs",
                               sc["wT_%s_%s" % (nm, half)][
                                   h, :, ic * 128:(ic + 1) * 128],
                               s_tw[i][:], reads=[d_tw[i]],
                               writes=[dep["wT"]])
        s_to = [nc_sb(nc, c1, "p0_to%d" % i, [128, HPC * HD], BF16)
                for i in range(2)]
        d_to = [Dep(), Dep()]
        for hx, half in enumerate(("hi", "lo")):
            coff = 2048 + hx * 512
            for ic in range(NIC):
                i = pi % 2
                pi += 1
                tr.dma("qa", s_to[i][:],
                       sc["wn_o_%s" % half][:, ic * 128:(ic + 1) * 128],
                       reads=[dep["wn_o"]], writes=[d_to[i]], transpose=True)
                tr.dma("qs",
                       sc["contrib1"][ic * 128:(ic + 1) * 128,
                                      coff:coff + 512],
                       s_to[i][:], reads=[d_to[i]], writes=[dep["c1"]])


def phase_ffn_split(nc, tr, io, sc, dep):
    with ExitStack() as cf:
        bufs = []
        for i in range(2):
            bufs.append((nc_sb(nc, cf, "fs_f%d" % i, [128, HID]),
                         nc_sb(nc, cf, "fs_h%d" % i, [128, HID], BF16),
                         nc_sb(nc, cf, "fs_l%d" % i, [128, HID], BF16),
                         Dep(), Dep(), Dep()))
        pi = 0
        for wnm, rows, cols in (("w1", HID, DM), ("w3", HID, DM),
                                ("w2", DM, HID)):
            for r in range(rows // 128):
                sf, sh, sl, df, dh, dl = bufs[pi % 2]
                pi += 1
                tr.dma("qa", sf[:, :cols],
                       io[wnm][r * 128:(r + 1) * 128, :], writes=[df])
                emit_split(tr, "pe", sf[:, :cols], df, sh[:, :cols], dh,
                           sl[:, :cols], dl)
                tr.dma("qs", sc["%sn_hi" % wnm][r * 128:(r + 1) * 128, :],
                       sh[:, :cols], reads=[dh], writes=[dep["%sn" % wnm]])
                tr.dma("qs", sc["%sn_lo" % wnm][r * 128:(r + 1) * 128, :],
                       sl[:, :cols], reads=[dl], writes=[dep["%sn" % wnm]])


def phase_qkv(nc, tr, io, sc, dep, s_perm, d_perm, s_ident, d_ident):
    WIN = 512
    NWIN = TOK // WIN
    NG = 2
    GCH = NIC // NG
    with ExitStack() as c1:
        s_w = {}
        d_w = {}
        for nm in ("q", "k", "v"):
            for half in ("hi", "lo"):
                for st in range(2):
                    s_w[nm, half, st] = nc_sb(
                        nc, c1, "q_w%s%s%d" % (nm, half, st),
                        [128, DM], BF16)
                    d_w[nm, half, st] = Dep()
        s_x = {half: nc_sb(nc, c1, "q_x" + half, [128, NIC * WIN], BF16)
               for half in ("hi", "lo")}
        d_x = {(half, g): Dep() for half in ("hi", "lo") for g in range(NG)}
        s_cos = nc_sb(nc, c1, "q_cos", [128, WIN])
        s_sin = nc_sb(nc, c1, "q_sin", [128, WIN])
        d_cos, d_sin = Dep(), Dep()
        ps_q = nc_ps(nc, c1, "q_psq", [128, WIN])
        ps_k = nc_ps(nc, c1, "q_psk", [128, WIN])
        ps_v = nc_ps(nc, c1, "q_psv", [128, WIN])
        ps_sw = nc_ps(nc, c1, "q_pssw", [128, WIN])
        ps_t = nc_ps(nc, c1, "q_pst", [128, 128])
        d_psq, d_psk, d_psv, d_pssw, d_pst = (Dep() for _ in range(5))
        s_f = nc_sb(nc, c1, "q_f", [128, WIN])
        d_f = Dep()
        s_a = nc_sb(nc, c1, "q_a", [128, WIN])
        d_a = Dep()
        s_b = nc_sb(nc, c1, "q_b", [128, WIN])
        d_b = Dep()
        s_bh = nc_sb(nc, c1, "q_bh", [128, WIN], BF16)
        s_bl = nc_sb(nc, c1, "q_bl", [128, WIN], BF16)
        d_bh, d_bl = Dep(), Dep()
        s_ve = nc_sb(nc, c1, "q_ve", [128, WIN])
        d_ve = Dep()
        s_vt = nc_sb(nc, c1, "q_vt", [128, 128])
        d_vt = Dep()

        for h in range(HPC):
            st = h % 2
            for nm in ("q", "k", "v"):
                for half in ("hi", "lo"):
                    tr.dma("qs", s_w[nm, half, st][:],
                           sc["wT_%s_%s" % (nm, half)][h],
                           reads=[dep["wT"]], writes=[d_w[nm, half, st]])
            for w in range(NWIN):
                slot, coff = w // 2, (w % 2) * WIN
                seqpos = (w % (SEQ // WIN)) * WIN
                for g in range(NG):
                    for half, dd in (("hi", 0), ("lo", TPC)):
                        for sub in range(2):
                            ic0 = g * GCH + sub * (GCH // 2)
                            srcap = sc["gath1"][
                                slot * DM + ic0 * 128:
                                slot * DM + (ic0 + GCH // 2) * 128,
                                dd + coff:dd + coff + WIN].rearrange(
                                "(i p) j -> p i j", p=128)
                            dstap = s_x[half].ap()[
                                :, ic0 * WIN:(ic0 + GCH // 2) * WIN
                            ].rearrange("p (i j) -> p i j", j=WIN)
                            tr.dma("qs", dstap, srcap,
                                   reads=[dep["g1"]],
                                   writes=[d_x[half, g]])
                tr.dma("qs", s_cos[:], io["cosF"][:, seqpos:seqpos + WIN],
                       writes=[d_cos])
                tr.dma("qs", s_sin[:], io["sinF2"][:, seqpos:seqpos + WIN],
                       writes=[d_sin])
                for nm, pst, dps in (("q", ps_q, d_psq), ("k", ps_k, d_psk),
                                     ("v", ps_v, d_psv)):
                    for g in range(NG):
                        facs = []
                        for icc in range(GCH):
                            ic = g * GCH + icc
                            facs += mm3f(pst[:],
                                s_w[nm, "hi", st].ap()[
                                    :, ic * 128:(ic + 1) * 128],
                                s_w[nm, "lo", st].ap()[
                                    :, ic * 128:(ic + 1) * 128],
                                s_x["hi"].ap()[:, ic * WIN:(ic + 1) * WIN],
                                s_x["lo"].ap()[:, ic * WIN:(ic + 1) * WIN],
                                ic == 0, ic == NIC - 1)
                        tr.op_multi("te", facs,
                                    reads=[d_w[nm, "hi", st],
                                           d_w[nm, "lo", st],
                                           d_x["hi", g], d_x["lo", g]],
                                    writes=[dps])
                for nm, pst, dps, pre in (("q", ps_q, d_psq, "qs"),
                                          ("k", ps_k, d_psk, "ks")):
                    tr.op("ve", lambda e, _p=pst: e.tensor_copy(
                        s_f[:], _p[:]), reads=[dps], writes=[d_f])
                    tr.op("te", lambda e: e.matmul(
                        ps_sw[:], s_perm[:], s_f[:], start=True, stop=True),
                        reads=[d_perm, d_f], writes=[d_pssw])
                    tr.op("ve", lambda e: e.tensor_tensor(
                        s_a[:], s_f.ap()[:, :], s_cos.ap()[:, :], ALU.mult),
                        reads=[d_f, d_cos], writes=[d_a])
                    tr.op("ve", lambda e: e.tensor_tensor(
                        s_b[:], ps_sw.ap()[:, :], s_sin.ap()[:, :],
                        ALU.mult),
                        reads=[d_pssw, d_sin], writes=[d_b])
                    tr.op("ve", lambda e: e.tensor_tensor(
                        s_f[:], s_a.ap()[:, :], s_b.ap()[:, :], ALU.add),
                        reads=[d_a, d_b], writes=[d_f])
                    emit_split(tr, "ve", s_f[:], d_f, s_bh[:], d_bh,
                               s_bl[:], d_bl)
                    tr.dma("qs", sc[pre + "_hi"][h * HD:(h + 1) * HD,
                                                 w * WIN:(w + 1) * WIN],
                           s_bh[:], reads=[d_bh], writes=[dep["qk"]])
                    tr.dma("qs", sc[pre + "_lo"][h * HD:(h + 1) * HD,
                                                 w * WIN:(w + 1) * WIN],
                           s_bl[:], reads=[d_bl], writes=[dep["qk"]])
                tr.op("ve", lambda e: e.tensor_copy(s_ve[:], ps_v[:]),
                      reads=[d_psv], writes=[d_ve])
                for j in range(WIN // 128):
                    tr.op("te", lambda e, _j=j: e.transpose(
                        ps_t[:], s_ve.ap()[:, _j * 128:(_j + 1) * 128],
                        s_ident[:]),
                        reads=[d_ve, d_ident], writes=[d_pst])
                    tr.op("se", lambda e: e.copy(s_vt[:], ps_t[:]),
                          reads=[d_pst], writes=[d_vt])
                    tr.dma("qs",
                           sc["vs"][w * WIN + j * 128:
                                    w * WIN + (j + 1) * 128,
                                    h * HD:(h + 1) * HD],
                           s_vt[:], reads=[d_vt], writes=[dep["vs"]])


def phase_attn(nc, tr, io, sc, dep, s_ident, d_ident, s_mask, d_mask):
    KW = 512
    with ExitStack() as ca:
        s_k = {}
        d_k = {}
        for h in range(HPC):
            for half in ("hi", "lo"):
                s_k[h, half] = nc_sb(nc, ca, "a_k%d%s" % (h, half),
                                     [128, SEQ], BF16)
                d_k[h, half] = Dep()
        s_v = nc_sb(nc, ca, "a_v", [128, (SEQ // 128) * HPC * HD])
        d_v = Dep()
        s_q = {half: nc_sb(nc, ca, "a_q" + half, [128, HPC * HD], BF16)
               for half in ("hi", "lo")}
        d_q = {half: Dep() for half in ("hi", "lo")}
        s_S = nc_sb(nc, ca, "a_S", [128, SEQ])
        d_S = Dep()
        s_P = nc_sb(nc, ca, "a_P", [128, SEQ])
        d_P = Dep()
        s_m = nc_sb(nc, ca, "a_m", [128, 4])
        d_m = Dep()
        s_PT = nc_sb(nc, ca, "a_PT", [128, 256])
        d_PT = [Dep(), Dep()]
        s_stage = nc_sb(nc, ca, "a_stage", [128, HPC * HD])
        d_stage = Dep()
        ps_S = nc_ps(nc, ca, "a_psS", [128, KW])
        d_psS = Dep()
        ps_T = [nc_ps(nc, ca, "a_psT%d" % i, [128, 128]) for i in range(2)]
        d_psT = [Dep(), Dep()]
        ps_o = nc_ps(nc, ca, "a_pso", [128, HD])
        d_pso = Dep()

        for s in range(BATCH):
            for h in range(HPC):
                for half in ("hi", "lo"):
                    tr.dma("qs", s_k[h, half][:],
                           sc["ks_%s" % half][h * HD:(h + 1) * HD,
                                              s * SEQ:(s + 1) * SEQ],
                           reads=[dep["qk"]], writes=[d_k[h, half]])
            for j in range(SEQ // 128):
                tr.dma("qs", s_v[:, j * HPC * HD:(j + 1) * HPC * HD],
                       sc["vs"][s * SEQ + j * 128:s * SEQ + (j + 1) * 128, :],
                       reads=[dep["vs"]], writes=[d_v])
            for qb in range(SEQ // 128):
                nk = qb // 4 + 1
                nkc = nk * 4
                for half in ("hi", "lo"):
                    for h in range(HPC):
                        tr.dma("qs", s_q[half][:, h * HD:(h + 1) * HD],
                               sc["qs_%s" % half][
                                   h * HD:(h + 1) * HD,
                                   s * SEQ + qb * 128:
                                   s * SEQ + qb * 128 + 128],
                               reads=[dep["qk"]], writes=[d_q[half]])
                for h in range(HPC):
                    for kw in range(nk):
                        tr.op_multi("te", mm3f(ps_S[:],
                            s_q["hi"].ap()[:, h * HD:(h + 1) * HD],
                            s_q["lo"].ap()[:, h * HD:(h + 1) * HD],
                            s_k[h, "hi"].ap()[:, kw * KW:(kw + 1) * KW],
                            s_k[h, "lo"].ap()[:, kw * KW:(kw + 1) * KW],
                            True, True),
                            reads=[d_q["hi"], d_q["lo"],
                                   d_k[h, "hi"], d_k[h, "lo"]],
                            writes=[d_psS])
                        if kw == nk - 1:
                            moff = 384 - (qb % 4) * 128
                            tr.op("ve", lambda e, _k=kw, _m=moff:
                                  e.tensor_tensor(
                                      s_S[:, _k * KW:(_k + 1) * KW],
                                      ps_S.ap()[:, :],
                                      s_mask.ap()[:, _m:_m + KW], ALU.add),
                                  reads=[d_psS, d_mask], writes=[d_S])
                        else:
                            tr.op("se", lambda e, _k=kw: e.copy(
                                s_S[:, _k * KW:(_k + 1) * KW], ps_S[:]),
                                reads=[d_psS], writes=[d_S])
                    tr.op("ve", lambda e, _n=nk: e.reduce_max(
                        s_m[:, 0:1], s_S.ap()[:, :_n * KW], AX),
                        reads=[d_S], writes=[d_m])
                    tr.op("ve", lambda e: e.tensor_single_scalar(
                        s_m[:, 1:2], s_m[:, 0:1], -SCALE, ALU.mult),
                        reads=[d_m], writes=[d_m])
                    tr.op("se", lambda e, _n=nk: e.activation(
                        s_P[:, :_n * KW], s_S[:, :_n * KW], AF.Exp,
                        bias=s_m.ap()[:, 1:2], scale=SCALE,
                        accum_out=s_m[:, 2:3]),
                        reads=[d_S, d_m], writes=[d_P, d_m])
                    tr.op("ve", lambda e: e.reciprocal(
                        s_m[:, 3:4], s_m[:, 2:3]),
                        reads=[d_m], writes=[d_m])
                    for kc in range(nkc):
                        pi = kc % 2
                        tr.op("te", lambda e, _k=kc, _p=pi: e.transpose(
                            ps_T[_p][:],
                            s_P.ap()[:, _k * 128:(_k + 1) * 128],
                            s_ident[:]),
                            reads=[d_P, d_ident], writes=[d_psT[pi]])
                        if pi == 0:
                            tr.op("se", lambda e, _p=pi: e.copy(
                                s_PT[:, _p * 128:(_p + 1) * 128],
                                ps_T[_p][:]),
                                reads=[d_psT[pi]], writes=[d_PT[pi]])
                        else:
                            tr.op("ve", lambda e, _p=pi: e.tensor_copy(
                                s_PT[:, _p * 128:(_p + 1) * 128],
                                ps_T[_p][:]),
                                reads=[d_psT[pi]], writes=[d_PT[pi]])
                        tr.op("te", lambda e, _k=kc, _p=pi, _h=h: e.matmul(
                            ps_o[:],
                            s_PT.ap()[:, _p * 128:(_p + 1) * 128],
                            s_v.ap()[:, _k * HPC * HD + _h * HD:
                                     _k * HPC * HD + (_h + 1) * HD],
                            start=(_k == 0), stop=(_k == nkc - 1)),
                            reads=[d_PT[pi], d_v], writes=[d_pso])
                    tr.op("se", lambda e, _h=h: e.activation(
                        s_stage[:, _h * HD:(_h + 1) * HD], ps_o[:], AF.Copy,
                        scale=s_m.ap()[:, 3:4]),
                        reads=[d_pso, d_m], writes=[d_stage])
                tr.dma("qs",
                       sc["contrib2"][s * SEQ + qb * 128:
                                      s * SEQ + qb * 128 + 128, :],
                       s_stage[:], reads=[d_stage], writes=[dep["c2"]])


def phase_lora_prep(nc, tr, ctx, io, sc, dep, s_ident, d_ident):
    lora = {}
    s_tmp = nc_sb(nc, ctx, "lp_tmp", [128, 128])
    d_tmp = Dep()
    t_ps = nc_ps(nc, ctx, "lp_ps", [128, 128])
    d_ps = Dep()
    s_asrc = nc_sb(nc, ctx, "lp_asrc", [RK, 128])
    d_asrc = Dep()
    for nm, dim in (("a1", DM), ("a3", DM), ("a2", HID)):
        nch = dim // 128
        hi = nc_sb(nc, ctx, "lp_%s_hi" % nm, [128, nch * RK], BF16)
        lo = nc_sb(nc, ctx, "lp_%s_lo" % nm, [128, nch * RK], BF16)
        dhi, dlo = Dep(), Dep()
        for c in range(nch):
            tr.dma("qs", s_asrc[:], io[nm][:, c * 128:(c + 1) * 128],
                   writes=[d_asrc])
            tr.op("te", lambda e: e.transpose(
                t_ps[:, :RK], s_asrc.ap()[:, :],
                s_ident.ap()[0:RK, 0:RK]),
                reads=[d_asrc, d_ident], writes=[d_ps])
            tr.op("ve", lambda e: e.tensor_copy(s_tmp[:, :RK], t_ps[:, :RK]),
                  reads=[d_ps], writes=[d_tmp])
            emit_split(tr, "ve", s_tmp[:, :RK], d_tmp,
                       hi[:, c * RK:(c + 1) * RK], dhi,
                       lo[:, c * RK:(c + 1) * RK], dlo)
        lora[nm] = (hi, lo, dhi, dlo)
    t_ps16 = nc_ps(nc, ctx, "lp_ps16", [RK, 128])
    d_ps16 = Dep()
    s_t2 = nc_sb(nc, ctx, "lp_t2", [RK, 128])
    s_t2h = nc_sb(nc, ctx, "lp_t2h", [RK, 128], BF16)
    s_t2l = nc_sb(nc, ctx, "lp_t2l", [RK, 128], BF16)
    d_t2, d_t2h, d_t2l = Dep(), Dep(), Dep()
    s_s2 = nc_sb(nc, ctx, "lp_s2", [128, RK])
    d_s2 = Dep()
    for nm, rows in (("b1", HID), ("b3", HID), ("b2", DM)):
        for c in range(rows // 128):
            tr.dma("qs", s_s2[:], io[nm][c * 128:(c + 1) * 128, :],
                   writes=[d_s2])
            tr.op("te", lambda e: e.transpose(
                t_ps16[:, :], s_s2.ap()[:, :], s_ident[:]),
                reads=[d_s2, d_ident], writes=[d_ps16])
            tr.op("ve", lambda e: e.tensor_copy(s_t2[:, :], t_ps16[:, :]),
                  reads=[d_ps16], writes=[d_t2])
            emit_split(tr, "ve", s_t2[:], d_t2, s_t2h[:], d_t2h,
                       s_t2l[:], d_t2l)
            tr.dma("qs", sc["%sT_hi" % nm][:, c * 128:(c + 1) * 128],
                   s_t2h[:], reads=[d_t2h], writes=[dep["%sT" % nm]])
            tr.dma("qs", sc["%sT_lo" % nm][:, c * 128:(c + 1) * 128],
                   s_t2l[:], reads=[d_t2l], writes=[dep["%sT" % nm]])
    return lora


def phase_tail(nc, tr, io, sc, dep, qk, s_ident, d_ident):
    pid = nc.partition_id()
    base = nc.sync.snap(pid * TPC)
    with ExitStack() as c0:
        s_o = [nc_sb(nc, c0, "t_o%d" % i, [128, DM]) for i in range(2)]
        s_dq = nc_sb(nc, c0, "t_dq", [128, DM])
        s_hi = nc_sb(nc, c0, "t_hi", [128, DM], BF16)
        s_lo = nc_sb(nc, c0, "t_lo", [128, DM], BF16)
        d_o = [Dep(), Dep()]
        d_dq, d_hi, d_lo = (Dep() for _ in range(3))
        for tb in range(TPC // 128):
            bi = tb % 2
            off = nc.sync.snap(base + tb * 128, donate=True)
            g2v = sc["gath2"].rearrange("(j t) f -> j t f", j=NCORE)
            tr.dma("qs",
                   s_o[bi].ap().rearrange("p (j f) -> p j f", j=NCORE),
                   g2v[:, bass.ds(off, 128), :].transpose([1, 0, 2]),
                   reads=[dep["g2"]], writes=[d_o[bi]])
            emit_quant(tr, qk, s_o[bi], d_o[bi], s_dq, d_dq)
            emit_split(tr, "ve", s_dq[:], d_dq, s_hi[:], d_hi, s_lo[:], d_lo)
            tr.dma("qs", sc["odqn_hi"][tb * 128:(tb + 1) * 128, :], s_hi[:],
                   reads=[d_hi], writes=[dep["odqn"]])
            tr.dma("qs", sc["odqn_lo"][tb * 128:(tb + 1) * 128, :], s_lo[:],
                   reads=[d_lo], writes=[dep["odqn"]])
        tr.fence()
        s_tx = nc_sb(nc, c0, "t_tx", [128, TPC], BF16)
        d_tx = Dep()
        for half in ("hi", "lo"):
            for ic in range(NIC):
                tr.dma("qa", s_tx[:],
                       sc["odqn_%s" % half][:, ic * 128:(ic + 1) * 128],
                       reads=[dep["odqn"]], writes=[d_tx], transpose=True)
                for tb in range(TPC // 128):
                    tr.dma("qs",
                           sc["odqT_%s" % half][
                               :, tb * (NIC * 128) + ic * 128:
                               tb * (NIC * 128) + (ic + 1) * 128],
                           s_tx[:, tb * 128:(tb + 1) * 128],
                           reads=[d_tx], writes=[dep["odqT"]])

    with ExitStack() as c1:
        s_oT = {half: nc_sb(nc, c1, "t_oT" + half, [128, 2 * NIC * 128],
                            BF16) for half in ("hi", "lo")}
        d_oT = {half: Dep() for half in ("hi", "lo")}
        s_wo = {half: nc_sb(nc, c1, "t_wo" + half, [128, (NIC // 2) * 512],
                            BF16) for half in ("hi", "lo")}
        d_wo = {half: Dep() for half in ("hi", "lo")}
        s_xr = nc_sb(nc, c1, "t_xr", [128, 512])
        d_xr = Dep()
        s_x1 = [nc_sb(nc, c1, "t_x1_%d" % i, [128, DM]) for i in range(2)]
        d_x1 = [Dep(), Dep()]
        s_xn2 = nc_sb(nc, c1, "t_xn2", [128, DM])
        d_xn2 = Dep()
        s_nh = nc_sb(nc, c1, "t_nh", [128, DM], BF16)
        s_nl = nc_sb(nc, c1, "t_nl", [128, DM], BF16)
        d_nh, d_nl = Dep(), Dep()
        ps_a = [nc_ps(nc, c1, "t_psa%d" % i, [128, 512]) for i in range(2)]
        d_psa = [Dep(), Dep()]
        for tp in range(TPC // 256):
            for half in ("hi", "lo"):
                for tbl in range(2):
                    tb = tp * 2 + tbl
                    tr.dma("qs",
                           s_oT[half][:, tbl * NIC * 128:
                                      (tbl + 1) * NIC * 128],
                           sc["odqT_%s" % half][:, tb * NIC * 128:
                                                (tb + 1) * NIC * 128],
                           reads=[dep["odqT"]], writes=[d_oT[half]])
            for ow in range(NCORE):
                for ih in range(2):
                    for half in ("hi", "lo"):
                        for icc in range(NIC // 2):
                            ic = ih * (NIC // 2) + icc
                            c0_ = 2048 if half == "hi" else 2560
                            tr.dma("qs",
                                   s_wo[half][:, icc * 512:(icc + 1) * 512],
                                   sc["gath1"][ow * DM + ic * 128:
                                               ow * DM + (ic + 1) * 128,
                                               c0_:c0_ + 512],
                                   reads=[dep["g1"]], writes=[d_wo[half]])
                    for tbl in range(2):
                        facs = []
                        for icc in range(NIC // 2):
                            ic = ih * (NIC // 2) + icc
                            facs += mm3f(ps_a[tbl][:],
                                s_oT["hi"].ap()[
                                    :, (tbl * NIC + ic) * 128:
                                    (tbl * NIC + ic + 1) * 128],
                                s_oT["lo"].ap()[
                                    :, (tbl * NIC + ic) * 128:
                                    (tbl * NIC + ic + 1) * 128],
                                s_wo["hi"].ap()[:, icc * 512:
                                                (icc + 1) * 512],
                                s_wo["lo"].ap()[:, icc * 512:
                                                (icc + 1) * 512],
                                ih == 0 and icc == 0,
                                ih == 1 and icc == NIC // 2 - 1)
                        tr.op_multi("te", facs,
                                    reads=[d_oT["hi"], d_oT["lo"],
                                           d_wo["hi"], d_wo["lo"]],
                                    writes=[d_psa[tbl]])
                        if ih == 1:
                            tb = tp * 2 + tbl
                            tr.dma("qs", s_xr[:],
                                   io["x_my"][tb * 128:(tb + 1) * 128,
                                              ow * 512:(ow + 1) * 512],
                                   writes=[d_xr])
                            tr.op("ve", lambda e, _t=tbl, _o=ow:
                                  e.tensor_tensor(
                                      s_x1[_t][:, _o * 512:(_o + 1) * 512],
                                      ps_a[_t].ap()[:, :], s_xr.ap()[:, :],
                                      ALU.add),
                                  reads=[d_psa[tbl], d_xr],
                                  writes=[d_x1[tbl]])
            for tbl in range(2):
                tb = tp * 2 + tbl
                tr.dma("qs", sc["x1_s"][tb * 128:(tb + 1) * 128, :],
                       s_x1[tbl][:], reads=[d_x1[tbl]], writes=[dep["x1"]])
                emit_ln(tr, qk, s_x1[tbl], d_x1[tbl], s_xn2, d_xn2)
                emit_split(tr, "ve", s_xn2[:], d_xn2, s_nh[:], d_nh,
                           s_nl[:], d_nl)
                tr.dma("qs", sc["xn2n_hi"][tb * 128:(tb + 1) * 128, :],
                       s_nh[:], reads=[d_nh], writes=[dep["xn2n"]])
                tr.dma("qs", sc["xn2n_lo"][tb * 128:(tb + 1) * 128, :],
                       s_nl[:], reads=[d_nl], writes=[dep["xn2n"]])
        tr.fence()
        s_tx2 = nc_sb(nc, c1, "t_tx2", [128, TPC], BF16)
        d_tx2 = Dep()
        for half in ("hi", "lo"):
            for ic in range(NIC):
                tr.dma("qa", s_tx2[:],
                       sc["xn2n_%s" % half][:, ic * 128:(ic + 1) * 128],
                       reads=[dep["xn2n"]], writes=[d_tx2], transpose=True)
                for tb in range(TPC // 128):
                    tr.dma("qs",
                           sc["xn2T_%s" % half][
                               :, tb * (NIC * 128) + ic * 128:
                               tb * (NIC * 128) + (ic + 1) * 128],
                           s_tx2[:, tb * 128:(tb + 1) * 128],
                           reads=[d_tx2], writes=[dep["xn2T"]])

    with ExitStack() as cl:
        lora = phase_lora_prep(nc, tr, cl, io, sc, dep, s_ident, d_ident)
        phase_ffn_a(nc, tr, cl, io, sc, dep, lora, s_ident, d_ident)
    with ExitStack() as cw:
        phase_ffn_b(nc, tr, cw, io, sc, dep, s_ident, d_ident)


def phase_ffn_a(nc, tr, cl, io, sc, dep, lora, s_ident, d_ident):
    NHW = (HID + 511) // 512
    s_xT = {half: nc_sb(nc, cl, "f_xT" + half, [128, NIC * 128], BF16)
            for half in ("hi", "lo")}
    d_xT = {half: Dep() for half in ("hi", "lo")}
    a1h, a1l, da1h, da1l = lora["a1"]
    a3h, a3l, da3h, da3l = lora["a3"]
    ps_xa = nc_ps(nc, cl, "f_psxa", [128, 2 * RK])
    d_psxa = Dep()
    s_xa = nc_sb(nc, cl, "f_xa", [128, 2 * RK])
    d_xa = Dep()
    ps_xat = nc_ps(nc, cl, "f_psxat", [RK, 128])
    d_psxat = Dep()
    s_xat = {i: nc_sb(nc, cl, "f_xat%d" % i, [RK, TPC]) for i in (0, 1)}
    d_xat = {i: Dep() for i in (0, 1)}
    s_xath = {i: nc_sb(nc, cl, "f_xath%d" % i, [RK, TPC], BF16)
              for i in (0, 1)}
    s_xatl = {i: nc_sb(nc, cl, "f_xatl%d" % i, [RK, TPC], BF16)
              for i in (0, 1)}
    d_xath = {i: Dep() for i in (0, 1)}
    d_xatl = {i: Dep() for i in (0, 1)}
    for tb in range(TPC // 128):
        for half in ("hi", "lo"):
            tr.dma("qs", s_xT[half][:],
                   sc["xn2T_%s" % half][:, tb * NIC * 128:
                                        (tb + 1) * NIC * 128],
                   reads=[dep["xn2T"]], writes=[d_xT[half]])
        facs = []
        for ic in range(NIC):
            for ri, (rh, rl, cr) in enumerate(
                    ((a1h, a1l, 0), (a3h, a3l, RK))):
                combos = ((s_xT["hi"], rh), (s_xT["hi"], rl),
                          (s_xT["lo"], rh))
                for li, (lw, rx) in enumerate(combos):
                    first = (ic == 0 and ri == 0 and li == 0)
                    last = (ic == NIC - 1 and ri == 1 and li == 2)
                    facs.append(lambda e, _lw=lw, _rx=rx, _ic=ic, _cr=cr,
                                _f=first, _e=last:
                                e.matmul(ps_xa[:, _cr:_cr + RK],
                                         _lw.ap()[:, _ic * 128:
                                                  (_ic + 1) * 128],
                                         _rx.ap()[:, _ic * RK:
                                                  (_ic + 1) * RK],
                                         start=_f, stop=_e))
        tr.op_multi("te", facs,
                    reads=[d_xT["hi"], d_xT["lo"], da1h, da1l, da3h, da3l],
                    writes=[d_psxa])
        tr.op("se", lambda e: e.activation(
            s_xa[:], ps_xa[:], AF.Copy, scale=2.0),
            reads=[d_psxa], writes=[d_xa])
        for i in (0, 1):
            tr.op("te", lambda e, _i=i: e.transpose(
                ps_xat[:, :], s_xa.ap()[:, _i * RK:(_i + 1) * RK],
                s_ident[:]),
                reads=[d_xa, d_ident], writes=[d_psxat])
            tr.op("ve", lambda e, _t=tb, _i=i: e.tensor_copy(
                s_xat[_i][:, _t * 128:(_t + 1) * 128], ps_xat[:, :]),
                reads=[d_psxat], writes=[d_xat[i]])
    for i in (0, 1):
        emit_split(tr, "ve", s_xat[i][:], d_xat[i], s_xath[i][:],
                   d_xath[i], s_xatl[i][:], d_xatl[i])

    s_wf = {half: nc_sb(nc, cl, "f_wf" + half, [128, NIC * 512], BF16)
            for half in ("hi", "lo")}
    d_wf = {(half, g): Dep() for half in ("hi", "lo") for g in range(2)}
    s_bT = {half: nc_sb(nc, cl, "f_bT" + half, [RK, 512], BF16)
            for half in ("hi", "lo")}
    d_bT = {half: Dep() for half in ("hi", "lo")}
    s_xT2 = [{half: nc_sb(nc, cl, "f_xT2%s%d" % (half, i),
                          [128, NIC * 128], BF16) for half in ("hi", "lo")}
             for i in range(2)]
    d_xT2 = [{half: Dep() for half in ("hi", "lo")} for i in range(2)]
    s_h1 = nc_sb(nc, cl, "f_h1", [128, 8 * 512])
    d_h1 = Dep()
    s_g = nc_sb(nc, cl, "f_g", [128, 512])
    d_g = Dep()
    s_gh = nc_sb(nc, cl, "f_gh", [128, 512], BF16)
    s_gl = nc_sb(nc, cl, "f_gl", [128, 512], BF16)
    d_gh, d_gl = Dep(), Dep()
    ps_h = nc_ps(nc, cl, "f_psh", [128, 512])
    d_psh = Dep()
    GC2 = NIC // 2
    for hw in range(NHW):
        W = min(512, HID - 512 * hw)
        for sub, wsrc, bnm in ((0, "w1n", "b1T"), (1, "w3n", "b3T")):
            for g in range(2):
                for half in ("hi", "lo"):
                    for icc in range(GC2):
                        ic = g * GC2 + icc
                        tr.dma("qa", s_wf[half][:, ic * 512:ic * 512 + W],
                               sc["%s_%s" % (wsrc, half)][
                                   hw * 512:hw * 512 + W,
                                   ic * 128:(ic + 1) * 128],
                               reads=[dep[wsrc]], writes=[d_wf[half, g]],
                               transpose=True)
            for half in ("hi", "lo"):
                tr.dma("qs", s_bT[half][:, :W],
                       sc["%s_%s" % (bnm, half)][:, hw * 512:hw * 512 + W],
                       reads=[dep[bnm]], writes=[d_bT[half]])
            for tb in range(TPC // 128):
                bi = tb % 2
                for half in ("hi", "lo"):
                    tr.dma("qs", s_xT2[bi][half][:],
                           sc["xn2T_%s" % half][:, tb * NIC * 128:
                                                (tb + 1) * NIC * 128],
                           reads=[dep["xn2T"]],
                           writes=[d_xT2[bi][half]])
                for g in range(2):
                    facs = []
                    for icc in range(GC2):
                        ic = g * GC2 + icc
                        facs += mm3f(ps_h[:, :W],
                            s_xT2[bi]["hi"].ap()[:, ic * 128:(ic + 1) * 128],
                            s_xT2[bi]["lo"].ap()[:, ic * 128:(ic + 1) * 128],
                            s_wf["hi"].ap()[:, ic * 512:ic * 512 + W],
                            s_wf["lo"].ap()[:, ic * 512:ic * 512 + W],
                            ic == 0, False)
                    reads = [d_xT2[bi]["hi"], d_xT2[bi]["lo"],
                             d_wf["hi", g], d_wf["lo", g]]
                    if g == 1:
                        combos = ((s_xath[sub], s_bT["hi"]),
                                  (s_xath[sub], s_bT["lo"]),
                                  (s_xatl[sub], s_bT["hi"]))
                        for li, (lw, rx) in enumerate(combos):
                            facs.append(lambda e, _lw=lw, _rx=rx, _tb=tb,
                                        _W=W, _e=(li == 2):
                                        e.matmul(ps_h[:, :_W],
                                                 _lw.ap()[:, _tb * 128:
                                                          (_tb + 1) * 128],
                                                 _rx.ap()[:, :_W],
                                                 start=False, stop=_e))
                        reads += [d_xath[sub], d_xatl[sub],
                                  d_bT["hi"], d_bT["lo"]]
                    tr.op_multi("te", facs, reads=reads, writes=[d_psh])
                if sub == 0:
                    tr.op("se", lambda e, _t=tb, _W=W: e.activation(
                        s_h1[:, _t * 512:_t * 512 + _W], ps_h[:, :_W],
                        AF.Silu), reads=[d_psh], writes=[d_h1])
                else:
                    tr.op("ve", lambda e, _t=tb, _W=W: e.tensor_tensor(
                        s_g[:, :_W], s_h1.ap()[:, _t * 512:_t * 512 + _W],
                        ps_h.ap()[:, :_W], ALU.mult),
                        reads=[d_h1, d_psh], writes=[d_g])
                    emit_split(tr, "ve", s_g[:, :W], d_g, s_gh[:, :W],
                               d_gh, s_gl[:, :W], d_gl)
                    tr.dma("qs", sc["gn_hi"][tb * 128:(tb + 1) * 128,
                                             hw * 512:hw * 512 + W],
                           s_gh[:, :W], reads=[d_gh], writes=[dep["gn"]])
                    tr.dma("qs", sc["gn_lo"][tb * 128:(tb + 1) * 128,
                                             hw * 512:hw * 512 + W],
                           s_gl[:, :W], reads=[d_gl], writes=[dep["gn"]])


def phase_ffn_b(nc, tr, cw, io, sc, dep, s_ident, d_ident):
    tr.fence()
    s_tg = nc_sb(nc, cw, "f_tg", [128, TPC], BF16)
    d_tg = Dep()
    for half in ("hi", "lo"):
        for hc in range(NHC):
            tr.dma("qa", s_tg[:],
                   sc["gn_%s" % half][:, hc * 128:(hc + 1) * 128],
                   reads=[dep["gn"]], writes=[d_tg], transpose=True)
            for tb in range(TPC // 128):
                tr.dma("qs",
                       sc["gT_%s" % half][:, tb * (NHC * 128) + hc * 128:
                                          tb * (NHC * 128) + (hc + 1) * 128],
                       s_tg[:, tb * 128:(tb + 1) * 128],
                       reads=[d_tg], writes=[dep["gT"]])

    s_tmp = nc_sb(nc, cw, "fb_tmp", [128, 128])
    d_tmp = Dep()
    t_ps = nc_ps(nc, cw, "fb_ps", [128, 128])
    d_ps = Dep()
    s_a2src = nc_sb(nc, cw, "fb_a2src", [RK, 128])
    d_a2src = Dep()
    a2h = nc_sb(nc, cw, "fb_a2h", [128, NHC * RK], BF16)
    a2l = nc_sb(nc, cw, "fb_a2l", [128, NHC * RK], BF16)
    da2h, da2l = Dep(), Dep()
    for c in range(NHC):
        tr.dma("qs", s_a2src[:], io["a2"][:, c * 128:(c + 1) * 128],
               writes=[d_a2src])
        tr.op("te", lambda e: e.transpose(
            t_ps[:, :RK], s_a2src.ap()[:, :],
            s_ident.ap()[0:RK, 0:RK]),
            reads=[d_a2src, d_ident], writes=[d_ps])
        tr.op("ve", lambda e: e.tensor_copy(s_tmp[:, :RK], t_ps[:, :RK]),
              reads=[d_ps], writes=[d_tmp])
        emit_split(tr, "ve", s_tmp[:, :RK], d_tmp,
                   a2h[:, c * RK:(c + 1) * RK], da2h,
                   a2l[:, c * RK:(c + 1) * RK], da2l)

    s_gT = {half: nc_sb(nc, cw, "fb_gT" + half, [128, NHC * 128], BF16)
            for half in ("hi", "lo")}
    d_gT = {(half, g): Dep() for half in ("hi", "lo") for g in range(2)}
    ps_ga = nc_ps(nc, cw, "fb_psga", [128, RK])
    d_psga = Dep()
    s_ga = nc_sb(nc, cw, "fb_ga", [128, RK])
    d_ga = Dep()
    ps_gat = nc_ps(nc, cw, "fb_psgat", [RK, 128])
    d_psgat = Dep()
    s_gat = nc_sb(nc, cw, "fb_gat", [RK, TPC])
    d_gat = Dep()
    s_gath = nc_sb(nc, cw, "fb_gath", [RK, TPC], BF16)
    s_gatl = nc_sb(nc, cw, "fb_gatl", [RK, TPC], BF16)
    d_gath, d_gatl = Dep(), Dep()
    for tb in range(TPC // 128):
        for half in ("hi", "lo"):
            tr.dma("qs", s_gT[half][:],
                   sc["gT_%s" % half][:, tb * NHC * 128:
                                      (tb + 1) * NHC * 128],
                   reads=[dep["gT"]],
                   writes=[d_gT[half, 0], d_gT[half, 1]])
        facs = []
        for hc in range(NHC):
            combos = ((s_gT["hi"], a2h), (s_gT["hi"], a2l),
                      (s_gT["lo"], a2h))
            for li, (lw, rx) in enumerate(combos):
                facs.append(lambda e, _lw=lw, _rx=rx, _hc=hc,
                            _f=(hc == 0 and li == 0),
                            _e=(hc == NHC - 1 and li == 2):
                            e.matmul(ps_ga[:],
                                     _lw.ap()[:, _hc * 128:(_hc + 1) * 128],
                                     _rx.ap()[:, _hc * RK:(_hc + 1) * RK],
                                     start=_f, stop=_e))
        tr.op_multi("te", facs,
                    reads=[d_gT["hi", 0], d_gT["hi", 1],
                           d_gT["lo", 0], d_gT["lo", 1], da2h, da2l],
                    writes=[d_psga])
        tr.op("se", lambda e: e.activation(
            s_ga[:], ps_ga[:], AF.Copy, scale=2.0),
            reads=[d_psga], writes=[d_ga])
        tr.op("te", lambda e: e.transpose(
            ps_gat[:, :], s_ga.ap()[:, :], s_ident[:]),
            reads=[d_ga, d_ident], writes=[d_psgat])
        tr.op("ve", lambda e, _t=tb: e.tensor_copy(
            s_gat[:, _t * 128:(_t + 1) * 128], ps_gat[:, :]),
            reads=[d_psgat], writes=[d_gat])
    emit_split(tr, "ve", s_gat[:], d_gat, s_gath[:], d_gath,
               s_gatl[:], d_gatl)

    OW = 256
    NOW = DM // OW
    s_w2 = {half: nc_sb(nc, cw, "fb_w2" + half, [128, NHC * OW],
                        BF16) for half in ("hi", "lo")}
    d_w2 = {(half, g): Dep() for half in ("hi", "lo") for g in range(2)}
    s_b2 = {half: nc_sb(nc, cw, "fb_b2" + half, [RK, OW], BF16)
            for half in ("hi", "lo")}
    d_b2 = {half: Dep() for half in ("hi", "lo")}
    s_x1t = nc_sb(nc, cw, "fb_x1t", [128, OW])
    d_x1t = Dep()
    s_out = nc_sb(nc, cw, "fb_out", [128, OW])
    d_out = Dep()
    ps_o2 = nc_ps(nc, cw, "fb_pso2", [128, OW])
    d_pso2 = Dep()
    for ow in range(NOW):
        for half in ("hi", "lo"):
            tr.dma("qs", s_b2[half][:, :],
                   sc["b2T_%s" % half][:, ow * OW:(ow + 1) * OW],
                   reads=[dep["b2T"]], writes=[d_b2[half]])
            for g in range(2):
                for hcx in range(43):
                    hc = g * 43 + hcx
                    if hc >= NHC:
                        continue
                    tr.dma("qa", s_w2[half][:, hc * OW:(hc + 1) * OW],
                           sc["w2n_%s" % half][ow * OW:(ow + 1) * OW,
                                               hc * 128:(hc + 1) * 128],
                           reads=[dep["w2n"]], writes=[d_w2[half, g]],
                           transpose=True)
        for tb in range(TPC // 128):
            for g in range(2):
                h0, h1_ = g * 43, min((g + 1) * 43, NHC)
                for half in ("hi", "lo"):
                    tr.dma("qs",
                           s_gT[half][:, h0 * 128:h1_ * 128],
                           sc["gT_%s" % half][:, tb * NHC * 128 + h0 * 128:
                                              tb * NHC * 128 + h1_ * 128],
                           reads=[dep["gT"]], writes=[d_gT[half, g]])
            for g in range(2):
                h0, h1_ = g * 43, min((g + 1) * 43, NHC)
                facs = []
                for hc in range(h0, h1_):
                    facs += mm3f(ps_o2[:],
                        s_gT["hi"].ap()[:, hc * 128:(hc + 1) * 128],
                        s_gT["lo"].ap()[:, hc * 128:(hc + 1) * 128],
                        s_w2["hi"].ap()[:, hc * OW:(hc + 1) * OW],
                        s_w2["lo"].ap()[:, hc * OW:(hc + 1) * OW],
                        hc == 0, False)
                reads = [d_gT["hi", g], d_gT["lo", g], d_w2["hi", g],
                         d_w2["lo", g]]
                if g == 1:
                    combos = ((s_gath, s_b2["hi"]), (s_gath, s_b2["lo"]),
                              (s_gatl, s_b2["hi"]))
                    for li, (lw, rx) in enumerate(combos):
                        facs.append(lambda e, _lw=lw, _rx=rx, _tb=tb,
                                    _e=(li == 2):
                                    e.matmul(ps_o2[:],
                                             _lw.ap()[:, _tb * 128:
                                                      (_tb + 1) * 128],
                                             _rx.ap()[:, :],
                                             start=False, stop=_e))
                    reads += [d_gath, d_gatl, d_b2["hi"], d_b2["lo"]]
                tr.op_multi("te", facs, reads=reads, writes=[d_pso2])
            tr.dma("qs", s_x1t[:],
                   sc["x1_s"][tb * 128:(tb + 1) * 128,
                              ow * OW:(ow + 1) * OW],
                   reads=[dep["x1"]], writes=[d_x1t])
            tr.op("ve", lambda e: e.tensor_tensor(
                s_out[:], ps_o2.ap()[:, :], s_x1t.ap()[:, :], ALU.add),
                reads=[d_pso2, d_x1t], writes=[d_out])
            tr.dma("qs",
                   io["out"][tb * 128:(tb + 1) * 128, ow * OW:(ow + 1) * OW],
                   s_out[:], reads=[d_out], writes=[dep["out"]])


_NC_CACHE = [None]
_LAST_RES = [None]


def _host_consts():
    import jax
    import jax.numpy as jnp
    cpu = jax.devices("cpu")[0]
    with jax.default_device(cpu):
        inv = 1.0 / (10000.0 ** (jnp.arange(0, HD, 2, dtype=jnp.float32)
                                 / HD))
        ang = jnp.arange(SEQ, dtype=jnp.float32)[:, None] * inv[None, :]
        cos = np.ascontiguousarray(np.asarray(jnp.cos(ang)).T)
        sin = np.ascontiguousarray(np.asarray(jnp.sin(ang)).T)
    cosF = np.ascontiguousarray(np.vstack([cos, cos]))
    sinF2 = np.ascontiguousarray(np.vstack([-sin, sin]))
    mask = np.where(np.arange(896)[None, :] <= np.arange(128)[:, None] + 384,
                    np.float32(0.0), np.float32(NEGBIG)).astype(np.float32)
    ident = np.eye(128, dtype=np.float32)
    perm = np.zeros((128, 128), dtype=np.float32)
    for m in range(128):
        perm[(m + 64) % 128, m] = 1.0
    return cosF, sinF2, mask, ident, perm


def kernel(L0_attn_hidden_states, L0_attn_Wq, L0_attn_Wk, L0_attn_Wv,
           L0_attn_Wo, L0_ffn_w1, L0_ffn_w3, L0_ffn_w2, L0_ffn_lora_A1,
           L0_ffn_lora_B1, L0_ffn_lora_A3, L0_ffn_lora_B3, L0_ffn_lora_A2,
           L0_ffn_lora_B2):
    f32 = lambda a: np.ascontiguousarray(np.asarray(a, dtype=np.float32))
    x = f32(L0_attn_hidden_states).reshape(TOK, DM)
    wq, wk, wv, wo = (f32(L0_attn_Wq), f32(L0_attn_Wk), f32(L0_attn_Wv),
                      f32(L0_attn_Wo))
    w1, w3, w2 = f32(L0_ffn_w1), f32(L0_ffn_w3), f32(L0_ffn_w2)
    a1, b1 = f32(L0_ffn_lora_A1), f32(L0_ffn_lora_B1)
    a3, b3 = f32(L0_ffn_lora_A3), f32(L0_ffn_lora_B3)
    a2, b2 = f32(L0_ffn_lora_A2), f32(L0_ffn_lora_B2)
    cosF, sinF2, mask, ident, perm = _host_consts()

    if _NC_CACHE[0] is None:
        _NC_CACHE[0] = build_nc()
    nc = _NC_CACHE[0]

    in_maps = []
    for c in range(NCORE):
        hs = slice(c * HPC * HD, (c + 1) * HPC * HD)
        in_maps.append({
            "x_my": x[c * TPC:(c + 1) * TPC],
            "wq_my": wq[hs], "wk_my": wk[hs], "wv_my": wv[hs],
            "wo_my": wo[hs],
            "w1": w1, "w3": w3, "w2": w2,
            "a1": a1, "b1": b1, "a3": a3, "b3": b3, "a2": a2, "b2": b2,
            "cosF": cosF, "sinF2": sinF2, "maskS": mask, "identf": ident,
            "permF": perm,
        })
    res = run_bass_kernel_spmd(nc, in_maps, core_ids=list(range(NCORE)))
    _LAST_RES[0] = res.results
    outs = np.concatenate([np.asarray(res.results[c]["out"])
                           for c in range(NCORE)], axis=0)
    return outs.reshape(BATCH, SEQ, DM)


# version 10 (final)
# speedup vs baseline: 1.1762x; 1.0143x over previous
import math
import numpy as np
from contextlib import ExitStack

import concourse.bass as bass
import concourse.mybir as mybir
from concourse.expressions import RuntimeValue
from concourse.bass_utils import run_bass_kernel_spmd

F32 = mybir.dt.float32
BF16 = mybir.dt.bfloat16
AF = mybir.ActivationFunctionType
ALU = mybir.AluOpType
AX = mybir.AxisListType.X

NCORE = 8
BATCH = 2
SEQ = 4096
DM = 4096
NH = 32
HD = 128
HID = 11008
RK = 16
TOK = BATCH * SEQ
TPC = TOK // NCORE
HPC = NH // NCORE
NIC = DM // 128
NHC = HID // 128
SCALE = 1.0 / math.sqrt(HD)
EPS = 1e-5
NEGBIG = -3.0e38
SEGS = [(0, 2048, 4), (2048, 3072, 6), (3072, 4096, 8)]
DBG = ()


class Dep:
    __slots__ = ("w", "readers")

    def __init__(self):
        self.w = None
        self.readers = []


class Tr:
    def __init__(self, nc, ctx):
        self.nc = nc
        self.eng = {"te": nc.tensor, "ve": nc.vector, "se": nc.scalar,
                    "pe": nc.gpsimd, "sy": nc.sync}
        self.sems = {}
        self.cnt = {}
        for name in ("te", "ve", "se", "pe", "qs", "qa", "qg", "cc"):
            self.sems[name] = ctx.enter_context(nc.semaphore("sem_" + name))
            self.cnt[name] = 0
        self.wm = {}

    def _wait(self, en, need):
        e = self.eng[en]
        for s, v in sorted(need.items()):
            if v > 0 and self.wm.get((en, s), 0) < v:
                e.wait_ge(self.sems[s], v)
                self.wm[(en, s)] = v

    def op(self, en, factory, reads=(), writes=(), sem=None, step=1):
        need = {}
        for d in reads:
            if d.w:
                s, v = d.w
                if need.get(s, 0) < v:
                    need[s] = v
        for d in writes:
            if d.w:
                s, v = d.w
                if need.get(s, 0) < v:
                    need[s] = v
            for s, v in d.readers:
                if need.get(s, 0) < v:
                    need[s] = v
        self._wait(en, need)
        inst = factory(self.eng[en])
        s = sem if sem is not None else en
        inst.then_inc(self.sems[s], step)
        self.cnt[s] += step
        stamp = (s, self.cnt[s])
        for d in writes:
            d.w = stamp
            d.readers = []
        for d in reads:
            d.readers.append(stamp)
        return inst

    def op_multi(self, en, factories, reads=(), writes=(), sem=None,
                 step=1):
        need = {}
        for d in reads:
            if d.w:
                s, v = d.w
                if need.get(s, 0) < v:
                    need[s] = v
        for d in writes:
            if d.w:
                s, v = d.w
                if need.get(s, 0) < v:
                    need[s] = v
            for s, v in d.readers:
                if need.get(s, 0) < v:
                    need[s] = v
        self._wait(en, need)
        e = self.eng[en]
        for f in factories[:-1]:
            f(e)
        inst = factories[-1](e)
        s = sem if sem is not None else en
        inst.then_inc(self.sems[s], step)
        self.cnt[s] += step
        stamp = (s, self.cnt[s])
        for d in writes:
            d.w = stamp
            d.readers = []
        for d in reads:
            d.readers.append(stamp)
        return inst

    def dma(self, queue, out, in_, reads=(), writes=(), transpose=False):
        en = {"qs": "sy", "qa": "se", "qg": "pe"}[queue]
        if transpose:
            fac = lambda e: e.dma_start(out=out, in_=in_, transpose=True)
        else:
            fac = lambda e: e.dma_start(out=out, in_=in_)
        return self.op(en, fac, reads=reads, writes=writes, sem=queue,
                       step=16)

    def wait_all(self, en):
        self._wait(en, dict(self.cnt))

    def fence(self):
        for en in ("te", "ve", "se", "pe", "sy"):
            self._wait(en, dict(self.cnt))


def nc_sb(nc, ctx, name, shape, dtype=F32):
    return ctx.enter_context(nc.sbuf_tensor(name, shape, dtype))


def nc_ps(nc, ctx, name, shape, dtype=F32):
    return ctx.enter_context(nc.psum_tensor(name, shape, dtype))


def emit_split(tr, en, src_ap, d_src, hi_ap, d_hi, lo_ap, d_lo):
    tr.op(en, lambda e: e.tensor_copy(hi_ap, src_ap),
          reads=[d_src], writes=[d_hi])
    tr.op(en, lambda e: e.tensor_tensor(lo_ap, src_ap, hi_ap, ALU.subtract),
          reads=[d_src, d_hi], writes=[d_lo])


_QK_SEQ = [0]


class QuantKit:
    def __init__(self, nc, ctx):
        _QK_SEQ[0] += 1
        p = "qk%d_" % _QK_SEQ[0]
        self.amax = nc_sb(nc, ctx, p + "amax", [128, 128])
        self.rs = nc_sb(nc, ctx, p + "rs", [128, 128])
        self.sneg = nc_sb(nc, ctx, p + "sneg", [128, 128])
        self.u = nc_sb(nc, ctx, p + "u", [128, 2048])
        self.c = nc_sb(nc, ctx, p + "c", [128, 2048])
        self.stats = nc_sb(nc, ctx, p + "stats", [128, 64])
        self.mv = nc_sb(nc, ctx, p + "mv", [128, 8])
        (self.d_amax, self.d_rs, self.d_sneg, self.d_u, self.d_c,
         self.d_stats, self.d_mv) = (Dep() for _ in range(7))


def emit_quant(tr, qk, t_in, d_in, t_dq, d_dq):
    for (s0, s1, bits) in SEGS:
        w = s1 - s0
        G = w // 32
        maxq = float(2 ** (bits - 1) - 1)
        shift = float(2 ** (bits - 1))
        src3 = t_in.ap()[:, s0:s1].rearrange("p (g k) -> p g k", k=32)
        tr.op("ve", lambda e, _s=src3, _G=G: e.tensor_reduce(
            qk.amax[:, :_G], _s, AX, ALU.max, apply_absolute_value=True),
            reads=[d_in], writes=[qk.d_amax])
        tr.op("ve", lambda e, _G=G: e.tensor_single_scalar(
            qk.amax[:, :_G], qk.amax[:, :_G], 1e-12, ALU.max),
            reads=[qk.d_amax], writes=[qk.d_amax])
        tr.op("ve", lambda e, _G=G: e.reciprocal(
            qk.rs[:, :_G], qk.amax[:, :_G]),
            reads=[qk.d_amax], writes=[qk.d_rs])
        rs_b = qk.rs.ap()[:, :G].unsqueeze(2).broadcast_to([128, G, 32])
        u3 = qk.u.ap()[:, :w].rearrange("p (g k) -> p g k", k=32)
        tr.op("ve", lambda e, _s=src3, _m=maxq, _r=rs_b, _u=u3:
              e.scalar_tensor_tensor(_u, _s, _m, _r, ALU.mult, ALU.mult),
              reads=[d_in, qk.d_rs], writes=[qk.d_u])
        tr.op("ve", lambda e, _a=s0, _b=s1, _w=w: e.tensor_copy(
            t_dq[:, _a:_b], qk.u[:, :_w]),
            reads=[qk.d_u], writes=[d_dq])
        for i in range(bits):
            bbit = float(2 ** (bits - 1 - i))
            thr = bbit - 0.5 - shift
            tr.op("ve", lambda e, _t=thr, _w=w: e.tensor_single_scalar(
                qk.c[:, :_w], qk.u[:, :_w], _t, ALU.is_ge),
                reads=[qk.d_u], writes=[qk.d_c])
            tr.op("ve", lambda e, _b=bbit, _w=w: e.scalar_tensor_tensor(
                qk.u[:, :_w], qk.c[:, :_w], -_b, qk.u[:, :_w],
                ALU.mult, ALU.add),
                reads=[qk.d_c, qk.d_u], writes=[qk.d_u])
        tr.op("ve", lambda e, _s=shift, _w=w, _a=s0, _b=s1:
              e.scalar_tensor_tensor(qk.c[:, :_w], qk.u[:, :_w], _s,
                                     t_dq[:, _a:_b], ALU.add, ALU.subtract),
              reads=[qk.d_u, d_dq], writes=[qk.d_c])
        tr.op("ve", lambda e, _G=G, _m=maxq: e.tensor_single_scalar(
            qk.sneg[:, :_G], qk.amax[:, :_G], -1.0 / _m, ALU.mult),
            reads=[qk.d_amax], writes=[qk.d_sneg])
        sneg_b = qk.sneg.ap()[:, :G].unsqueeze(2).broadcast_to([128, G, 32])
        dq3 = t_dq.ap()[:, s0:s1].rearrange("p (g k) -> p g k", k=32)
        c3 = qk.c.ap()[:, :w].rearrange("p (g k) -> p g k", k=32)
        tr.op("ve", lambda e, _d=dq3, _c=c3, _sb=sneg_b: e.tensor_tensor(
            _d, _c, _sb, ALU.mult),
            reads=[qk.d_c, qk.d_sneg], writes=[d_dq])


def emit_ln(tr, qk, t_x, d_x, t_out, d_out):
    nch = DM // 512
    for c in range(nch):
        tr.op("ve", lambda e, _c=c: e.bn_stats(
            qk.stats[:, 6 * _c:6 * _c + 6],
            t_x[:, 512 * _c:512 * (_c + 1)]),
            reads=[d_x], writes=[qk.d_stats])
    tr.op("ve", lambda e, _n=nch: e.bn_aggr(
        qk.mv[:, 0:2], qk.stats[:, 0:6 * _n]),
        reads=[qk.d_stats], writes=[qk.d_mv])
    tr.op("ve", lambda e: e.tensor_single_scalar(
        qk.mv[:, 2:3], qk.mv[:, 1:2], EPS, ALU.add),
        reads=[qk.d_mv], writes=[qk.d_mv])
    tr.op("ve", lambda e: e.reciprocal(qk.mv[:, 3:4], qk.mv[:, 2:3]),
          reads=[qk.d_mv], writes=[qk.d_mv])
    tr.op("se", lambda e: e.activation(qk.mv[:, 4:5], qk.mv[:, 3:4], AF.Sqrt),
          reads=[qk.d_mv], writes=[qk.d_mv])
    tr.op("ve", lambda e: e.tensor_scalar(
        t_out[:], t_x[:], qk.mv.ap()[:, 0:1], qk.mv.ap()[:, 4:5],
        ALU.subtract, ALU.mult),
        reads=[d_x, qk.d_mv], writes=[d_out])


def mm3f(pst, lh, ll, rh, rl, first, last):
    out = []
    combos = ((lh, rh), (lh, rl), (ll, rh))
    for li, (lw, rx) in enumerate(combos):
        out.append(lambda e, _l=lw, _r=rx, _s=(first and li == 0),
                   _e=(last and li == 2), _p=pst:
                   e.matmul(_p, _l, _r, start=_s, stop=_e))
    return out


def build_nc():
    nc = bass.Bass("TRN2", target_bir_lowering=False, debug=False,
                   num_devices=NCORE)

    def din(name, shape, dtype=F32):
        return nc.dram_tensor(name, shape, dtype, kind="ExternalInput").ap()

    def dint(name, shape, dtype):
        return nc.dram_tensor(name, shape, dtype, kind="Internal").ap()

    io = {}
    io["x_my"] = din("x_my", [TPC, DM])
    for nm in ("q", "k", "v", "o"):
        io["w%s_my" % nm] = din("w%s_my" % nm, [HPC * HD, DM])
    io["w1"] = din("w1", [HID, DM])
    io["w3"] = din("w3", [HID, DM])
    io["w2"] = din("w2", [DM, HID])
    for nm in ("a1", "b1", "a3", "b3", "a2", "b2"):
        shp = {"a1": [RK, DM], "a3": [RK, DM], "a2": [RK, HID],
               "b1": [HID, RK], "b3": [HID, RK], "b2": [DM, RK]}[nm]
        io[nm] = din(nm, shp)
    io["cosF"] = din("cosF", [128, SEQ])
    io["sinF2"] = din("sinF2", [128, SEQ])
    io["maskS"] = din("maskS", [128, 896])
    io["identf"] = din("identf", [128, 128])
    io["permF"] = din("permF", [128, 128])
    io["out"] = nc.dram_tensor("out", [TPC, DM], F32,
                               kind="ExternalOutput").ap()

    sc = {}
    sc["contrib1"] = dint("contrib1", [DM, 3072], BF16)
    sc["gath1"] = nc.dram_tensor("gath1", [NCORE * DM, 3072], BF16, kind="Internal", addr_space="Shared").ap()
    sc["contrib2"] = dint("contrib2", [TOK, HPC * HD], F32)
    sc["gath2"] = nc.dram_tensor("gath2", [NCORE * TOK, HPC * HD], F32, kind="Internal", addr_space="Shared").ap()
    sc["xq_hi_n"] = dint("xq_hi_n", [TPC, DM], BF16)
    sc["xq_lo_n"] = dint("xq_lo_n", [TPC, DM], BF16)
    for nm in ("q", "k", "v", "o"):
        for h in ("hi", "lo"):
            sc["wn_%s_%s" % (nm, h)] = dint("wn_%s_%s" % (nm, h),
                                            [HPC * HD, DM], BF16)
    for nm in ("q", "k", "v"):
        for h in ("hi", "lo"):
            sc["wT_%s_%s" % (nm, h)] = dint("wT_%s_%s" % (nm, h),
                                            [HPC, 128, DM], BF16)
    for t in ("q", "k"):
        for h in ("hi", "lo"):
            sc["%ss_%s" % (t, h)] = dint("%ss_%s" % (t, h),
                                         [HPC * HD, TOK], BF16)
    sc["vs"] = dint("vs", [TOK, HPC * HD], F32)
    for wnm, rows, cols in (("w1", HID, DM), ("w3", HID, DM),
                            ("w2", DM, HID)):
        for h in ("hi", "lo"):
            sc["%sn_%s" % (wnm, h)] = dint("%sn_%s" % (wnm, h),
                                           [rows, cols], BF16)
    sc["x1_s"] = dint("x1_s", [TPC, DM], F32)
    for h in ("hi", "lo"):
        sc["xn2n_%s" % h] = dint("xn2n_%s" % h, [TPC, DM], BF16)
        sc["xn2T_%s" % h] = dint("xn2T_%s" % h, [128, NIC * TPC], BF16)
        sc["odqn_%s" % h] = dint("odqn_%s" % h, [TPC, DM], BF16)
        sc["odqT_%s" % h] = dint("odqT_%s" % h, [128, NIC * TPC], BF16)
        sc["gn_%s" % h] = dint("gn_%s" % h, [TPC, HID], BF16)
        sc["gT_%s" % h] = dint("gT_%s" % h, [128, NHC * TPC], BF16)
        sc["b1T_%s" % h] = dint("b1T_%s" % h, [RK, HID], BF16)
        sc["b3T_%s" % h] = dint("b3T_%s" % h, [RK, HID], BF16)
        sc["b2T_%s" % h] = dint("b2T_%s" % h, [RK, DM], BF16)

    dbg = {}
    for nm in DBG:
        src = sc[nm]
        dbg[nm] = nc.dram_tensor("dbg_" + nm, list(src.shape), src.dtype,
                                 kind="ExternalOutput").ap()

    with ExitStack() as ctx:
        tr = Tr(nc, ctx)
        dep = {k: Dep() for k in (
            "xqh", "xql", "c1", "g1", "qk", "vs", "c2", "g2", "wT",
            "w1n", "w3n", "w2n", "x1", "xn2n", "xn2T", "odqn", "odqT",
            "gn", "gT", "out", "b1T", "b3T", "b2T")}
        for nm in ("q", "k", "v", "o"):
            dep["wn_" + nm] = Dep()

        s_ident = nc_sb(nc, ctx, "s_ident", [128, 128])
        d_ident = Dep()
        tr.dma("qs", s_ident[:], io["identf"], writes=[d_ident])
        s_perm = nc_sb(nc, ctx, "s_perm", [128, 128])
        d_perm = Dep()
        tr.dma("qs", s_perm[:], io["permF"], writes=[d_perm])
        s_mask = nc_sb(nc, ctx, "s_mask", [128, 896])
        d_mask = Dep()
        tr.dma("qs", s_mask[:], io["maskS"], writes=[d_mask])

        with ExitStack() as cp0:
            qk0 = QuantKit(nc, cp0)
            phase0(nc, tr, io, sc, dep, qk0)
        tr.fence()
        tr.op("pe", lambda e: e.collective_compute(
            "AllGather", ALU.bypass, replica_groups=[list(range(NCORE))],
            ins=[sc["contrib1"]], outs=[sc["gath1"]]),
            reads=[dep["c1"]], writes=[dep["g1"]], sem="cc")
        phase_ffn_split(nc, tr, io, sc, dep)
        tr.fence()
        phase_qkv(nc, tr, io, sc, dep, s_perm, d_perm, s_ident, d_ident)
        tr.fence()
        phase_attn(nc, tr, io, sc, dep, s_ident, d_ident, s_mask, d_mask)
        tr.fence()
        tr.op("pe", lambda e: e.collective_compute(
            "AllGather", ALU.bypass, replica_groups=[list(range(NCORE))],
            ins=[sc["contrib2"]], outs=[sc["gath2"]]),
            reads=[dep["c2"]], writes=[dep["g2"]], sem="cc")
        tr.fence()
        with ExitStack() as ctl:
            qk2 = QuantKit(nc, ctl)
            phase_tail(nc, tr, io, sc, dep, qk2, s_ident, d_ident)

        for nm in DBG:
            tr.wait_all("pe")
            tr.dma("qs", dbg[nm], sc[nm])
        tr.wait_all("pe")
        tr.wait_all("sy")
        tr.wait_all("se")
    return nc


def phase0(nc, tr, io, sc, dep, qk):
    with ExitStack() as c0:
        s_xt = [nc_sb(nc, c0, "p0_xt%d" % i, [128, DM]) for i in range(2)]
        s_xn = nc_sb(nc, c0, "p0_xn", [128, DM])
        s_dq = nc_sb(nc, c0, "p0_dq", [128, DM])
        s_hi = nc_sb(nc, c0, "p0_hi", [128, DM], BF16)
        s_lo = nc_sb(nc, c0, "p0_lo", [128, DM], BF16)
        d_xt = [Dep(), Dep()]
        d_xn, d_dq, d_hi, d_lo = (Dep() for _ in range(4))
        for tb in range(TPC // 128):
            bi = tb % 2
            tr.dma("qs", s_xt[bi][:],
                   io["x_my"][tb * 128:(tb + 1) * 128, :],
                   writes=[d_xt[bi]])
            emit_ln(tr, qk, s_xt[bi], d_xt[bi], s_xn, d_xn)
            emit_quant(tr, qk, s_xn, d_xn, s_dq, d_dq)
            emit_split(tr, "ve", s_dq[:], d_dq, s_hi[:], d_hi, s_lo[:], d_lo)
            tr.dma("qs", sc["xq_hi_n"][tb * 128:(tb + 1) * 128, :], s_hi[:],
                   reads=[d_hi], writes=[dep["xqh"]])
            tr.dma("qs", sc["xq_lo_n"][tb * 128:(tb + 1) * 128, :], s_lo[:],
                   reads=[d_lo], writes=[dep["xql"]])
        for nm in ("q", "k", "v", "o"):
            dn = dep["wn_" + nm]
            for r in range(HPC * HD // 128):
                bi = r % 2
                tr.dma("qs", s_xt[bi][:],
                       io["w%s_my" % nm][r * 128:(r + 1) * 128, :],
                       writes=[d_xt[bi]])
                emit_quant(tr, qk, s_xt[bi], d_xt[bi], s_dq, d_dq)
                emit_split(tr, "ve", s_dq[:], d_dq, s_hi[:], d_hi,
                           s_lo[:], d_lo)
                tr.dma("qs", sc["wn_%s_hi" % nm][r * 128:(r + 1) * 128, :],
                       s_hi[:], reads=[d_hi], writes=[dn])
                tr.dma("qs", sc["wn_%s_lo" % nm][r * 128:(r + 1) * 128, :],
                       s_lo[:], reads=[d_lo], writes=[dn])

    tr.fence()
    with ExitStack() as c1:
        s_tx = [nc_sb(nc, c1, "p0_tx%d" % i, [128, TPC], BF16)
                for i in range(2)]
        d_tx = [Dep(), Dep()]
        pi = 0
        for half, coff in (("hi", 0), ("lo", TPC)):
            src = sc["xq_%s_n" % half]
            rdep = dep["xqh"] if half == "hi" else dep["xql"]
            for ic in range(NIC):
                i = pi % 2
                pi += 1
                tr.dma("qa", s_tx[i][:], src[:, ic * 128:(ic + 1) * 128],
                       reads=[rdep], writes=[d_tx[i]], transpose=True)
                tr.dma("qs",
                       sc["contrib1"][ic * 128:(ic + 1) * 128,
                                      coff:coff + TPC],
                       s_tx[i][:], reads=[d_tx[i]], writes=[dep["c1"]])
        s_tw = [nc_sb(nc, c1, "p0_tw%d" % i, [128, HD], BF16)
                for i in range(2)]
        d_tw = [Dep(), Dep()]
        for nm in ("q", "k", "v"):
            for half in ("hi", "lo"):
                for h in range(HPC):
                    for ic in range(NIC):
                        i = pi % 2
                        pi += 1
                        tr.dma("qa", s_tw[i][:],
                               sc["wn_%s_%s" % (nm, half)][
                                   h * HD:(h + 1) * HD,
                                   ic * 128:(ic + 1) * 128],
                               reads=[dep["wn_" + nm]], writes=[d_tw[i]],
                               transpose=True)
                        tr.dma("qs",
                               sc["wT_%s_%s" % (nm, half)][
                                   h, :, ic * 128:(ic + 1) * 128],
                               s_tw[i][:], reads=[d_tw[i]],
                               writes=[dep["wT"]])
        s_to = [nc_sb(nc, c1, "p0_to%d" % i, [128, HPC * HD], BF16)
                for i in range(2)]
        d_to = [Dep(), Dep()]
        for hx, half in enumerate(("hi", "lo")):
            coff = 2048 + hx * 512
            for ic in range(NIC):
                i = pi % 2
                pi += 1
                tr.dma("qa", s_to[i][:],
                       sc["wn_o_%s" % half][:, ic * 128:(ic + 1) * 128],
                       reads=[dep["wn_o"]], writes=[d_to[i]], transpose=True)
                tr.dma("qs",
                       sc["contrib1"][ic * 128:(ic + 1) * 128,
                                      coff:coff + 512],
                       s_to[i][:], reads=[d_to[i]], writes=[dep["c1"]])


def phase_ffn_split(nc, tr, io, sc, dep):
    with ExitStack() as cf:
        bufs = []
        for i in range(2):
            bufs.append((nc_sb(nc, cf, "fs_f%d" % i, [128, HID]),
                         nc_sb(nc, cf, "fs_h%d" % i, [128, HID], BF16),
                         nc_sb(nc, cf, "fs_l%d" % i, [128, HID], BF16),
                         Dep(), Dep(), Dep()))
        pi = 0
        for wnm, rows, cols in (("w1", HID, DM), ("w3", HID, DM),
                                ("w2", DM, HID)):
            for r in range(rows // 128):
                sf, sh, sl, df, dh, dl = bufs[pi % 2]
                pi += 1
                tr.dma("qa", sf[:, :cols],
                       io[wnm][r * 128:(r + 1) * 128, :], writes=[df])
                emit_split(tr, "pe", sf[:, :cols], df, sh[:, :cols], dh,
                           sl[:, :cols], dl)
                tr.dma("qs", sc["%sn_hi" % wnm][r * 128:(r + 1) * 128, :],
                       sh[:, :cols], reads=[dh], writes=[dep["%sn" % wnm]])
                tr.dma("qs", sc["%sn_lo" % wnm][r * 128:(r + 1) * 128, :],
                       sl[:, :cols], reads=[dl], writes=[dep["%sn" % wnm]])


def phase_qkv(nc, tr, io, sc, dep, s_perm, d_perm, s_ident, d_ident):
    WIN = 512
    NWIN = TOK // WIN
    NG = 2
    GCH = NIC // NG
    with ExitStack() as c1:
        s_w = {}
        d_w = {}
        for nm in ("q", "k", "v"):
            for half in ("hi", "lo"):
                for st in range(2):
                    s_w[nm, half, st] = nc_sb(
                        nc, c1, "q_w%s%s%d" % (nm, half, st),
                        [128, DM], BF16)
                    d_w[nm, half, st] = Dep()
        s_x = {half: nc_sb(nc, c1, "q_x" + half, [128, NIC * WIN], BF16)
               for half in ("hi", "lo")}
        d_x = {(half, g): Dep() for half in ("hi", "lo") for g in range(NG)}
        s_cos = nc_sb(nc, c1, "q_cos", [128, WIN])
        s_sin = nc_sb(nc, c1, "q_sin", [128, WIN])
        d_cos, d_sin = Dep(), Dep()
        ps_q = [nc_ps(nc, c1, "q_psq%d" % i, [128, WIN]) for i in range(2)]
        ps_k = [nc_ps(nc, c1, "q_psk%d" % i, [128, WIN]) for i in range(2)]
        ps_v = [nc_ps(nc, c1, "q_psv%d" % i, [128, WIN]) for i in range(2)]
        ps_sw = nc_ps(nc, c1, "q_pssw", [128, WIN])
        ps_t = nc_ps(nc, c1, "q_pst", [128, 128])
        d_psq = [Dep(), Dep()]
        d_psk = [Dep(), Dep()]
        d_psv = [Dep(), Dep()]
        d_pssw, d_pst = Dep(), Dep()
        s_f = [nc_sb(nc, c1, "q_f%d" % i, [128, WIN]) for i in range(2)]
        d_f = [Dep(), Dep()]
        s_a = nc_sb(nc, c1, "q_a", [128, WIN])
        d_a = Dep()
        s_b = nc_sb(nc, c1, "q_b", [128, WIN])
        d_b = Dep()
        s_bh = [nc_sb(nc, c1, "q_bh%d" % i, [128, WIN], BF16)
                for i in range(2)]
        s_bl = [nc_sb(nc, c1, "q_bl%d" % i, [128, WIN], BF16)
                for i in range(2)]
        d_bh = [Dep(), Dep()]
        d_bl = [Dep(), Dep()]
        s_ve = [nc_sb(nc, c1, "q_ve%d" % i, [128, WIN]) for i in range(2)]
        d_ve = [Dep(), Dep()]
        s_vt = [nc_sb(nc, c1, "q_vt%d" % i, [128, 128]) for i in range(2)]
        d_vt = [Dep(), Dep()]

        for h in range(HPC):
            st = h % 2
            for nm in ("q", "k", "v"):
                for half in ("hi", "lo"):
                    tr.dma("qs", s_w[nm, half, st][:],
                           sc["wT_%s_%s" % (nm, half)][h],
                           reads=[dep["wT"]], writes=[d_w[nm, half, st]])
            for w in range(NWIN):
                slot, coff = w // 2, (w % 2) * WIN
                seqpos = (w % (SEQ // WIN)) * WIN
                for g in range(NG):
                    for half, dd in (("hi", 0), ("lo", TPC)):
                        for sub in range(2):
                            ic0 = g * GCH + sub * (GCH // 2)
                            srcap = sc["gath1"][
                                slot * DM + ic0 * 128:
                                slot * DM + (ic0 + GCH // 2) * 128,
                                dd + coff:dd + coff + WIN].rearrange(
                                "(i p) j -> p i j", p=128)
                            dstap = s_x[half].ap()[
                                :, ic0 * WIN:(ic0 + GCH // 2) * WIN
                            ].rearrange("p (i j) -> p i j", j=WIN)
                            tr.dma("qs", dstap, srcap,
                                   reads=[dep["g1"]],
                                   writes=[d_x[half, g]])
                tr.dma("qs", s_cos[:], io["cosF"][:, seqpos:seqpos + WIN],
                       writes=[d_cos])
                tr.dma("qs", s_sin[:], io["sinF2"][:, seqpos:seqpos + WIN],
                       writes=[d_sin])
                wi = w % 2
                for nm, pst, dps in (("q", ps_q[wi], d_psq[wi]),
                                     ("k", ps_k[wi], d_psk[wi]),
                                     ("v", ps_v[wi], d_psv[wi])):
                    for g in range(NG):
                        facs = []
                        for icc in range(GCH):
                            ic = g * GCH + icc
                            facs += mm3f(pst[:],
                                s_w[nm, "hi", st].ap()[
                                    :, ic * 128:(ic + 1) * 128],
                                s_w[nm, "lo", st].ap()[
                                    :, ic * 128:(ic + 1) * 128],
                                s_x["hi"].ap()[:, ic * WIN:(ic + 1) * WIN],
                                s_x["lo"].ap()[:, ic * WIN:(ic + 1) * WIN],
                                ic == 0, ic == NIC - 1)
                        tr.op_multi("te", facs,
                                    reads=[d_w[nm, "hi", st],
                                           d_w[nm, "lo", st],
                                           d_x["hi", g], d_x["lo", g]],
                                    writes=[dps])
                for nm, pst, dps, pre in (
                        ("q", ps_q[wi], d_psq[wi], "qs"),
                        ("k", ps_k[wi], d_psk[wi], "ks")):
                    tr.op("ve", lambda e, _p=pst, _w=wi: e.tensor_copy(
                        s_f[_w][:], _p[:]), reads=[dps], writes=[d_f[wi]])
                    tr.op("te", lambda e, _w=wi: e.matmul(
                        ps_sw[:], s_perm[:], s_f[_w][:],
                        start=True, stop=True),
                        reads=[d_perm, d_f[wi]], writes=[d_pssw])
                    tr.op("ve", lambda e, _w=wi: e.tensor_tensor(
                        s_a[:], s_f[_w].ap()[:, :], s_cos.ap()[:, :],
                        ALU.mult),
                        reads=[d_f[wi], d_cos], writes=[d_a])
                    tr.op("ve", lambda e: e.tensor_tensor(
                        s_b[:], ps_sw.ap()[:, :], s_sin.ap()[:, :],
                        ALU.mult),
                        reads=[d_pssw, d_sin], writes=[d_b])
                    tr.op("ve", lambda e, _w=wi: e.tensor_tensor(
                        s_f[_w][:], s_a.ap()[:, :], s_b.ap()[:, :],
                        ALU.add),
                        reads=[d_a, d_b], writes=[d_f[wi]])
                    emit_split(tr, "ve", s_f[wi][:], d_f[wi],
                               s_bh[wi][:], d_bh[wi],
                               s_bl[wi][:], d_bl[wi])
                    tr.dma("qs", sc[pre + "_hi"][h * HD:(h + 1) * HD,
                                                 w * WIN:(w + 1) * WIN],
                           s_bh[wi][:], reads=[d_bh[wi]],
                           writes=[dep["qk"]])
                    tr.dma("qs", sc[pre + "_lo"][h * HD:(h + 1) * HD,
                                                 w * WIN:(w + 1) * WIN],
                           s_bl[wi][:], reads=[d_bl[wi]],
                           writes=[dep["qk"]])
                tr.op("ve", lambda e, _w=wi: e.tensor_copy(
                    s_ve[_w][:], ps_v[_w][:]),
                    reads=[d_psv[wi]], writes=[d_ve[wi]])
                for j in range(WIN // 128):
                    tr.op("te", lambda e, _j=j, _w=wi: e.transpose(
                        ps_t[:], s_ve[_w].ap()[:, _j * 128:(_j + 1) * 128],
                        s_ident[:]),
                        reads=[d_ve[wi], d_ident], writes=[d_pst])
                    tr.op("se", lambda e, _w=wi: e.copy(
                        s_vt[_w][:], ps_t[:]),
                        reads=[d_pst], writes=[d_vt[wi]])
                    tr.dma("qs",
                           sc["vs"][w * WIN + j * 128:
                                    w * WIN + (j + 1) * 128,
                                    h * HD:(h + 1) * HD],
                           s_vt[wi][:], reads=[d_vt[wi]],
                           writes=[dep["vs"]])


def phase_attn(nc, tr, io, sc, dep, s_ident, d_ident, s_mask, d_mask):
    KW = 512
    with ExitStack() as ca:
        s_k = {}
        d_k = {}
        for h in range(HPC):
            for half in ("hi", "lo"):
                s_k[h, half] = nc_sb(nc, ca, "a_k%d%s" % (h, half),
                                     [128, SEQ], BF16)
                d_k[h, half] = Dep()
        s_v = nc_sb(nc, ca, "a_v", [128, (SEQ // 128) * HPC * HD])
        d_v = Dep()
        s_q = [{half: nc_sb(nc, ca, "a_q%s%d" % (half, i),
                            [128, HPC * HD], BF16) for half in ("hi", "lo")}
               for i in range(2)]
        d_q = [{half: Dep() for half in ("hi", "lo")} for i in range(2)]
        s_S = [nc_sb(nc, ca, "a_S%d" % p, [128, SEQ]) for p in range(2)]
        d_S = [Dep(), Dep()]
        s_P = [nc_sb(nc, ca, "a_P%d" % p, [128, SEQ]) for p in range(2)]
        d_P = [Dep(), Dep()]
        s_m = [nc_sb(nc, ca, "a_m%d" % p, [128, 4]) for p in range(2)]
        d_m = [Dep(), Dep()]
        s_PT = [nc_sb(nc, ca, "a_PT%d" % p, [128, 256]) for p in range(2)]
        d_PT = [[Dep(), Dep()], [Dep(), Dep()]]
        s_stage = nc_sb(nc, ca, "a_stage", [128, HPC * HD])
        d_stage = Dep()
        ps_S = [nc_ps(nc, ca, "a_psS%d" % p, [128, KW]) for p in range(2)]
        d_psS = [Dep(), Dep()]
        ps_T = [nc_ps(nc, ca, "a_psT%d" % i, [128, 128]) for i in range(2)]
        d_psT = [Dep(), Dep()]
        ps_o = [nc_ps(nc, ca, "a_pso%d" % p, [128, HD]) for p in range(2)]
        d_pso = [Dep(), Dep()]

        for s in range(BATCH):
            for h in range(HPC):
                for half in ("hi", "lo"):
                    tr.dma("qs", s_k[h, half][:],
                           sc["ks_%s" % half][h * HD:(h + 1) * HD,
                                              s * SEQ:(s + 1) * SEQ],
                           reads=[dep["qk"]], writes=[d_k[h, half]])
            for j in range(SEQ // 128):
                tr.dma("qs", s_v[:, j * HPC * HD:(j + 1) * HPC * HD],
                       sc["vs"][s * SEQ + j * 128:s * SEQ + (j + 1) * 128, :],
                       reads=[dep["vs"]], writes=[d_v])
            for qb in range(SEQ // 128):
                nk = qb // 4 + 1
                nkc = nk * 4
                qi = qb % 2
                c0 = s * SEQ + qb * 128
                for half in ("hi", "lo"):
                    tr.dma("qs",
                           s_q[qi][half].ap().rearrange(
                               "p (h j) -> p h j", h=HPC),
                           sc["qs_%s" % half][:, c0:c0 + 128].rearrange(
                               "(h p) j -> p h j", p=HD),
                           reads=[dep["qk"]], writes=[d_q[qi][half]])
                for h in range(HPC):
                    p = h % 2
                    for kw in range(nk):
                        tr.op_multi("te", mm3f(ps_S[p][:],
                            s_q[qi]["hi"].ap()[:, h * HD:(h + 1) * HD],
                            s_q[qi]["lo"].ap()[:, h * HD:(h + 1) * HD],
                            s_k[h, "hi"].ap()[:, kw * KW:(kw + 1) * KW],
                            s_k[h, "lo"].ap()[:, kw * KW:(kw + 1) * KW],
                            True, True),
                            reads=[d_q[qi]["hi"], d_q[qi]["lo"],
                                   d_k[h, "hi"], d_k[h, "lo"]],
                            writes=[d_psS[p]])
                        if kw == nk - 1:
                            moff = 384 - (qb % 4) * 128
                            tr.op("ve", lambda e, _k=kw, _m=moff, _p=p:
                                  e.tensor_tensor(
                                      s_S[_p][:, _k * KW:(_k + 1) * KW],
                                      ps_S[_p].ap()[:, :],
                                      s_mask.ap()[:, _m:_m + KW], ALU.add),
                                  reads=[d_psS[p], d_mask], writes=[d_S[p]])
                        elif kw % 2 == 0:
                            tr.op("se", lambda e, _k=kw, _p=p: e.copy(
                                s_S[_p][:, _k * KW:(_k + 1) * KW],
                                ps_S[_p][:]),
                                reads=[d_psS[p]], writes=[d_S[p]])
                        else:
                            tr.op("ve", lambda e, _k=kw, _p=p:
                                  e.tensor_copy(
                                      s_S[_p][:, _k * KW:(_k + 1) * KW],
                                      ps_S[_p][:]),
                                  reads=[d_psS[p]], writes=[d_S[p]])
                    tr.op("ve", lambda e, _n=nk, _p=p: e.reduce_max(
                        s_m[_p][:, 0:1], s_S[_p].ap()[:, :_n * KW], AX),
                        reads=[d_S[p]], writes=[d_m[p]])
                    tr.op("ve", lambda e, _p=p: e.tensor_single_scalar(
                        s_m[_p][:, 1:2], s_m[_p][:, 0:1], -SCALE, ALU.mult),
                        reads=[d_m[p]], writes=[d_m[p]])
                    tr.op("se", lambda e, _n=nk, _p=p: e.activation(
                        s_P[_p][:, :_n * KW], s_S[_p][:, :_n * KW], AF.Exp,
                        bias=s_m[_p].ap()[:, 1:2], scale=SCALE,
                        accum_out=s_m[_p][:, 2:3]),
                        reads=[d_S[p], d_m[p]], writes=[d_P[p], d_m[p]])
                    tr.op("ve", lambda e, _p=p: e.reciprocal(
                        s_m[_p][:, 3:4], s_m[_p][:, 2:3]),
                        reads=[d_m[p]], writes=[d_m[p]])
                    for kc in range(nkc):
                        ti = kc % 2
                        tr.op("te", lambda e, _k=kc, _t=ti, _p=p:
                              e.transpose(
                                  ps_T[_t][:],
                                  s_P[_p].ap()[:, _k * 128:(_k + 1) * 128],
                                  s_ident[:]),
                              reads=[d_P[p], d_ident], writes=[d_psT[ti]])
                        if ti == 0:
                            tr.op("ve", lambda e, _t=ti, _p=p:
                                  e.tensor_copy(
                                      s_PT[_p][:, _t * 128:(_t + 1) * 128],
                                      ps_T[_t][:]),
                                  reads=[d_psT[ti]], writes=[d_PT[p][ti]])
                        else:
                            tr.op("se", lambda e, _t=ti, _p=p: e.copy(
                                s_PT[_p][:, _t * 128:(_t + 1) * 128],
                                ps_T[_t][:]),
                                reads=[d_psT[ti]], writes=[d_PT[p][ti]])
                        tr.op("te", lambda e, _k=kc, _t=ti, _p=p, _h=h:
                              e.matmul(
                                  ps_o[_p][:],
                                  s_PT[_p].ap()[:, _t * 128:(_t + 1) * 128],
                                  s_v.ap()[:, _k * HPC * HD + _h * HD:
                                           _k * HPC * HD + (_h + 1) * HD],
                                  start=(_k == 0), stop=(_k == nkc - 1)),
                              reads=[d_PT[p][ti], d_v], writes=[d_pso[p]])
                    tr.op("se", lambda e, _h=h, _p=p: e.activation(
                        s_stage[:, _h * HD:(_h + 1) * HD], ps_o[_p][:],
                        AF.Copy, scale=s_m[_p].ap()[:, 3:4]),
                        reads=[d_pso[p], d_m[p]], writes=[d_stage])
                tr.dma("qs",
                       sc["contrib2"][c0:c0 + 128, :],
                       s_stage[:], reads=[d_stage], writes=[dep["c2"]])


def phase_lora_prep(nc, tr, ctx, io, sc, dep, s_ident, d_ident):
    lora = {}
    s_tmp = nc_sb(nc, ctx, "lp_tmp", [128, 128])
    d_tmp = Dep()
    t_ps = nc_ps(nc, ctx, "lp_ps", [128, 128])
    d_ps = Dep()
    s_asrc = nc_sb(nc, ctx, "lp_asrc", [RK, 128])
    d_asrc = Dep()
    for nm, dim in (("a1", DM), ("a3", DM), ("a2", HID)):
        nch = dim // 128
        hi = nc_sb(nc, ctx, "lp_%s_hi" % nm, [128, nch * RK], BF16)
        lo = nc_sb(nc, ctx, "lp_%s_lo" % nm, [128, nch * RK], BF16)
        dhi, dlo = Dep(), Dep()
        for c in range(nch):
            tr.dma("qs", s_asrc[:], io[nm][:, c * 128:(c + 1) * 128],
                   writes=[d_asrc])
            tr.op("te", lambda e: e.transpose(
                t_ps[:, :RK], s_asrc.ap()[:, :],
                s_ident.ap()[0:RK, 0:RK]),
                reads=[d_asrc, d_ident], writes=[d_ps])
            tr.op("ve", lambda e: e.tensor_copy(s_tmp[:, :RK], t_ps[:, :RK]),
                  reads=[d_ps], writes=[d_tmp])
            emit_split(tr, "ve", s_tmp[:, :RK], d_tmp,
                       hi[:, c * RK:(c + 1) * RK], dhi,
                       lo[:, c * RK:(c + 1) * RK], dlo)
        lora[nm] = (hi, lo, dhi, dlo)
    t_ps16 = nc_ps(nc, ctx, "lp_ps16", [RK, 128])
    d_ps16 = Dep()
    s_t2 = nc_sb(nc, ctx, "lp_t2", [RK, 128])
    s_t2h = nc_sb(nc, ctx, "lp_t2h", [RK, 128], BF16)
    s_t2l = nc_sb(nc, ctx, "lp_t2l", [RK, 128], BF16)
    d_t2, d_t2h, d_t2l = Dep(), Dep(), Dep()
    s_s2 = nc_sb(nc, ctx, "lp_s2", [128, RK])
    d_s2 = Dep()
    for nm, rows in (("b1", HID), ("b3", HID), ("b2", DM)):
        for c in range(rows // 128):
            tr.dma("qs", s_s2[:], io[nm][c * 128:(c + 1) * 128, :],
                   writes=[d_s2])
            tr.op("te", lambda e: e.transpose(
                t_ps16[:, :], s_s2.ap()[:, :], s_ident[:]),
                reads=[d_s2, d_ident], writes=[d_ps16])
            tr.op("ve", lambda e: e.tensor_copy(s_t2[:, :], t_ps16[:, :]),
                  reads=[d_ps16], writes=[d_t2])
            emit_split(tr, "ve", s_t2[:], d_t2, s_t2h[:], d_t2h,
                       s_t2l[:], d_t2l)
            tr.dma("qs", sc["%sT_hi" % nm][:, c * 128:(c + 1) * 128],
                   s_t2h[:], reads=[d_t2h], writes=[dep["%sT" % nm]])
            tr.dma("qs", sc["%sT_lo" % nm][:, c * 128:(c + 1) * 128],
                   s_t2l[:], reads=[d_t2l], writes=[dep["%sT" % nm]])
    return lora


def phase_tail(nc, tr, io, sc, dep, qk, s_ident, d_ident):
    pid = nc.partition_id()
    base = nc.sync.snap(pid * TPC)
    with ExitStack() as c0:
        s_o = [nc_sb(nc, c0, "t_o%d" % i, [128, DM]) for i in range(2)]
        s_dq = nc_sb(nc, c0, "t_dq", [128, DM])
        s_hi = nc_sb(nc, c0, "t_hi", [128, DM], BF16)
        s_lo = nc_sb(nc, c0, "t_lo", [128, DM], BF16)
        d_o = [Dep(), Dep()]
        d_dq, d_hi, d_lo = (Dep() for _ in range(3))
        for tb in range(TPC // 128):
            bi = tb % 2
            off = nc.sync.snap(base + tb * 128, donate=True)
            g2v = sc["gath2"].rearrange("(j t) f -> j t f", j=NCORE)
            tr.dma("qs",
                   s_o[bi].ap().rearrange("p (j f) -> p j f", j=NCORE),
                   g2v[:, bass.ds(off, 128), :].transpose([1, 0, 2]),
                   reads=[dep["g2"]], writes=[d_o[bi]])
            emit_quant(tr, qk, s_o[bi], d_o[bi], s_dq, d_dq)
            emit_split(tr, "ve", s_dq[:], d_dq, s_hi[:], d_hi, s_lo[:], d_lo)
            tr.dma("qs", sc["odqn_hi"][tb * 128:(tb + 1) * 128, :], s_hi[:],
                   reads=[d_hi], writes=[dep["odqn"]])
            tr.dma("qs", sc["odqn_lo"][tb * 128:(tb + 1) * 128, :], s_lo[:],
                   reads=[d_lo], writes=[dep["odqn"]])
        tr.fence()
        s_tx = nc_sb(nc, c0, "t_tx", [128, TPC], BF16)
        d_tx = Dep()
        for half in ("hi", "lo"):
            for ic in range(NIC):
                tr.dma("qa", s_tx[:],
                       sc["odqn_%s" % half][:, ic * 128:(ic + 1) * 128],
                       reads=[dep["odqn"]], writes=[d_tx], transpose=True)
                for tb in range(TPC // 128):
                    tr.dma("qs",
                           sc["odqT_%s" % half][
                               :, tb * (NIC * 128) + ic * 128:
                               tb * (NIC * 128) + (ic + 1) * 128],
                           s_tx[:, tb * 128:(tb + 1) * 128],
                           reads=[d_tx], writes=[dep["odqT"]])

    with ExitStack() as c1:
        s_oT = {half: nc_sb(nc, c1, "t_oT" + half, [128, 2 * NIC * 128],
                            BF16) for half in ("hi", "lo")}
        d_oT = {half: Dep() for half in ("hi", "lo")}
        s_wo = {half: nc_sb(nc, c1, "t_wo" + half, [128, (NIC // 2) * 512],
                            BF16) for half in ("hi", "lo")}
        d_wo = {half: Dep() for half in ("hi", "lo")}
        s_xr = nc_sb(nc, c1, "t_xr", [128, 512])
        d_xr = Dep()
        s_x1 = [nc_sb(nc, c1, "t_x1_%d" % i, [128, DM]) for i in range(2)]
        d_x1 = [Dep(), Dep()]
        s_xn2 = nc_sb(nc, c1, "t_xn2", [128, DM])
        d_xn2 = Dep()
        s_nh = nc_sb(nc, c1, "t_nh", [128, DM], BF16)
        s_nl = nc_sb(nc, c1, "t_nl", [128, DM], BF16)
        d_nh, d_nl = Dep(), Dep()
        ps_a = [nc_ps(nc, c1, "t_psa%d" % i, [128, 512]) for i in range(2)]
        d_psa = [Dep(), Dep()]
        for tp in range(TPC // 256):
            for half in ("hi", "lo"):
                for tbl in range(2):
                    tb = tp * 2 + tbl
                    tr.dma("qs",
                           s_oT[half][:, tbl * NIC * 128:
                                      (tbl + 1) * NIC * 128],
                           sc["odqT_%s" % half][:, tb * NIC * 128:
                                                (tb + 1) * NIC * 128],
                           reads=[dep["odqT"]], writes=[d_oT[half]])
            for ow in range(NCORE):
                for ih in range(2):
                    for half in ("hi", "lo"):
                        for icc in range(NIC // 2):
                            ic = ih * (NIC // 2) + icc
                            c0_ = 2048 if half == "hi" else 2560
                            tr.dma("qs",
                                   s_wo[half][:, icc * 512:(icc + 1) * 512],
                                   sc["gath1"][ow * DM + ic * 128:
                                               ow * DM + (ic + 1) * 128,
                                               c0_:c0_ + 512],
                                   reads=[dep["g1"]], writes=[d_wo[half]])
                    for tbl in range(2):
                        facs = []
                        for icc in range(NIC // 2):
                            ic = ih * (NIC // 2) + icc
                            facs += mm3f(ps_a[tbl][:],
                                s_oT["hi"].ap()[
                                    :, (tbl * NIC + ic) * 128:
                                    (tbl * NIC + ic + 1) * 128],
                                s_oT["lo"].ap()[
                                    :, (tbl * NIC + ic) * 128:
                                    (tbl * NIC + ic + 1) * 128],
                                s_wo["hi"].ap()[:, icc * 512:
                                                (icc + 1) * 512],
                                s_wo["lo"].ap()[:, icc * 512:
                                                (icc + 1) * 512],
                                ih == 0 and icc == 0,
                                ih == 1 and icc == NIC // 2 - 1)
                        tr.op_multi("te", facs,
                                    reads=[d_oT["hi"], d_oT["lo"],
                                           d_wo["hi"], d_wo["lo"]],
                                    writes=[d_psa[tbl]])
                        if ih == 1:
                            tb = tp * 2 + tbl
                            tr.dma("qs", s_xr[:],
                                   io["x_my"][tb * 128:(tb + 1) * 128,
                                              ow * 512:(ow + 1) * 512],
                                   writes=[d_xr])
                            tr.op("ve", lambda e, _t=tbl, _o=ow:
                                  e.tensor_tensor(
                                      s_x1[_t][:, _o * 512:(_o + 1) * 512],
                                      ps_a[_t].ap()[:, :], s_xr.ap()[:, :],
                                      ALU.add),
                                  reads=[d_psa[tbl], d_xr],
                                  writes=[d_x1[tbl]])
            for tbl in range(2):
                tb = tp * 2 + tbl
                tr.dma("qs", sc["x1_s"][tb * 128:(tb + 1) * 128, :],
                       s_x1[tbl][:], reads=[d_x1[tbl]], writes=[dep["x1"]])
                emit_ln(tr, qk, s_x1[tbl], d_x1[tbl], s_xn2, d_xn2)
                emit_split(tr, "ve", s_xn2[:], d_xn2, s_nh[:], d_nh,
                           s_nl[:], d_nl)
                tr.dma("qs", sc["xn2n_hi"][tb * 128:(tb + 1) * 128, :],
                       s_nh[:], reads=[d_nh], writes=[dep["xn2n"]])
                tr.dma("qs", sc["xn2n_lo"][tb * 128:(tb + 1) * 128, :],
                       s_nl[:], reads=[d_nl], writes=[dep["xn2n"]])
        tr.fence()
        s_tx2 = nc_sb(nc, c1, "t_tx2", [128, TPC], BF16)
        d_tx2 = Dep()
        for half in ("hi", "lo"):
            for ic in range(NIC):
                tr.dma("qa", s_tx2[:],
                       sc["xn2n_%s" % half][:, ic * 128:(ic + 1) * 128],
                       reads=[dep["xn2n"]], writes=[d_tx2], transpose=True)
                for tb in range(TPC // 128):
                    tr.dma("qs",
                           sc["xn2T_%s" % half][
                               :, tb * (NIC * 128) + ic * 128:
                               tb * (NIC * 128) + (ic + 1) * 128],
                           s_tx2[:, tb * 128:(tb + 1) * 128],
                           reads=[d_tx2], writes=[dep["xn2T"]])

    with ExitStack() as cl:
        lora = phase_lora_prep(nc, tr, cl, io, sc, dep, s_ident, d_ident)
        phase_ffn_a(nc, tr, cl, io, sc, dep, lora, s_ident, d_ident)
    with ExitStack() as cw:
        phase_ffn_b(nc, tr, cw, io, sc, dep, s_ident, d_ident)


def phase_ffn_a(nc, tr, cl, io, sc, dep, lora, s_ident, d_ident):
    NHW = (HID + 511) // 512
    s_xT = {half: nc_sb(nc, cl, "f_xT" + half, [128, NIC * 128], BF16)
            for half in ("hi", "lo")}
    d_xT = {half: Dep() for half in ("hi", "lo")}
    a1h, a1l, da1h, da1l = lora["a1"]
    a3h, a3l, da3h, da3l = lora["a3"]
    ps_xa = nc_ps(nc, cl, "f_psxa", [128, 2 * RK])
    d_psxa = Dep()
    s_xa = nc_sb(nc, cl, "f_xa", [128, 2 * RK])
    d_xa = Dep()
    ps_xat = nc_ps(nc, cl, "f_psxat", [RK, 128])
    d_psxat = Dep()
    s_xat = {i: nc_sb(nc, cl, "f_xat%d" % i, [RK, TPC]) for i in (0, 1)}
    d_xat = {i: Dep() for i in (0, 1)}
    s_xath = {i: nc_sb(nc, cl, "f_xath%d" % i, [RK, TPC], BF16)
              for i in (0, 1)}
    s_xatl = {i: nc_sb(nc, cl, "f_xatl%d" % i, [RK, TPC], BF16)
              for i in (0, 1)}
    d_xath = {i: Dep() for i in (0, 1)}
    d_xatl = {i: Dep() for i in (0, 1)}
    for tb in range(TPC // 128):
        for half in ("hi", "lo"):
            tr.dma("qs", s_xT[half][:],
                   sc["xn2T_%s" % half][:, tb * NIC * 128:
                                        (tb + 1) * NIC * 128],
                   reads=[dep["xn2T"]], writes=[d_xT[half]])
        facs = []
        for ic in range(NIC):
            for ri, (rh, rl, cr) in enumerate(
                    ((a1h, a1l, 0), (a3h, a3l, RK))):
                combos = ((s_xT["hi"], rh), (s_xT["hi"], rl),
                          (s_xT["lo"], rh))
                for li, (lw, rx) in enumerate(combos):
                    first = (ic == 0 and ri == 0 and li == 0)
                    last = (ic == NIC - 1 and ri == 1 and li == 2)
                    facs.append(lambda e, _lw=lw, _rx=rx, _ic=ic, _cr=cr,
                                _f=first, _e=last:
                                e.matmul(ps_xa[:, _cr:_cr + RK],
                                         _lw.ap()[:, _ic * 128:
                                                  (_ic + 1) * 128],
                                         _rx.ap()[:, _ic * RK:
                                                  (_ic + 1) * RK],
                                         start=_f, stop=_e))
        tr.op_multi("te", facs,
                    reads=[d_xT["hi"], d_xT["lo"], da1h, da1l, da3h, da3l],
                    writes=[d_psxa])
        tr.op("se", lambda e: e.activation(
            s_xa[:], ps_xa[:], AF.Copy, scale=2.0),
            reads=[d_psxa], writes=[d_xa])
        for i in (0, 1):
            tr.op("te", lambda e, _i=i: e.transpose(
                ps_xat[:, :], s_xa.ap()[:, _i * RK:(_i + 1) * RK],
                s_ident[:]),
                reads=[d_xa, d_ident], writes=[d_psxat])
            tr.op("ve", lambda e, _t=tb, _i=i: e.tensor_copy(
                s_xat[_i][:, _t * 128:(_t + 1) * 128], ps_xat[:, :]),
                reads=[d_psxat], writes=[d_xat[i]])
    for i in (0, 1):
        emit_split(tr, "ve", s_xat[i][:], d_xat[i], s_xath[i][:],
                   d_xath[i], s_xatl[i][:], d_xatl[i])

    s_wf = {half: nc_sb(nc, cl, "f_wf" + half, [128, NIC * 512], BF16)
            for half in ("hi", "lo")}
    d_wf = {(half, g): Dep() for half in ("hi", "lo") for g in range(2)}
    s_bT = {half: nc_sb(nc, cl, "f_bT" + half, [RK, 512], BF16)
            for half in ("hi", "lo")}
    d_bT = {half: Dep() for half in ("hi", "lo")}
    s_xT2 = [{half: nc_sb(nc, cl, "f_xT2%s%d" % (half, i),
                          [128, NIC * 128], BF16) for half in ("hi", "lo")}
             for i in range(2)]
    d_xT2 = [{half: Dep() for half in ("hi", "lo")} for i in range(2)]
    s_h1 = nc_sb(nc, cl, "f_h1", [128, 8 * 512])
    d_h1 = Dep()
    s_g = nc_sb(nc, cl, "f_g", [128, 512])
    d_g = Dep()
    s_gh = nc_sb(nc, cl, "f_gh", [128, 512], BF16)
    s_gl = nc_sb(nc, cl, "f_gl", [128, 512], BF16)
    d_gh, d_gl = Dep(), Dep()
    ps_h = nc_ps(nc, cl, "f_psh", [128, 512])
    d_psh = Dep()
    GC2 = NIC // 2
    for hw in range(NHW):
        W = min(512, HID - 512 * hw)
        for sub, wsrc, bnm in ((0, "w1n", "b1T"), (1, "w3n", "b3T")):
            for g in range(2):
                for half in ("hi", "lo"):
                    for icc in range(GC2):
                        ic = g * GC2 + icc
                        tr.dma("qa", s_wf[half][:, ic * 512:ic * 512 + W],
                               sc["%s_%s" % (wsrc, half)][
                                   hw * 512:hw * 512 + W,
                                   ic * 128:(ic + 1) * 128],
                               reads=[dep[wsrc]], writes=[d_wf[half, g]],
                               transpose=True)
            for half in ("hi", "lo"):
                tr.dma("qs", s_bT[half][:, :W],
                       sc["%s_%s" % (bnm, half)][:, hw * 512:hw * 512 + W],
                       reads=[dep[bnm]], writes=[d_bT[half]])
            for tb in range(TPC // 128):
                bi = tb % 2
                for half in ("hi", "lo"):
                    tr.dma("qs", s_xT2[bi][half][:],
                           sc["xn2T_%s" % half][:, tb * NIC * 128:
                                                (tb + 1) * NIC * 128],
                           reads=[dep["xn2T"]],
                           writes=[d_xT2[bi][half]])
                for g in range(2):
                    facs = []
                    for icc in range(GC2):
                        ic = g * GC2 + icc
                        facs += mm3f(ps_h[:, :W],
                            s_xT2[bi]["hi"].ap()[:, ic * 128:(ic + 1) * 128],
                            s_xT2[bi]["lo"].ap()[:, ic * 128:(ic + 1) * 128],
                            s_wf["hi"].ap()[:, ic * 512:ic * 512 + W],
                            s_wf["lo"].ap()[:, ic * 512:ic * 512 + W],
                            ic == 0, False)
                    reads = [d_xT2[bi]["hi"], d_xT2[bi]["lo"],
                             d_wf["hi", g], d_wf["lo", g]]
                    if g == 1:
                        combos = ((s_xath[sub], s_bT["hi"]),
                                  (s_xath[sub], s_bT["lo"]),
                                  (s_xatl[sub], s_bT["hi"]))
                        for li, (lw, rx) in enumerate(combos):
                            facs.append(lambda e, _lw=lw, _rx=rx, _tb=tb,
                                        _W=W, _e=(li == 2):
                                        e.matmul(ps_h[:, :_W],
                                                 _lw.ap()[:, _tb * 128:
                                                          (_tb + 1) * 128],
                                                 _rx.ap()[:, :_W],
                                                 start=False, stop=_e))
                        reads += [d_xath[sub], d_xatl[sub],
                                  d_bT["hi"], d_bT["lo"]]
                    tr.op_multi("te", facs, reads=reads, writes=[d_psh])
                if sub == 0:
                    tr.op("se", lambda e, _t=tb, _W=W: e.activation(
                        s_h1[:, _t * 512:_t * 512 + _W], ps_h[:, :_W],
                        AF.Silu), reads=[d_psh], writes=[d_h1])
                else:
                    tr.op("ve", lambda e, _t=tb, _W=W: e.tensor_tensor(
                        s_g[:, :_W], s_h1.ap()[:, _t * 512:_t * 512 + _W],
                        ps_h.ap()[:, :_W], ALU.mult),
                        reads=[d_h1, d_psh], writes=[d_g])
                    emit_split(tr, "ve", s_g[:, :W], d_g, s_gh[:, :W],
                               d_gh, s_gl[:, :W], d_gl)
                    tr.dma("qs", sc["gn_hi"][tb * 128:(tb + 1) * 128,
                                             hw * 512:hw * 512 + W],
                           s_gh[:, :W], reads=[d_gh], writes=[dep["gn"]])
                    tr.dma("qs", sc["gn_lo"][tb * 128:(tb + 1) * 128,
                                             hw * 512:hw * 512 + W],
                           s_gl[:, :W], reads=[d_gl], writes=[dep["gn"]])


def phase_ffn_b(nc, tr, cw, io, sc, dep, s_ident, d_ident):
    tr.fence()
    s_tg = nc_sb(nc, cw, "f_tg", [128, TPC], BF16)
    d_tg = Dep()
    for half in ("hi", "lo"):
        for hc in range(NHC):
            tr.dma("qa", s_tg[:],
                   sc["gn_%s" % half][:, hc * 128:(hc + 1) * 128],
                   reads=[dep["gn"]], writes=[d_tg], transpose=True)
            for tb in range(TPC // 128):
                tr.dma("qs",
                       sc["gT_%s" % half][:, tb * (NHC * 128) + hc * 128:
                                          tb * (NHC * 128) + (hc + 1) * 128],
                       s_tg[:, tb * 128:(tb + 1) * 128],
                       reads=[d_tg], writes=[dep["gT"]])

    s_tmp = nc_sb(nc, cw, "fb_tmp", [128, 128])
    d_tmp = Dep()
    t_ps = nc_ps(nc, cw, "fb_ps", [128, 128])
    d_ps = Dep()
    s_a2src = nc_sb(nc, cw, "fb_a2src", [RK, 128])
    d_a2src = Dep()
    a2h = nc_sb(nc, cw, "fb_a2h", [128, NHC * RK], BF16)
    a2l = nc_sb(nc, cw, "fb_a2l", [128, NHC * RK], BF16)
    da2h, da2l = Dep(), Dep()
    for c in range(NHC):
        tr.dma("qs", s_a2src[:], io["a2"][:, c * 128:(c + 1) * 128],
               writes=[d_a2src])
        tr.op("te", lambda e: e.transpose(
            t_ps[:, :RK], s_a2src.ap()[:, :],
            s_ident.ap()[0:RK, 0:RK]),
            reads=[d_a2src, d_ident], writes=[d_ps])
        tr.op("ve", lambda e: e.tensor_copy(s_tmp[:, :RK], t_ps[:, :RK]),
              reads=[d_ps], writes=[d_tmp])
        emit_split(tr, "ve", s_tmp[:, :RK], d_tmp,
                   a2h[:, c * RK:(c + 1) * RK], da2h,
                   a2l[:, c * RK:(c + 1) * RK], da2l)

    s_gT = {half: nc_sb(nc, cw, "fb_gT" + half, [128, NHC * 128], BF16)
            for half in ("hi", "lo")}
    d_gT = {(half, g): Dep() for half in ("hi", "lo") for g in range(2)}
    ps_ga = nc_ps(nc, cw, "fb_psga", [128, RK])
    d_psga = Dep()
    s_ga = nc_sb(nc, cw, "fb_ga", [128, RK])
    d_ga = Dep()
    ps_gat = nc_ps(nc, cw, "fb_psgat", [RK, 128])
    d_psgat = Dep()
    s_gat = nc_sb(nc, cw, "fb_gat", [RK, TPC])
    d_gat = Dep()
    s_gath = nc_sb(nc, cw, "fb_gath", [RK, TPC], BF16)
    s_gatl = nc_sb(nc, cw, "fb_gatl", [RK, TPC], BF16)
    d_gath, d_gatl = Dep(), Dep()
    for tb in range(TPC // 128):
        for half in ("hi", "lo"):
            tr.dma("qs", s_gT[half][:],
                   sc["gT_%s" % half][:, tb * NHC * 128:
                                      (tb + 1) * NHC * 128],
                   reads=[dep["gT"]],
                   writes=[d_gT[half, 0], d_gT[half, 1]])
        facs = []
        for hc in range(NHC):
            combos = ((s_gT["hi"], a2h), (s_gT["hi"], a2l),
                      (s_gT["lo"], a2h))
            for li, (lw, rx) in enumerate(combos):
                facs.append(lambda e, _lw=lw, _rx=rx, _hc=hc,
                            _f=(hc == 0 and li == 0),
                            _e=(hc == NHC - 1 and li == 2):
                            e.matmul(ps_ga[:],
                                     _lw.ap()[:, _hc * 128:(_hc + 1) * 128],
                                     _rx.ap()[:, _hc * RK:(_hc + 1) * RK],
                                     start=_f, stop=_e))
        tr.op_multi("te", facs,
                    reads=[d_gT["hi", 0], d_gT["hi", 1],
                           d_gT["lo", 0], d_gT["lo", 1], da2h, da2l],
                    writes=[d_psga])
        tr.op("se", lambda e: e.activation(
            s_ga[:], ps_ga[:], AF.Copy, scale=2.0),
            reads=[d_psga], writes=[d_ga])
        tr.op("te", lambda e: e.transpose(
            ps_gat[:, :], s_ga.ap()[:, :], s_ident[:]),
            reads=[d_ga, d_ident], writes=[d_psgat])
        tr.op("ve", lambda e, _t=tb: e.tensor_copy(
            s_gat[:, _t * 128:(_t + 1) * 128], ps_gat[:, :]),
            reads=[d_psgat], writes=[d_gat])
    emit_split(tr, "ve", s_gat[:], d_gat, s_gath[:], d_gath,
               s_gatl[:], d_gatl)

    OW = 256
    NOW = DM // OW
    s_w2 = {half: nc_sb(nc, cw, "fb_w2" + half, [128, NHC * OW],
                        BF16) for half in ("hi", "lo")}
    d_w2 = {(half, g): Dep() for half in ("hi", "lo") for g in range(2)}
    s_b2 = {half: nc_sb(nc, cw, "fb_b2" + half, [RK, OW], BF16)
            for half in ("hi", "lo")}
    d_b2 = {half: Dep() for half in ("hi", "lo")}
    s_x1t = nc_sb(nc, cw, "fb_x1t", [128, OW])
    d_x1t = Dep()
    s_out = nc_sb(nc, cw, "fb_out", [128, OW])
    d_out = Dep()
    ps_o2 = nc_ps(nc, cw, "fb_pso2", [128, OW])
    d_pso2 = Dep()
    for ow in range(NOW):
        for half in ("hi", "lo"):
            tr.dma("qs", s_b2[half][:, :],
                   sc["b2T_%s" % half][:, ow * OW:(ow + 1) * OW],
                   reads=[dep["b2T"]], writes=[d_b2[half]])
            for g in range(2):
                for hcx in range(43):
                    hc = g * 43 + hcx
                    if hc >= NHC:
                        continue
                    tr.dma("qa", s_w2[half][:, hc * OW:(hc + 1) * OW],
                           sc["w2n_%s" % half][ow * OW:(ow + 1) * OW,
                                               hc * 128:(hc + 1) * 128],
                           reads=[dep["w2n"]], writes=[d_w2[half, g]],
                           transpose=True)
        for tb in range(TPC // 128):
            for g in range(2):
                h0, h1_ = g * 43, min((g + 1) * 43, NHC)
                for half in ("hi", "lo"):
                    tr.dma("qs",
                           s_gT[half][:, h0 * 128:h1_ * 128],
                           sc["gT_%s" % half][:, tb * NHC * 128 + h0 * 128:
                                              tb * NHC * 128 + h1_ * 128],
                           reads=[dep["gT"]], writes=[d_gT[half, g]])
            for g in range(2):
                h0, h1_ = g * 43, min((g + 1) * 43, NHC)
                facs = []
                for hc in range(h0, h1_):
                    facs += mm3f(ps_o2[:],
                        s_gT["hi"].ap()[:, hc * 128:(hc + 1) * 128],
                        s_gT["lo"].ap()[:, hc * 128:(hc + 1) * 128],
                        s_w2["hi"].ap()[:, hc * OW:(hc + 1) * OW],
                        s_w2["lo"].ap()[:, hc * OW:(hc + 1) * OW],
                        hc == 0, False)
                reads = [d_gT["hi", g], d_gT["lo", g], d_w2["hi", g],
                         d_w2["lo", g]]
                if g == 1:
                    combos = ((s_gath, s_b2["hi"]), (s_gath, s_b2["lo"]),
                              (s_gatl, s_b2["hi"]))
                    for li, (lw, rx) in enumerate(combos):
                        facs.append(lambda e, _lw=lw, _rx=rx, _tb=tb,
                                    _e=(li == 2):
                                    e.matmul(ps_o2[:],
                                             _lw.ap()[:, _tb * 128:
                                                      (_tb + 1) * 128],
                                             _rx.ap()[:, :],
                                             start=False, stop=_e))
                    reads += [d_gath, d_gatl, d_b2["hi"], d_b2["lo"]]
                tr.op_multi("te", facs, reads=reads, writes=[d_pso2])
            tr.dma("qs", s_x1t[:],
                   sc["x1_s"][tb * 128:(tb + 1) * 128,
                              ow * OW:(ow + 1) * OW],
                   reads=[dep["x1"]], writes=[d_x1t])
            tr.op("ve", lambda e: e.tensor_tensor(
                s_out[:], ps_o2.ap()[:, :], s_x1t.ap()[:, :], ALU.add),
                reads=[d_pso2, d_x1t], writes=[d_out])
            tr.dma("qs",
                   io["out"][tb * 128:(tb + 1) * 128, ow * OW:(ow + 1) * OW],
                   s_out[:], reads=[d_out], writes=[dep["out"]])


_NC_CACHE = [None]
_LAST_RES = [None]


def _host_consts():
    import jax
    import jax.numpy as jnp
    cpu = jax.devices("cpu")[0]
    with jax.default_device(cpu):
        inv = 1.0 / (10000.0 ** (jnp.arange(0, HD, 2, dtype=jnp.float32)
                                 / HD))
        ang = jnp.arange(SEQ, dtype=jnp.float32)[:, None] * inv[None, :]
        cos = np.ascontiguousarray(np.asarray(jnp.cos(ang)).T)
        sin = np.ascontiguousarray(np.asarray(jnp.sin(ang)).T)
    cosF = np.ascontiguousarray(np.vstack([cos, cos]))
    sinF2 = np.ascontiguousarray(np.vstack([-sin, sin]))
    mask = np.where(np.arange(896)[None, :] <= np.arange(128)[:, None] + 384,
                    np.float32(0.0), np.float32(NEGBIG)).astype(np.float32)
    ident = np.eye(128, dtype=np.float32)
    perm = np.zeros((128, 128), dtype=np.float32)
    for m in range(128):
        perm[(m + 64) % 128, m] = 1.0
    return cosF, sinF2, mask, ident, perm


def kernel(L0_attn_hidden_states, L0_attn_Wq, L0_attn_Wk, L0_attn_Wv,
           L0_attn_Wo, L0_ffn_w1, L0_ffn_w3, L0_ffn_w2, L0_ffn_lora_A1,
           L0_ffn_lora_B1, L0_ffn_lora_A3, L0_ffn_lora_B3, L0_ffn_lora_A2,
           L0_ffn_lora_B2):
    f32 = lambda a: np.ascontiguousarray(np.asarray(a, dtype=np.float32))
    x = f32(L0_attn_hidden_states).reshape(TOK, DM)
    wq, wk, wv, wo = (f32(L0_attn_Wq), f32(L0_attn_Wk), f32(L0_attn_Wv),
                      f32(L0_attn_Wo))
    w1, w3, w2 = f32(L0_ffn_w1), f32(L0_ffn_w3), f32(L0_ffn_w2)
    a1, b1 = f32(L0_ffn_lora_A1), f32(L0_ffn_lora_B1)
    a3, b3 = f32(L0_ffn_lora_A3), f32(L0_ffn_lora_B3)
    a2, b2 = f32(L0_ffn_lora_A2), f32(L0_ffn_lora_B2)
    cosF, sinF2, mask, ident, perm = _host_consts()

    if _NC_CACHE[0] is None:
        _NC_CACHE[0] = build_nc()
    nc = _NC_CACHE[0]

    in_maps = []
    for c in range(NCORE):
        hs = slice(c * HPC * HD, (c + 1) * HPC * HD)
        in_maps.append({
            "x_my": x[c * TPC:(c + 1) * TPC],
            "wq_my": wq[hs], "wk_my": wk[hs], "wv_my": wv[hs],
            "wo_my": wo[hs],
            "w1": w1, "w3": w3, "w2": w2,
            "a1": a1, "b1": b1, "a3": a3, "b3": b3, "a2": a2, "b2": b2,
            "cosF": cosF, "sinF2": sinF2, "maskS": mask, "identf": ident,
            "permF": perm,
        })
    res = run_bass_kernel_spmd(nc, in_maps, core_ids=list(range(NCORE)))
    _LAST_RES[0] = res.results
    outs = np.concatenate([np.asarray(res.results[c]["out"])
                           for c in range(NCORE)], axis=0)
    return outs.reshape(BATCH, SEQ, DM)


# version 11
# speedup vs baseline: 1.2002x; 1.0204x over previous
import math
import numpy as np
from contextlib import ExitStack

import concourse.bass as bass
import concourse.mybir as mybir
from concourse.expressions import RuntimeValue
from concourse.bass_utils import run_bass_kernel_spmd

F32 = mybir.dt.float32
BF16 = mybir.dt.bfloat16
AF = mybir.ActivationFunctionType
ALU = mybir.AluOpType
AX = mybir.AxisListType.X

NCORE = 8
BATCH = 2
SEQ = 4096
DM = 4096
NH = 32
HD = 128
HID = 11008
RK = 16
TOK = BATCH * SEQ
TPC = TOK // NCORE
HPC = NH // NCORE
NIC = DM // 128
NHC = HID // 128
SCALE = 1.0 / math.sqrt(HD)
EPS = 1e-5
NEGBIG = -3.0e38
SEGS = [(0, 2048, 4), (2048, 3072, 6), (3072, 4096, 8)]
DBG = ()


class Dep:
    __slots__ = ("w", "readers")

    def __init__(self):
        self.w = None
        self.readers = []


class Tr:
    def __init__(self, nc, ctx):
        self.nc = nc
        self.eng = {"te": nc.tensor, "ve": nc.vector, "se": nc.scalar,
                    "pe": nc.gpsimd, "sy": nc.sync}
        self.sems = {}
        self.cnt = {}
        for name in ("te", "ve", "se", "pe", "qs", "qa", "qg", "cc"):
            self.sems[name] = ctx.enter_context(nc.semaphore("sem_" + name))
            self.cnt[name] = 0
        self.wm = {}

    def _wait(self, en, need):
        e = self.eng[en]
        for s, v in sorted(need.items()):
            if v > 0 and self.wm.get((en, s), 0) < v:
                e.wait_ge(self.sems[s], v)
                self.wm[(en, s)] = v

    def op(self, en, factory, reads=(), writes=(), sem=None, step=1):
        need = {}
        for d in reads:
            if d.w:
                s, v = d.w
                if need.get(s, 0) < v:
                    need[s] = v
        for d in writes:
            if d.w:
                s, v = d.w
                if need.get(s, 0) < v:
                    need[s] = v
            for s, v in d.readers:
                if need.get(s, 0) < v:
                    need[s] = v
        self._wait(en, need)
        inst = factory(self.eng[en])
        s = sem if sem is not None else en
        inst.then_inc(self.sems[s], step)
        self.cnt[s] += step
        stamp = (s, self.cnt[s])
        for d in writes:
            d.w = stamp
            d.readers = []
        for d in reads:
            d.readers.append(stamp)
        return inst

    def op_multi(self, en, factories, reads=(), writes=(), sem=None,
                 step=1):
        need = {}
        for d in reads:
            if d.w:
                s, v = d.w
                if need.get(s, 0) < v:
                    need[s] = v
        for d in writes:
            if d.w:
                s, v = d.w
                if need.get(s, 0) < v:
                    need[s] = v
            for s, v in d.readers:
                if need.get(s, 0) < v:
                    need[s] = v
        self._wait(en, need)
        e = self.eng[en]
        for f in factories[:-1]:
            f(e)
        inst = factories[-1](e)
        s = sem if sem is not None else en
        inst.then_inc(self.sems[s], step)
        self.cnt[s] += step
        stamp = (s, self.cnt[s])
        for d in writes:
            d.w = stamp
            d.readers = []
        for d in reads:
            d.readers.append(stamp)
        return inst

    def dma(self, queue, out, in_, reads=(), writes=(), transpose=False):
        en = {"qs": "sy", "qa": "se", "qg": "pe"}[queue]
        if transpose:
            fac = lambda e: e.dma_start(out=out, in_=in_, transpose=True)
        else:
            fac = lambda e: e.dma_start(out=out, in_=in_)
        return self.op(en, fac, reads=reads, writes=writes, sem=queue,
                       step=16)

    def wait_all(self, en):
        self._wait(en, dict(self.cnt))

    def fence(self):
        for en in ("te", "ve", "se", "pe", "sy"):
            self._wait(en, dict(self.cnt))


def nc_sb(nc, ctx, name, shape, dtype=F32):
    return ctx.enter_context(nc.sbuf_tensor(name, shape, dtype))


def nc_ps(nc, ctx, name, shape, dtype=F32):
    return ctx.enter_context(nc.psum_tensor(name, shape, dtype))


def emit_split(tr, en, src_ap, d_src, hi_ap, d_hi, lo_ap, d_lo):
    tr.op(en, lambda e: e.tensor_copy(hi_ap, src_ap),
          reads=[d_src], writes=[d_hi])
    tr.op(en, lambda e: e.tensor_tensor(lo_ap, src_ap, hi_ap, ALU.subtract),
          reads=[d_src, d_hi], writes=[d_lo])


_QK_SEQ = [0]


class QuantKit:
    def __init__(self, nc, ctx):
        _QK_SEQ[0] += 1
        p = "qk%d_" % _QK_SEQ[0]
        self.amax = nc_sb(nc, ctx, p + "amax", [128, 128])
        self.rs = nc_sb(nc, ctx, p + "rs", [128, 128])
        self.sneg = nc_sb(nc, ctx, p + "sneg", [128, 128])
        self.u = nc_sb(nc, ctx, p + "u", [128, 2048])
        self.c = nc_sb(nc, ctx, p + "c", [128, 2048])
        self.stats = nc_sb(nc, ctx, p + "stats", [128, 64])
        self.mv = nc_sb(nc, ctx, p + "mv", [128, 8])
        (self.d_amax, self.d_rs, self.d_sneg, self.d_u, self.d_c,
         self.d_stats, self.d_mv) = (Dep() for _ in range(7))


def emit_quant(tr, qk, t_in, d_in, t_dq, d_dq):
    for (s0, s1, bits) in SEGS:
        w = s1 - s0
        G = w // 32
        maxq = float(2 ** (bits - 1) - 1)
        shift = float(2 ** (bits - 1))
        src3 = t_in.ap()[:, s0:s1].rearrange("p (g k) -> p g k", k=32)
        tr.op("ve", lambda e, _s=src3, _G=G: e.tensor_reduce(
            qk.amax[:, :_G], _s, AX, ALU.max, apply_absolute_value=True),
            reads=[d_in], writes=[qk.d_amax])
        tr.op("ve", lambda e, _G=G: e.tensor_single_scalar(
            qk.amax[:, :_G], qk.amax[:, :_G], 1e-12, ALU.max),
            reads=[qk.d_amax], writes=[qk.d_amax])
        tr.op("ve", lambda e, _G=G: e.reciprocal(
            qk.rs[:, :_G], qk.amax[:, :_G]),
            reads=[qk.d_amax], writes=[qk.d_rs])
        rs_b = qk.rs.ap()[:, :G].unsqueeze(2).broadcast_to([128, G, 32])
        u3 = qk.u.ap()[:, :w].rearrange("p (g k) -> p g k", k=32)
        tr.op("ve", lambda e, _s=src3, _m=maxq, _r=rs_b, _u=u3:
              e.scalar_tensor_tensor(_u, _s, _m, _r, ALU.mult, ALU.mult),
              reads=[d_in, qk.d_rs], writes=[qk.d_u])
        tr.op("ve", lambda e, _a=s0, _b=s1, _w=w: e.tensor_copy(
            t_dq[:, _a:_b], qk.u[:, :_w]),
            reads=[qk.d_u], writes=[d_dq])
        for i in range(bits):
            bbit = float(2 ** (bits - 1 - i))
            thr = bbit - 0.5 - shift
            tr.op("ve", lambda e, _t=thr, _w=w: e.tensor_single_scalar(
                qk.c[:, :_w], qk.u[:, :_w], _t, ALU.is_ge),
                reads=[qk.d_u], writes=[qk.d_c])
            tr.op("ve", lambda e, _b=bbit, _w=w: e.scalar_tensor_tensor(
                qk.u[:, :_w], qk.c[:, :_w], -_b, qk.u[:, :_w],
                ALU.mult, ALU.add),
                reads=[qk.d_c, qk.d_u], writes=[qk.d_u])
        tr.op("ve", lambda e, _s=shift, _w=w, _a=s0, _b=s1:
              e.scalar_tensor_tensor(qk.c[:, :_w], qk.u[:, :_w], _s,
                                     t_dq[:, _a:_b], ALU.add, ALU.subtract),
              reads=[qk.d_u, d_dq], writes=[qk.d_c])
        tr.op("ve", lambda e, _G=G, _m=maxq: e.tensor_single_scalar(
            qk.sneg[:, :_G], qk.amax[:, :_G], -1.0 / _m, ALU.mult),
            reads=[qk.d_amax], writes=[qk.d_sneg])
        sneg_b = qk.sneg.ap()[:, :G].unsqueeze(2).broadcast_to([128, G, 32])
        dq3 = t_dq.ap()[:, s0:s1].rearrange("p (g k) -> p g k", k=32)
        c3 = qk.c.ap()[:, :w].rearrange("p (g k) -> p g k", k=32)
        tr.op("ve", lambda e, _d=dq3, _c=c3, _sb=sneg_b: e.tensor_tensor(
            _d, _c, _sb, ALU.mult),
            reads=[qk.d_c, qk.d_sneg], writes=[d_dq])


def emit_ln(tr, qk, t_x, d_x, t_out, d_out):
    nch = DM // 512
    for c in range(nch):
        tr.op("ve", lambda e, _c=c: e.bn_stats(
            qk.stats[:, 6 * _c:6 * _c + 6],
            t_x[:, 512 * _c:512 * (_c + 1)]),
            reads=[d_x], writes=[qk.d_stats])
    tr.op("ve", lambda e, _n=nch: e.bn_aggr(
        qk.mv[:, 0:2], qk.stats[:, 0:6 * _n]),
        reads=[qk.d_stats], writes=[qk.d_mv])
    tr.op("ve", lambda e: e.tensor_single_scalar(
        qk.mv[:, 2:3], qk.mv[:, 1:2], EPS, ALU.add),
        reads=[qk.d_mv], writes=[qk.d_mv])
    tr.op("ve", lambda e: e.reciprocal(qk.mv[:, 3:4], qk.mv[:, 2:3]),
          reads=[qk.d_mv], writes=[qk.d_mv])
    tr.op("se", lambda e: e.activation(qk.mv[:, 4:5], qk.mv[:, 3:4], AF.Sqrt),
          reads=[qk.d_mv], writes=[qk.d_mv])
    tr.op("ve", lambda e: e.tensor_scalar(
        t_out[:], t_x[:], qk.mv.ap()[:, 0:1], qk.mv.ap()[:, 4:5],
        ALU.subtract, ALU.mult),
        reads=[d_x, qk.d_mv], writes=[d_out])


def mm3f(pst, lh, ll, rh, rl, first, last):
    out = []
    combos = ((lh, rh), (lh, rl), (ll, rh))
    for li, (lw, rx) in enumerate(combos):
        out.append(lambda e, _l=lw, _r=rx, _s=(first and li == 0),
                   _e=(last and li == 2), _p=pst:
                   e.matmul(_p, _l, _r, start=_s, stop=_e))
    return out


def build_nc():
    nc = bass.Bass("TRN2", target_bir_lowering=False, debug=False,
                   num_devices=NCORE)

    def din(name, shape, dtype=F32):
        return nc.dram_tensor(name, shape, dtype, kind="ExternalInput").ap()

    def dint(name, shape, dtype):
        return nc.dram_tensor(name, shape, dtype, kind="Internal").ap()

    io = {}
    io["x_my"] = din("x_my", [TPC, DM])
    for nm in ("q", "k", "v", "o"):
        io["w%s_my" % nm] = din("w%s_my" % nm, [HPC * HD, DM])
    io["w1"] = din("w1", [HID, DM])
    io["w3"] = din("w3", [HID, DM])
    io["w2"] = din("w2", [DM, HID])
    for nm in ("a1", "b1", "a3", "b3", "a2", "b2"):
        shp = {"a1": [RK, DM], "a3": [RK, DM], "a2": [RK, HID],
               "b1": [HID, RK], "b3": [HID, RK], "b2": [DM, RK]}[nm]
        io[nm] = din(nm, shp)
    io["cosF"] = din("cosF", [128, SEQ])
    io["sinF2"] = din("sinF2", [128, SEQ])
    io["maskS"] = din("maskS", [128, 896])
    io["identf"] = din("identf", [128, 128])
    io["permF"] = din("permF", [128, 128])
    io["out"] = nc.dram_tensor("out", [TPC, DM], F32,
                               kind="ExternalOutput").ap()

    sc = {}
    sc["contrib1"] = dint("contrib1", [DM, 3072], BF16)
    sc["gath1"] = nc.dram_tensor("gath1", [NCORE * DM, 3072], BF16, kind="Internal", addr_space="Shared").ap()
    sc["contrib2"] = dint("contrib2", [TOK, HPC * HD], F32)
    sc["gath2"] = nc.dram_tensor("gath2", [NCORE * TOK, HPC * HD], F32, kind="Internal", addr_space="Shared").ap()
    sc["xq_hi_n"] = dint("xq_hi_n", [TPC, DM], BF16)
    sc["xq_lo_n"] = dint("xq_lo_n", [TPC, DM], BF16)
    for nm in ("q", "k", "v", "o"):
        for h in ("hi", "lo"):
            sc["wn_%s_%s" % (nm, h)] = dint("wn_%s_%s" % (nm, h),
                                            [HPC * HD, DM], BF16)
    for nm in ("q", "k", "v"):
        for h in ("hi", "lo"):
            sc["wT_%s_%s" % (nm, h)] = dint("wT_%s_%s" % (nm, h),
                                            [HPC, 128, DM], BF16)
    for t in ("q", "k"):
        for h in ("hi", "lo"):
            sc["%ss_%s" % (t, h)] = dint("%ss_%s" % (t, h),
                                         [HPC * HD, TOK], BF16)
    sc["vs"] = dint("vs", [TOK, HPC * HD], F32)
    for wnm, rows, cols in (("w1", HID, DM), ("w3", HID, DM),
                            ("w2", DM, HID)):
        for h in ("hi", "lo"):
            sc["%sn_%s" % (wnm, h)] = dint("%sn_%s" % (wnm, h),
                                           [rows, cols], BF16)
    for h in ("hi", "lo"):
        sc["w2T_%s" % h] = dint("w2T_%s" % h, [HID, DM], BF16)
    sc["x1_s"] = dint("x1_s", [TPC, DM], F32)
    for h in ("hi", "lo"):
        sc["xn2n_%s" % h] = dint("xn2n_%s" % h, [TPC, DM], BF16)
        sc["xn2T_%s" % h] = dint("xn2T_%s" % h, [128, NIC * TPC], BF16)
        sc["odqn_%s" % h] = dint("odqn_%s" % h, [TPC, DM], BF16)
        sc["odqT_%s" % h] = dint("odqT_%s" % h, [128, NIC * TPC], BF16)
        sc["gn_%s" % h] = dint("gn_%s" % h, [TPC, HID], BF16)
        sc["gT_%s" % h] = dint("gT_%s" % h, [128, NHC * TPC], BF16)
        sc["b1T_%s" % h] = dint("b1T_%s" % h, [RK, HID], BF16)
        sc["b3T_%s" % h] = dint("b3T_%s" % h, [RK, HID], BF16)
        sc["b2T_%s" % h] = dint("b2T_%s" % h, [RK, DM], BF16)

    dbg = {}
    for nm in DBG:
        src = sc[nm]
        dbg[nm] = nc.dram_tensor("dbg_" + nm, list(src.shape), src.dtype,
                                 kind="ExternalOutput").ap()

    with ExitStack() as ctx:
        tr = Tr(nc, ctx)
        dep = {k: Dep() for k in (
            "xqh", "xql", "c1", "g1", "qk", "vs", "c2", "g2", "wT",
            "w1n", "w3n", "w2n", "w2T", "x1", "xn2n", "xn2T", "odqn",
            "odqT",
            "gn", "gT", "out", "b1T", "b3T", "b2T")}
        for nm in ("q", "k", "v", "o"):
            dep["wn_" + nm] = Dep()

        s_ident = nc_sb(nc, ctx, "s_ident", [128, 128])
        d_ident = Dep()
        tr.dma("qs", s_ident[:], io["identf"], writes=[d_ident])
        s_perm = nc_sb(nc, ctx, "s_perm", [128, 128])
        d_perm = Dep()
        tr.dma("qs", s_perm[:], io["permF"], writes=[d_perm])
        s_mask = nc_sb(nc, ctx, "s_mask", [128, 896])
        d_mask = Dep()
        tr.dma("qs", s_mask[:], io["maskS"], writes=[d_mask])

        with ExitStack() as cp0:
            qk0 = QuantKit(nc, cp0)
            phase0(nc, tr, io, sc, dep, qk0)
        tr.fence()
        tr.op("pe", lambda e: e.collective_compute(
            "AllGather", ALU.bypass, replica_groups=[list(range(NCORE))],
            ins=[sc["contrib1"]], outs=[sc["gath1"]]),
            reads=[dep["c1"]], writes=[dep["g1"]], sem="cc")
        phase_ffn_split(nc, tr, io, sc, dep)
        tr.fence()
        with ExitStack() as cwt:
            s_wt2 = [nc_sb(nc, cwt, "wt2_%d" % i, [128, 1024], BF16)
                     for i in range(4)]
            d_wt2 = [Dep() for _ in range(4)]
            pi = 0
            for half in ("hi", "lo"):
                for hc in range(NHC):
                    for g in range(4):
                        i = pi % 4
                        pi += 1
                        tr.dma("qa", s_wt2[i][:],
                               sc["w2n_%s" % half][
                                   g * 1024:(g + 1) * 1024,
                                   hc * 128:(hc + 1) * 128],
                               reads=[dep["w2n"]], writes=[d_wt2[i]],
                               transpose=True)
                        tr.dma("qs",
                               sc["w2T_%s" % half][
                                   hc * 128:(hc + 1) * 128,
                                   g * 1024:(g + 1) * 1024],
                               s_wt2[i][:], reads=[d_wt2[i]],
                               writes=[dep["w2T"]])
        tr.fence()
        phase_qkv(nc, tr, io, sc, dep, s_perm, d_perm, s_ident, d_ident)
        tr.fence()
        phase_attn(nc, tr, io, sc, dep, s_ident, d_ident, s_mask, d_mask)
        tr.fence()
        tr.op("pe", lambda e: e.collective_compute(
            "AllGather", ALU.bypass, replica_groups=[list(range(NCORE))],
            ins=[sc["contrib2"]], outs=[sc["gath2"]]),
            reads=[dep["c2"]], writes=[dep["g2"]], sem="cc")
        tr.fence()
        with ExitStack() as ctl:
            qk2 = QuantKit(nc, ctl)
            phase_tail(nc, tr, io, sc, dep, qk2, s_ident, d_ident)

        for nm in DBG:
            tr.wait_all("pe")
            tr.dma("qs", dbg[nm], sc[nm])
        tr.wait_all("pe")
        tr.wait_all("sy")
        tr.wait_all("se")
    return nc


def phase0(nc, tr, io, sc, dep, qk):
    with ExitStack() as c0:
        s_xt = [nc_sb(nc, c0, "p0_xt%d" % i, [128, DM]) for i in range(2)]
        s_xn = nc_sb(nc, c0, "p0_xn", [128, DM])
        s_dq = nc_sb(nc, c0, "p0_dq", [128, DM])
        s_hi = nc_sb(nc, c0, "p0_hi", [128, DM], BF16)
        s_lo = nc_sb(nc, c0, "p0_lo", [128, DM], BF16)
        d_xt = [Dep(), Dep()]
        d_xn, d_dq, d_hi, d_lo = (Dep() for _ in range(4))
        for tb in range(TPC // 128):
            bi = tb % 2
            tr.dma("qs", s_xt[bi][:],
                   io["x_my"][tb * 128:(tb + 1) * 128, :],
                   writes=[d_xt[bi]])
            emit_ln(tr, qk, s_xt[bi], d_xt[bi], s_xn, d_xn)
            emit_quant(tr, qk, s_xn, d_xn, s_dq, d_dq)
            emit_split(tr, "ve", s_dq[:], d_dq, s_hi[:], d_hi, s_lo[:], d_lo)
            tr.dma("qs", sc["xq_hi_n"][tb * 128:(tb + 1) * 128, :], s_hi[:],
                   reads=[d_hi], writes=[dep["xqh"]])
            tr.dma("qs", sc["xq_lo_n"][tb * 128:(tb + 1) * 128, :], s_lo[:],
                   reads=[d_lo], writes=[dep["xql"]])
        for nm in ("q", "k", "v", "o"):
            dn = dep["wn_" + nm]
            for r in range(HPC * HD // 128):
                bi = r % 2
                tr.dma("qs", s_xt[bi][:],
                       io["w%s_my" % nm][r * 128:(r + 1) * 128, :],
                       writes=[d_xt[bi]])
                emit_quant(tr, qk, s_xt[bi], d_xt[bi], s_dq, d_dq)
                emit_split(tr, "ve", s_dq[:], d_dq, s_hi[:], d_hi,
                           s_lo[:], d_lo)
                tr.dma("qs", sc["wn_%s_hi" % nm][r * 128:(r + 1) * 128, :],
                       s_hi[:], reads=[d_hi], writes=[dn])
                tr.dma("qs", sc["wn_%s_lo" % nm][r * 128:(r + 1) * 128, :],
                       s_lo[:], reads=[d_lo], writes=[dn])

    tr.fence()
    with ExitStack() as c1:
        s_tx = [nc_sb(nc, c1, "p0_tx%d" % i, [128, TPC], BF16)
                for i in range(2)]
        d_tx = [Dep(), Dep()]
        pi = 0
        for half, coff in (("hi", 0), ("lo", TPC)):
            src = sc["xq_%s_n" % half]
            rdep = dep["xqh"] if half == "hi" else dep["xql"]
            for ic in range(NIC):
                i = pi % 2
                pi += 1
                tr.dma("qa", s_tx[i][:], src[:, ic * 128:(ic + 1) * 128],
                       reads=[rdep], writes=[d_tx[i]], transpose=True)
                tr.dma("qs",
                       sc["contrib1"][ic * 128:(ic + 1) * 128,
                                      coff:coff + TPC],
                       s_tx[i][:], reads=[d_tx[i]], writes=[dep["c1"]])
        s_tw = [nc_sb(nc, c1, "p0_tw%d" % i, [128, HD], BF16)
                for i in range(2)]
        d_tw = [Dep(), Dep()]
        for nm in ("q", "k", "v"):
            for half in ("hi", "lo"):
                for h in range(HPC):
                    for ic in range(NIC):
                        i = pi % 2
                        pi += 1
                        tr.dma("qa", s_tw[i][:],
                               sc["wn_%s_%s" % (nm, half)][
                                   h * HD:(h + 1) * HD,
                                   ic * 128:(ic + 1) * 128],
                               reads=[dep["wn_" + nm]], writes=[d_tw[i]],
                               transpose=True)
                        tr.dma("qs",
                               sc["wT_%s_%s" % (nm, half)][
                                   h, :, ic * 128:(ic + 1) * 128],
                               s_tw[i][:], reads=[d_tw[i]],
                               writes=[dep["wT"]])
        s_to = [nc_sb(nc, c1, "p0_to%d" % i, [128, HPC * HD], BF16)
                for i in range(2)]
        d_to = [Dep(), Dep()]
        for hx, half in enumerate(("hi", "lo")):
            coff = 2048 + hx * 512
            for ic in range(NIC):
                i = pi % 2
                pi += 1
                tr.dma("qa", s_to[i][:],
                       sc["wn_o_%s" % half][:, ic * 128:(ic + 1) * 128],
                       reads=[dep["wn_o"]], writes=[d_to[i]], transpose=True)
                tr.dma("qs",
                       sc["contrib1"][ic * 128:(ic + 1) * 128,
                                      coff:coff + 512],
                       s_to[i][:], reads=[d_to[i]], writes=[dep["c1"]])


def phase_ffn_split(nc, tr, io, sc, dep):
    with ExitStack() as cf:
        bufs = []
        for i in range(2):
            bufs.append((nc_sb(nc, cf, "fs_f%d" % i, [128, HID]),
                         nc_sb(nc, cf, "fs_h%d" % i, [128, HID], BF16),
                         nc_sb(nc, cf, "fs_l%d" % i, [128, HID], BF16),
                         Dep(), Dep(), Dep()))
        pi = 0
        for wnm, rows, cols in (("w1", HID, DM), ("w3", HID, DM),
                                ("w2", DM, HID)):
            for r in range(rows // 128):
                sf, sh, sl, df, dh, dl = bufs[pi % 2]
                pi += 1
                tr.dma("qa", sf[:, :cols],
                       io[wnm][r * 128:(r + 1) * 128, :], writes=[df])
                emit_split(tr, "pe", sf[:, :cols], df, sh[:, :cols], dh,
                           sl[:, :cols], dl)
                tr.dma("qs", sc["%sn_hi" % wnm][r * 128:(r + 1) * 128, :],
                       sh[:, :cols], reads=[dh], writes=[dep["%sn" % wnm]])
                tr.dma("qs", sc["%sn_lo" % wnm][r * 128:(r + 1) * 128, :],
                       sl[:, :cols], reads=[dl], writes=[dep["%sn" % wnm]])


def phase_qkv(nc, tr, io, sc, dep, s_perm, d_perm, s_ident, d_ident):
    WIN = 512
    NWIN = TOK // WIN
    NG = 2
    GCH = NIC // NG
    with ExitStack() as c1:
        s_w = {}
        d_w = {}
        for nm in ("q", "k", "v"):
            for half in ("hi", "lo"):
                for st in range(2):
                    s_w[nm, half, st] = nc_sb(
                        nc, c1, "q_w%s%s%d" % (nm, half, st),
                        [128, DM], BF16)
                    d_w[nm, half, st] = Dep()
        s_x = {half: nc_sb(nc, c1, "q_x" + half, [128, NIC * WIN], BF16)
               for half in ("hi", "lo")}
        d_x = {(half, g): Dep() for half in ("hi", "lo") for g in range(NG)}
        s_cos = nc_sb(nc, c1, "q_cos", [128, WIN])
        s_sin = nc_sb(nc, c1, "q_sin", [128, WIN])
        d_cos, d_sin = Dep(), Dep()
        ps_q = [nc_ps(nc, c1, "q_psq%d" % i, [128, WIN]) for i in range(2)]
        ps_k = [nc_ps(nc, c1, "q_psk%d" % i, [128, WIN]) for i in range(2)]
        ps_v = [nc_ps(nc, c1, "q_psv%d" % i, [128, WIN]) for i in range(2)]
        ps_sw = nc_ps(nc, c1, "q_pssw", [128, WIN])
        ps_t = nc_ps(nc, c1, "q_pst", [128, 128])
        d_psq = [Dep(), Dep()]
        d_psk = [Dep(), Dep()]
        d_psv = [Dep(), Dep()]
        d_pssw, d_pst = Dep(), Dep()
        s_f = [nc_sb(nc, c1, "q_f%d" % i, [128, WIN]) for i in range(2)]
        d_f = [Dep(), Dep()]
        s_a = nc_sb(nc, c1, "q_a", [128, WIN])
        d_a = Dep()
        s_b = nc_sb(nc, c1, "q_b", [128, WIN])
        d_b = Dep()
        s_bh = [nc_sb(nc, c1, "q_bh%d" % i, [128, WIN], BF16)
                for i in range(2)]
        s_bl = [nc_sb(nc, c1, "q_bl%d" % i, [128, WIN], BF16)
                for i in range(2)]
        d_bh = [Dep(), Dep()]
        d_bl = [Dep(), Dep()]
        s_ve = [nc_sb(nc, c1, "q_ve%d" % i, [128, WIN]) for i in range(2)]
        d_ve = [Dep(), Dep()]
        s_vt = [nc_sb(nc, c1, "q_vt%d" % i, [128, 128]) for i in range(2)]
        d_vt = [Dep(), Dep()]

        for h in range(HPC):
            st = h % 2
            for nm in ("q", "k", "v"):
                for half in ("hi", "lo"):
                    tr.dma("qs", s_w[nm, half, st][:],
                           sc["wT_%s_%s" % (nm, half)][h],
                           reads=[dep["wT"]], writes=[d_w[nm, half, st]])
            for w in range(NWIN):
                slot, coff = w // 2, (w % 2) * WIN
                seqpos = (w % (SEQ // WIN)) * WIN
                for g in range(NG):
                    for half, dd in (("hi", 0), ("lo", TPC)):
                        for sub in range(2):
                            ic0 = g * GCH + sub * (GCH // 2)
                            srcap = sc["gath1"][
                                slot * DM + ic0 * 128:
                                slot * DM + (ic0 + GCH // 2) * 128,
                                dd + coff:dd + coff + WIN].rearrange(
                                "(i p) j -> p i j", p=128)
                            dstap = s_x[half].ap()[
                                :, ic0 * WIN:(ic0 + GCH // 2) * WIN
                            ].rearrange("p (i j) -> p i j", j=WIN)
                            tr.dma("qs", dstap, srcap,
                                   reads=[dep["g1"]],
                                   writes=[d_x[half, g]])
                tr.dma("qs", s_cos[:], io["cosF"][:, seqpos:seqpos + WIN],
                       writes=[d_cos])
                tr.dma("qs", s_sin[:], io["sinF2"][:, seqpos:seqpos + WIN],
                       writes=[d_sin])
                wi = w % 2
                for nm, pst, dps in (("q", ps_q[wi], d_psq[wi]),
                                     ("k", ps_k[wi], d_psk[wi]),
                                     ("v", ps_v[wi], d_psv[wi])):
                    for g in range(NG):
                        facs = []
                        for icc in range(GCH):
                            ic = g * GCH + icc
                            facs += mm3f(pst[:],
                                s_w[nm, "hi", st].ap()[
                                    :, ic * 128:(ic + 1) * 128],
                                s_w[nm, "lo", st].ap()[
                                    :, ic * 128:(ic + 1) * 128],
                                s_x["hi"].ap()[:, ic * WIN:(ic + 1) * WIN],
                                s_x["lo"].ap()[:, ic * WIN:(ic + 1) * WIN],
                                ic == 0, ic == NIC - 1)
                        tr.op_multi("te", facs,
                                    reads=[d_w[nm, "hi", st],
                                           d_w[nm, "lo", st],
                                           d_x["hi", g], d_x["lo", g]],
                                    writes=[dps])
                for nm, pst, dps, pre in (
                        ("q", ps_q[wi], d_psq[wi], "qs"),
                        ("k", ps_k[wi], d_psk[wi], "ks")):
                    tr.op("ve", lambda e, _p=pst, _w=wi: e.tensor_copy(
                        s_f[_w][:], _p[:]), reads=[dps], writes=[d_f[wi]])
                    tr.op("te", lambda e, _w=wi: e.matmul(
                        ps_sw[:], s_perm[:], s_f[_w][:],
                        start=True, stop=True),
                        reads=[d_perm, d_f[wi]], writes=[d_pssw])
                    tr.op("ve", lambda e, _w=wi: e.tensor_tensor(
                        s_a[:], s_f[_w].ap()[:, :], s_cos.ap()[:, :],
                        ALU.mult),
                        reads=[d_f[wi], d_cos], writes=[d_a])
                    tr.op("ve", lambda e: e.tensor_tensor(
                        s_b[:], ps_sw.ap()[:, :], s_sin.ap()[:, :],
                        ALU.mult),
                        reads=[d_pssw, d_sin], writes=[d_b])
                    tr.op("ve", lambda e, _w=wi: e.tensor_tensor(
                        s_f[_w][:], s_a.ap()[:, :], s_b.ap()[:, :],
                        ALU.add),
                        reads=[d_a, d_b], writes=[d_f[wi]])
                    emit_split(tr, "ve", s_f[wi][:], d_f[wi],
                               s_bh[wi][:], d_bh[wi],
                               s_bl[wi][:], d_bl[wi])
                    tr.dma("qs", sc[pre + "_hi"][h * HD:(h + 1) * HD,
                                                 w * WIN:(w + 1) * WIN],
                           s_bh[wi][:], reads=[d_bh[wi]],
                           writes=[dep["qk"]])
                    tr.dma("qs", sc[pre + "_lo"][h * HD:(h + 1) * HD,
                                                 w * WIN:(w + 1) * WIN],
                           s_bl[wi][:], reads=[d_bl[wi]],
                           writes=[dep["qk"]])
                tr.op("ve", lambda e, _w=wi: e.tensor_copy(
                    s_ve[_w][:], ps_v[_w][:]),
                    reads=[d_psv[wi]], writes=[d_ve[wi]])
                for j in range(WIN // 128):
                    tr.op("te", lambda e, _j=j, _w=wi: e.transpose(
                        ps_t[:], s_ve[_w].ap()[:, _j * 128:(_j + 1) * 128],
                        s_ident[:]),
                        reads=[d_ve[wi], d_ident], writes=[d_pst])
                    tr.op("se", lambda e, _w=wi: e.copy(
                        s_vt[_w][:], ps_t[:]),
                        reads=[d_pst], writes=[d_vt[wi]])
                    tr.dma("qs",
                           sc["vs"][w * WIN + j * 128:
                                    w * WIN + (j + 1) * 128,
                                    h * HD:(h + 1) * HD],
                           s_vt[wi][:], reads=[d_vt[wi]],
                           writes=[dep["vs"]])


def phase_attn(nc, tr, io, sc, dep, s_ident, d_ident, s_mask, d_mask):
    KW = 512
    with ExitStack() as ca:
        s_k = {}
        d_k = {}
        for h in range(HPC):
            for half in ("hi", "lo"):
                s_k[h, half] = nc_sb(nc, ca, "a_k%d%s" % (h, half),
                                     [128, SEQ], BF16)
                d_k[h, half] = Dep()
        s_v = nc_sb(nc, ca, "a_v", [128, (SEQ // 128) * HPC * HD])
        d_v = Dep()
        s_q = [{half: nc_sb(nc, ca, "a_q%s%d" % (half, i),
                            [128, HPC * HD], BF16) for half in ("hi", "lo")}
               for i in range(2)]
        d_q = [{half: Dep() for half in ("hi", "lo")} for i in range(2)]
        s_S = [nc_sb(nc, ca, "a_S%d" % p, [128, SEQ]) for p in range(2)]
        d_S = [Dep(), Dep()]
        s_P = [nc_sb(nc, ca, "a_P%d" % p, [128, SEQ]) for p in range(2)]
        d_P = [Dep(), Dep()]
        s_m = [nc_sb(nc, ca, "a_m%d" % p, [128, 4]) for p in range(2)]
        d_m = [Dep(), Dep()]
        s_PT = [nc_sb(nc, ca, "a_PT%d" % p, [128, 256]) for p in range(2)]
        d_PT = [[Dep(), Dep()], [Dep(), Dep()]]
        s_stage = nc_sb(nc, ca, "a_stage", [128, HPC * HD])
        d_stage = Dep()
        ps_S = [nc_ps(nc, ca, "a_psS%d" % p, [128, KW]) for p in range(2)]
        d_psS = [Dep(), Dep()]
        ps_T = [nc_ps(nc, ca, "a_psT%d" % i, [128, 128]) for i in range(2)]
        d_psT = [Dep(), Dep()]
        ps_o = [nc_ps(nc, ca, "a_pso%d" % p, [128, HD]) for p in range(2)]
        d_pso = [Dep(), Dep()]

        for s in range(BATCH):
            for h in range(HPC):
                for half in ("hi", "lo"):
                    tr.dma("qs", s_k[h, half][:],
                           sc["ks_%s" % half][h * HD:(h + 1) * HD,
                                              s * SEQ:(s + 1) * SEQ],
                           reads=[dep["qk"]], writes=[d_k[h, half]])
            for j in range(SEQ // 128):
                tr.dma("qs", s_v[:, j * HPC * HD:(j + 1) * HPC * HD],
                       sc["vs"][s * SEQ + j * 128:s * SEQ + (j + 1) * 128, :],
                       reads=[dep["vs"]], writes=[d_v])
            for qb in range(SEQ // 128):
                nk = qb // 4 + 1
                nkc = nk * 4
                qi = qb % 2
                c0 = s * SEQ + qb * 128
                for half in ("hi", "lo"):
                    tr.dma("qs",
                           s_q[qi][half].ap().rearrange(
                               "p (h j) -> p h j", h=HPC),
                           sc["qs_%s" % half][:, c0:c0 + 128].rearrange(
                               "(h p) j -> p h j", p=HD),
                           reads=[dep["qk"]], writes=[d_q[qi][half]])
                for h in range(HPC):
                    p = h % 2
                    for kw in range(nk):
                        tr.op_multi("te", mm3f(ps_S[p][:],
                            s_q[qi]["hi"].ap()[:, h * HD:(h + 1) * HD],
                            s_q[qi]["lo"].ap()[:, h * HD:(h + 1) * HD],
                            s_k[h, "hi"].ap()[:, kw * KW:(kw + 1) * KW],
                            s_k[h, "lo"].ap()[:, kw * KW:(kw + 1) * KW],
                            True, True),
                            reads=[d_q[qi]["hi"], d_q[qi]["lo"],
                                   d_k[h, "hi"], d_k[h, "lo"]],
                            writes=[d_psS[p]])
                        if kw == nk - 1:
                            moff = 384 - (qb % 4) * 128
                            tr.op("ve", lambda e, _k=kw, _m=moff, _p=p:
                                  e.tensor_tensor(
                                      s_S[_p][:, _k * KW:(_k + 1) * KW],
                                      ps_S[_p].ap()[:, :],
                                      s_mask.ap()[:, _m:_m + KW], ALU.add),
                                  reads=[d_psS[p], d_mask], writes=[d_S[p]])
                        elif kw % 2 == 0:
                            tr.op("se", lambda e, _k=kw, _p=p: e.copy(
                                s_S[_p][:, _k * KW:(_k + 1) * KW],
                                ps_S[_p][:]),
                                reads=[d_psS[p]], writes=[d_S[p]])
                        else:
                            tr.op("ve", lambda e, _k=kw, _p=p:
                                  e.tensor_copy(
                                      s_S[_p][:, _k * KW:(_k + 1) * KW],
                                      ps_S[_p][:]),
                                  reads=[d_psS[p]], writes=[d_S[p]])
                    tr.op("ve", lambda e, _n=nk, _p=p: e.reduce_max(
                        s_m[_p][:, 0:1], s_S[_p].ap()[:, :_n * KW], AX),
                        reads=[d_S[p]], writes=[d_m[p]])
                    tr.op("ve", lambda e, _p=p: e.tensor_single_scalar(
                        s_m[_p][:, 1:2], s_m[_p][:, 0:1], -SCALE, ALU.mult),
                        reads=[d_m[p]], writes=[d_m[p]])
                    tr.op("se", lambda e, _n=nk, _p=p: e.activation(
                        s_P[_p][:, :_n * KW], s_S[_p][:, :_n * KW], AF.Exp,
                        bias=s_m[_p].ap()[:, 1:2], scale=SCALE,
                        accum_out=s_m[_p][:, 2:3]),
                        reads=[d_S[p], d_m[p]], writes=[d_P[p], d_m[p]])
                    tr.op("ve", lambda e, _p=p: e.reciprocal(
                        s_m[_p][:, 3:4], s_m[_p][:, 2:3]),
                        reads=[d_m[p]], writes=[d_m[p]])
                    for kc in range(nkc):
                        ti = kc % 2
                        tr.op("te", lambda e, _k=kc, _t=ti, _p=p:
                              e.transpose(
                                  ps_T[_t][:],
                                  s_P[_p].ap()[:, _k * 128:(_k + 1) * 128],
                                  s_ident[:]),
                              reads=[d_P[p], d_ident], writes=[d_psT[ti]])
                        if ti == 0:
                            tr.op("ve", lambda e, _t=ti, _p=p:
                                  e.tensor_copy(
                                      s_PT[_p][:, _t * 128:(_t + 1) * 128],
                                      ps_T[_t][:]),
                                  reads=[d_psT[ti]], writes=[d_PT[p][ti]])
                        else:
                            tr.op("se", lambda e, _t=ti, _p=p: e.copy(
                                s_PT[_p][:, _t * 128:(_t + 1) * 128],
                                ps_T[_t][:]),
                                reads=[d_psT[ti]], writes=[d_PT[p][ti]])
                        tr.op("te", lambda e, _k=kc, _t=ti, _p=p, _h=h:
                              e.matmul(
                                  ps_o[_p][:],
                                  s_PT[_p].ap()[:, _t * 128:(_t + 1) * 128],
                                  s_v.ap()[:, _k * HPC * HD + _h * HD:
                                           _k * HPC * HD + (_h + 1) * HD],
                                  start=(_k == 0), stop=(_k == nkc - 1)),
                              reads=[d_PT[p][ti], d_v], writes=[d_pso[p]])
                    tr.op("se", lambda e, _h=h, _p=p: e.activation(
                        s_stage[:, _h * HD:(_h + 1) * HD], ps_o[_p][:],
                        AF.Copy, scale=s_m[_p].ap()[:, 3:4]),
                        reads=[d_pso[p], d_m[p]], writes=[d_stage])
                tr.dma("qs",
                       sc["contrib2"][c0:c0 + 128, :],
                       s_stage[:], reads=[d_stage], writes=[dep["c2"]])


def phase_lora_prep(nc, tr, ctx, io, sc, dep, s_ident, d_ident):
    lora = {}
    s_tmp = nc_sb(nc, ctx, "lp_tmp", [128, 128])
    d_tmp = Dep()
    t_ps = nc_ps(nc, ctx, "lp_ps", [128, 128])
    d_ps = Dep()
    s_asrc = nc_sb(nc, ctx, "lp_asrc", [RK, 128])
    d_asrc = Dep()
    for nm, dim in (("a1", DM), ("a3", DM), ("a2", HID)):
        nch = dim // 128
        hi = nc_sb(nc, ctx, "lp_%s_hi" % nm, [128, nch * RK], BF16)
        lo = nc_sb(nc, ctx, "lp_%s_lo" % nm, [128, nch * RK], BF16)
        dhi, dlo = Dep(), Dep()
        for c in range(nch):
            tr.dma("qs", s_asrc[:], io[nm][:, c * 128:(c + 1) * 128],
                   writes=[d_asrc])
            tr.op("te", lambda e: e.transpose(
                t_ps[:, :RK], s_asrc.ap()[:, :],
                s_ident.ap()[0:RK, 0:RK]),
                reads=[d_asrc, d_ident], writes=[d_ps])
            tr.op("ve", lambda e: e.tensor_copy(s_tmp[:, :RK], t_ps[:, :RK]),
                  reads=[d_ps], writes=[d_tmp])
            emit_split(tr, "ve", s_tmp[:, :RK], d_tmp,
                       hi[:, c * RK:(c + 1) * RK], dhi,
                       lo[:, c * RK:(c + 1) * RK], dlo)
        lora[nm] = (hi, lo, dhi, dlo)
    t_ps16 = nc_ps(nc, ctx, "lp_ps16", [RK, 128])
    d_ps16 = Dep()
    s_t2 = nc_sb(nc, ctx, "lp_t2", [RK, 128])
    s_t2h = nc_sb(nc, ctx, "lp_t2h", [RK, 128], BF16)
    s_t2l = nc_sb(nc, ctx, "lp_t2l", [RK, 128], BF16)
    d_t2, d_t2h, d_t2l = Dep(), Dep(), Dep()
    s_s2 = nc_sb(nc, ctx, "lp_s2", [128, RK])
    d_s2 = Dep()
    for nm, rows in (("b1", HID), ("b3", HID), ("b2", DM)):
        for c in range(rows // 128):
            tr.dma("qs", s_s2[:], io[nm][c * 128:(c + 1) * 128, :],
                   writes=[d_s2])
            tr.op("te", lambda e: e.transpose(
                t_ps16[:, :], s_s2.ap()[:, :], s_ident[:]),
                reads=[d_s2, d_ident], writes=[d_ps16])
            tr.op("ve", lambda e: e.tensor_copy(s_t2[:, :], t_ps16[:, :]),
                  reads=[d_ps16], writes=[d_t2])
            emit_split(tr, "ve", s_t2[:], d_t2, s_t2h[:], d_t2h,
                       s_t2l[:], d_t2l)
            tr.dma("qs", sc["%sT_hi" % nm][:, c * 128:(c + 1) * 128],
                   s_t2h[:], reads=[d_t2h], writes=[dep["%sT" % nm]])
            tr.dma("qs", sc["%sT_lo" % nm][:, c * 128:(c + 1) * 128],
                   s_t2l[:], reads=[d_t2l], writes=[dep["%sT" % nm]])
    return lora


def phase_tail(nc, tr, io, sc, dep, qk, s_ident, d_ident):
    pid = nc.partition_id()
    base = nc.sync.snap(pid * TPC)
    with ExitStack() as c0:
        s_o = [nc_sb(nc, c0, "t_o%d" % i, [128, DM]) for i in range(2)]
        s_dq = nc_sb(nc, c0, "t_dq", [128, DM])
        s_hi = nc_sb(nc, c0, "t_hi", [128, DM], BF16)
        s_lo = nc_sb(nc, c0, "t_lo", [128, DM], BF16)
        d_o = [Dep(), Dep()]
        d_dq, d_hi, d_lo = (Dep() for _ in range(3))
        for tb in range(TPC // 128):
            bi = tb % 2
            off = nc.sync.snap(base + tb * 128, donate=True)
            g2v = sc["gath2"].rearrange("(j t) f -> j t f", j=NCORE)
            tr.dma("qs",
                   s_o[bi].ap().rearrange("p (j f) -> p j f", j=NCORE),
                   g2v[:, bass.ds(off, 128), :].transpose([1, 0, 2]),
                   reads=[dep["g2"]], writes=[d_o[bi]])
            emit_quant(tr, qk, s_o[bi], d_o[bi], s_dq, d_dq)
            emit_split(tr, "ve", s_dq[:], d_dq, s_hi[:], d_hi, s_lo[:], d_lo)
            tr.dma("qs", sc["odqn_hi"][tb * 128:(tb + 1) * 128, :], s_hi[:],
                   reads=[d_hi], writes=[dep["odqn"]])
            tr.dma("qs", sc["odqn_lo"][tb * 128:(tb + 1) * 128, :], s_lo[:],
                   reads=[d_lo], writes=[dep["odqn"]])
        tr.fence()
        s_tx = nc_sb(nc, c0, "t_tx", [128, TPC], BF16)
        d_tx = Dep()
        for half in ("hi", "lo"):
            for ic in range(NIC):
                tr.dma("qa", s_tx[:],
                       sc["odqn_%s" % half][:, ic * 128:(ic + 1) * 128],
                       reads=[dep["odqn"]], writes=[d_tx], transpose=True)
                for tb in range(TPC // 128):
                    tr.dma("qs",
                           sc["odqT_%s" % half][
                               :, tb * (NIC * 128) + ic * 128:
                               tb * (NIC * 128) + (ic + 1) * 128],
                           s_tx[:, tb * 128:(tb + 1) * 128],
                           reads=[d_tx], writes=[dep["odqT"]])

    with ExitStack() as c1:
        s_oT = {half: nc_sb(nc, c1, "t_oT" + half, [128, 2 * NIC * 128],
                            BF16) for half in ("hi", "lo")}
        d_oT = {half: Dep() for half in ("hi", "lo")}
        s_wo = {half: nc_sb(nc, c1, "t_wo" + half, [128, (NIC // 2) * 512],
                            BF16) for half in ("hi", "lo")}
        d_wo = {half: Dep() for half in ("hi", "lo")}
        s_xr = nc_sb(nc, c1, "t_xr", [128, 512])
        d_xr = Dep()
        s_x1 = [nc_sb(nc, c1, "t_x1_%d" % i, [128, DM]) for i in range(2)]
        d_x1 = [Dep(), Dep()]
        s_xn2 = nc_sb(nc, c1, "t_xn2", [128, DM])
        d_xn2 = Dep()
        s_nh = nc_sb(nc, c1, "t_nh", [128, DM], BF16)
        s_nl = nc_sb(nc, c1, "t_nl", [128, DM], BF16)
        d_nh, d_nl = Dep(), Dep()
        ps_a = [nc_ps(nc, c1, "t_psa%d" % i, [128, 512]) for i in range(2)]
        d_psa = [Dep(), Dep()]
        for tp in range(TPC // 256):
            for half in ("hi", "lo"):
                for tbl in range(2):
                    tb = tp * 2 + tbl
                    tr.dma("qs",
                           s_oT[half][:, tbl * NIC * 128:
                                      (tbl + 1) * NIC * 128],
                           sc["odqT_%s" % half][:, tb * NIC * 128:
                                                (tb + 1) * NIC * 128],
                           reads=[dep["odqT"]], writes=[d_oT[half]])
            for ow in range(NCORE):
                for ih in range(2):
                    for half in ("hi", "lo"):
                        for icc in range(NIC // 2):
                            ic = ih * (NIC // 2) + icc
                            c0_ = 2048 if half == "hi" else 2560
                            tr.dma("qs",
                                   s_wo[half][:, icc * 512:(icc + 1) * 512],
                                   sc["gath1"][ow * DM + ic * 128:
                                               ow * DM + (ic + 1) * 128,
                                               c0_:c0_ + 512],
                                   reads=[dep["g1"]], writes=[d_wo[half]])
                    for tbl in range(2):
                        facs = []
                        for icc in range(NIC // 2):
                            ic = ih * (NIC // 2) + icc
                            facs += mm3f(ps_a[tbl][:],
                                s_oT["hi"].ap()[
                                    :, (tbl * NIC + ic) * 128:
                                    (tbl * NIC + ic + 1) * 128],
                                s_oT["lo"].ap()[
                                    :, (tbl * NIC + ic) * 128:
                                    (tbl * NIC + ic + 1) * 128],
                                s_wo["hi"].ap()[:, icc * 512:
                                                (icc + 1) * 512],
                                s_wo["lo"].ap()[:, icc * 512:
                                                (icc + 1) * 512],
                                ih == 0 and icc == 0,
                                ih == 1 and icc == NIC // 2 - 1)
                        tr.op_multi("te", facs,
                                    reads=[d_oT["hi"], d_oT["lo"],
                                           d_wo["hi"], d_wo["lo"]],
                                    writes=[d_psa[tbl]])
                        if ih == 1:
                            tb = tp * 2 + tbl
                            tr.dma("qs", s_xr[:],
                                   io["x_my"][tb * 128:(tb + 1) * 128,
                                              ow * 512:(ow + 1) * 512],
                                   writes=[d_xr])
                            tr.op("ve", lambda e, _t=tbl, _o=ow:
                                  e.tensor_tensor(
                                      s_x1[_t][:, _o * 512:(_o + 1) * 512],
                                      ps_a[_t].ap()[:, :], s_xr.ap()[:, :],
                                      ALU.add),
                                  reads=[d_psa[tbl], d_xr],
                                  writes=[d_x1[tbl]])
            for tbl in range(2):
                tb = tp * 2 + tbl
                tr.dma("qs", sc["x1_s"][tb * 128:(tb + 1) * 128, :],
                       s_x1[tbl][:], reads=[d_x1[tbl]], writes=[dep["x1"]])
                emit_ln(tr, qk, s_x1[tbl], d_x1[tbl], s_xn2, d_xn2)
                emit_split(tr, "ve", s_xn2[:], d_xn2, s_nh[:], d_nh,
                           s_nl[:], d_nl)
                tr.dma("qs", sc["xn2n_hi"][tb * 128:(tb + 1) * 128, :],
                       s_nh[:], reads=[d_nh], writes=[dep["xn2n"]])
                tr.dma("qs", sc["xn2n_lo"][tb * 128:(tb + 1) * 128, :],
                       s_nl[:], reads=[d_nl], writes=[dep["xn2n"]])
        tr.fence()
        s_tx2 = nc_sb(nc, c1, "t_tx2", [128, TPC], BF16)
        d_tx2 = Dep()
        for half in ("hi", "lo"):
            for ic in range(NIC):
                tr.dma("qa", s_tx2[:],
                       sc["xn2n_%s" % half][:, ic * 128:(ic + 1) * 128],
                       reads=[dep["xn2n"]], writes=[d_tx2], transpose=True)
                for tb in range(TPC // 128):
                    tr.dma("qs",
                           sc["xn2T_%s" % half][
                               :, tb * (NIC * 128) + ic * 128:
                               tb * (NIC * 128) + (ic + 1) * 128],
                           s_tx2[:, tb * 128:(tb + 1) * 128],
                           reads=[d_tx2], writes=[dep["xn2T"]])

    with ExitStack() as cl:
        lora = phase_lora_prep(nc, tr, cl, io, sc, dep, s_ident, d_ident)
        phase_ffn_a(nc, tr, cl, io, sc, dep, lora, s_ident, d_ident)
    with ExitStack() as cw:
        phase_ffn_b(nc, tr, cw, io, sc, dep, s_ident, d_ident)


def phase_ffn_a(nc, tr, cl, io, sc, dep, lora, s_ident, d_ident):
    NHW = (HID + 511) // 512
    s_xT = {half: nc_sb(nc, cl, "f_xT" + half, [128, NIC * 128], BF16)
            for half in ("hi", "lo")}
    d_xT = {half: Dep() for half in ("hi", "lo")}
    a1h, a1l, da1h, da1l = lora["a1"]
    a3h, a3l, da3h, da3l = lora["a3"]
    ps_xa = nc_ps(nc, cl, "f_psxa", [128, 2 * RK])
    d_psxa = Dep()
    s_xa = nc_sb(nc, cl, "f_xa", [128, 2 * RK])
    d_xa = Dep()
    ps_xat = nc_ps(nc, cl, "f_psxat", [RK, 128])
    d_psxat = Dep()
    s_xat = {i: nc_sb(nc, cl, "f_xat%d" % i, [RK, TPC]) for i in (0, 1)}
    d_xat = {i: Dep() for i in (0, 1)}
    s_xath = {i: nc_sb(nc, cl, "f_xath%d" % i, [RK, TPC], BF16)
              for i in (0, 1)}
    s_xatl = {i: nc_sb(nc, cl, "f_xatl%d" % i, [RK, TPC], BF16)
              for i in (0, 1)}
    d_xath = {i: Dep() for i in (0, 1)}
    d_xatl = {i: Dep() for i in (0, 1)}
    for tb in range(TPC // 128):
        for half in ("hi", "lo"):
            tr.dma("qs", s_xT[half][:],
                   sc["xn2T_%s" % half][:, tb * NIC * 128:
                                        (tb + 1) * NIC * 128],
                   reads=[dep["xn2T"]], writes=[d_xT[half]])
        facs = []
        for ic in range(NIC):
            for ri, (rh, rl, cr) in enumerate(
                    ((a1h, a1l, 0), (a3h, a3l, RK))):
                combos = ((s_xT["hi"], rh), (s_xT["hi"], rl),
                          (s_xT["lo"], rh))
                for li, (lw, rx) in enumerate(combos):
                    first = (ic == 0 and ri == 0 and li == 0)
                    last = (ic == NIC - 1 and ri == 1 and li == 2)
                    facs.append(lambda e, _lw=lw, _rx=rx, _ic=ic, _cr=cr,
                                _f=first, _e=last:
                                e.matmul(ps_xa[:, _cr:_cr + RK],
                                         _lw.ap()[:, _ic * 128:
                                                  (_ic + 1) * 128],
                                         _rx.ap()[:, _ic * RK:
                                                  (_ic + 1) * RK],
                                         start=_f, stop=_e))
        tr.op_multi("te", facs,
                    reads=[d_xT["hi"], d_xT["lo"], da1h, da1l, da3h, da3l],
                    writes=[d_psxa])
        tr.op("se", lambda e: e.activation(
            s_xa[:], ps_xa[:], AF.Copy, scale=2.0),
            reads=[d_psxa], writes=[d_xa])
        for i in (0, 1):
            tr.op("te", lambda e, _i=i: e.transpose(
                ps_xat[:, :], s_xa.ap()[:, _i * RK:(_i + 1) * RK],
                s_ident[:]),
                reads=[d_xa, d_ident], writes=[d_psxat])
            tr.op("ve", lambda e, _t=tb, _i=i: e.tensor_copy(
                s_xat[_i][:, _t * 128:(_t + 1) * 128], ps_xat[:, :]),
                reads=[d_psxat], writes=[d_xat[i]])
    for i in (0, 1):
        emit_split(tr, "ve", s_xat[i][:], d_xat[i], s_xath[i][:],
                   d_xath[i], s_xatl[i][:], d_xatl[i])

    s_wf = {half: nc_sb(nc, cl, "f_wf" + half, [128, NIC * 512], BF16)
            for half in ("hi", "lo")}
    d_wf = {(half, g): Dep() for half in ("hi", "lo") for g in range(2)}
    s_bT = {half: nc_sb(nc, cl, "f_bT" + half, [RK, 512], BF16)
            for half in ("hi", "lo")}
    d_bT = {half: Dep() for half in ("hi", "lo")}
    s_xT2 = [{half: nc_sb(nc, cl, "f_xT2%s%d" % (half, i),
                          [128, NIC * 128], BF16) for half in ("hi", "lo")}
             for i in range(2)]
    d_xT2 = [{half: Dep() for half in ("hi", "lo")} for i in range(2)]
    s_h1 = nc_sb(nc, cl, "f_h1", [128, 8 * 512])
    d_h1 = Dep()
    s_g = nc_sb(nc, cl, "f_g", [128, 512])
    d_g = Dep()
    s_gh = nc_sb(nc, cl, "f_gh", [128, 512], BF16)
    s_gl = nc_sb(nc, cl, "f_gl", [128, 512], BF16)
    d_gh, d_gl = Dep(), Dep()
    ps_h = nc_ps(nc, cl, "f_psh", [128, 512])
    d_psh = Dep()
    GC2 = NIC // 2
    for hw in range(NHW):
        W = min(512, HID - 512 * hw)
        for sub, wsrc, bnm in ((0, "w1n", "b1T"), (1, "w3n", "b3T")):
            for g in range(2):
                for half in ("hi", "lo"):
                    for icc in range(GC2):
                        ic = g * GC2 + icc
                        tr.dma("qa", s_wf[half][:, ic * 512:ic * 512 + W],
                               sc["%s_%s" % (wsrc, half)][
                                   hw * 512:hw * 512 + W,
                                   ic * 128:(ic + 1) * 128],
                               reads=[dep[wsrc]], writes=[d_wf[half, g]],
                               transpose=True)
            for half in ("hi", "lo"):
                tr.dma("qs", s_bT[half][:, :W],
                       sc["%s_%s" % (bnm, half)][:, hw * 512:hw * 512 + W],
                       reads=[dep[bnm]], writes=[d_bT[half]])
            for tb in range(TPC // 128):
                bi = tb % 2
                for half in ("hi", "lo"):
                    tr.dma("qs", s_xT2[bi][half][:],
                           sc["xn2T_%s" % half][:, tb * NIC * 128:
                                                (tb + 1) * NIC * 128],
                           reads=[dep["xn2T"]],
                           writes=[d_xT2[bi][half]])
                for g in range(2):
                    facs = []
                    for icc in range(GC2):
                        ic = g * GC2 + icc
                        facs += mm3f(ps_h[:, :W],
                            s_xT2[bi]["hi"].ap()[:, ic * 128:(ic + 1) * 128],
                            s_xT2[bi]["lo"].ap()[:, ic * 128:(ic + 1) * 128],
                            s_wf["hi"].ap()[:, ic * 512:ic * 512 + W],
                            s_wf["lo"].ap()[:, ic * 512:ic * 512 + W],
                            ic == 0, False)
                    reads = [d_xT2[bi]["hi"], d_xT2[bi]["lo"],
                             d_wf["hi", g], d_wf["lo", g]]
                    if g == 1:
                        combos = ((s_xath[sub], s_bT["hi"]),
                                  (s_xath[sub], s_bT["lo"]),
                                  (s_xatl[sub], s_bT["hi"]))
                        for li, (lw, rx) in enumerate(combos):
                            facs.append(lambda e, _lw=lw, _rx=rx, _tb=tb,
                                        _W=W, _e=(li == 2):
                                        e.matmul(ps_h[:, :_W],
                                                 _lw.ap()[:, _tb * 128:
                                                          (_tb + 1) * 128],
                                                 _rx.ap()[:, :_W],
                                                 start=False, stop=_e))
                        reads += [d_xath[sub], d_xatl[sub],
                                  d_bT["hi"], d_bT["lo"]]
                    tr.op_multi("te", facs, reads=reads, writes=[d_psh])
                if sub == 0:
                    tr.op("se", lambda e, _t=tb, _W=W: e.activation(
                        s_h1[:, _t * 512:_t * 512 + _W], ps_h[:, :_W],
                        AF.Silu), reads=[d_psh], writes=[d_h1])
                else:
                    tr.op("ve", lambda e, _t=tb, _W=W: e.tensor_tensor(
                        s_g[:, :_W], s_h1.ap()[:, _t * 512:_t * 512 + _W],
                        ps_h.ap()[:, :_W], ALU.mult),
                        reads=[d_h1, d_psh], writes=[d_g])
                    emit_split(tr, "ve", s_g[:, :W], d_g, s_gh[:, :W],
                               d_gh, s_gl[:, :W], d_gl)
                    tr.dma("qs", sc["gn_hi"][tb * 128:(tb + 1) * 128,
                                             hw * 512:hw * 512 + W],
                           s_gh[:, :W], reads=[d_gh], writes=[dep["gn"]])
                    tr.dma("qs", sc["gn_lo"][tb * 128:(tb + 1) * 128,
                                             hw * 512:hw * 512 + W],
                           s_gl[:, :W], reads=[d_gl], writes=[dep["gn"]])


def phase_ffn_b(nc, tr, cw, io, sc, dep, s_ident, d_ident):
    tr.fence()
    s_tg = nc_sb(nc, cw, "f_tg", [128, TPC], BF16)
    d_tg = Dep()
    for half in ("hi", "lo"):
        for hc in range(NHC):
            tr.dma("qa", s_tg[:],
                   sc["gn_%s" % half][:, hc * 128:(hc + 1) * 128],
                   reads=[dep["gn"]], writes=[d_tg], transpose=True)
            for tb in range(TPC // 128):
                tr.dma("qs",
                       sc["gT_%s" % half][:, tb * (NHC * 128) + hc * 128:
                                          tb * (NHC * 128) + (hc + 1) * 128],
                       s_tg[:, tb * 128:(tb + 1) * 128],
                       reads=[d_tg], writes=[dep["gT"]])

    s_tmp = nc_sb(nc, cw, "fb_tmp", [128, 128])
    d_tmp = Dep()
    t_ps = nc_ps(nc, cw, "fb_ps", [128, 128])
    d_ps = Dep()
    s_a2src = nc_sb(nc, cw, "fb_a2src", [RK, 128])
    d_a2src = Dep()
    a2h = nc_sb(nc, cw, "fb_a2h", [128, NHC * RK], BF16)
    a2l = nc_sb(nc, cw, "fb_a2l", [128, NHC * RK], BF16)
    da2h, da2l = Dep(), Dep()
    for c in range(NHC):
        tr.dma("qs", s_a2src[:], io["a2"][:, c * 128:(c + 1) * 128],
               writes=[d_a2src])
        tr.op("te", lambda e: e.transpose(
            t_ps[:, :RK], s_a2src.ap()[:, :],
            s_ident.ap()[0:RK, 0:RK]),
            reads=[d_a2src, d_ident], writes=[d_ps])
        tr.op("ve", lambda e: e.tensor_copy(s_tmp[:, :RK], t_ps[:, :RK]),
              reads=[d_ps], writes=[d_tmp])
        emit_split(tr, "ve", s_tmp[:, :RK], d_tmp,
                   a2h[:, c * RK:(c + 1) * RK], da2h,
                   a2l[:, c * RK:(c + 1) * RK], da2l)

    s_gT = {half: nc_sb(nc, cw, "fb_gT" + half, [128, NHC * 128], BF16)
            for half in ("hi", "lo")}
    d_gT = {(half, g): Dep() for half in ("hi", "lo") for g in range(2)}
    ps_ga = nc_ps(nc, cw, "fb_psga", [128, RK])
    d_psga = Dep()
    s_ga = nc_sb(nc, cw, "fb_ga", [128, RK])
    d_ga = Dep()
    ps_gat = nc_ps(nc, cw, "fb_psgat", [RK, 128])
    d_psgat = Dep()
    s_gat = nc_sb(nc, cw, "fb_gat", [RK, TPC])
    d_gat = Dep()
    s_gath = nc_sb(nc, cw, "fb_gath", [RK, TPC], BF16)
    s_gatl = nc_sb(nc, cw, "fb_gatl", [RK, TPC], BF16)
    d_gath, d_gatl = Dep(), Dep()
    for tb in range(TPC // 128):
        for half in ("hi", "lo"):
            tr.dma("qs", s_gT[half][:],
                   sc["gT_%s" % half][:, tb * NHC * 128:
                                      (tb + 1) * NHC * 128],
                   reads=[dep["gT"]],
                   writes=[d_gT[half, 0], d_gT[half, 1]])
        facs = []
        for hc in range(NHC):
            combos = ((s_gT["hi"], a2h), (s_gT["hi"], a2l),
                      (s_gT["lo"], a2h))
            for li, (lw, rx) in enumerate(combos):
                facs.append(lambda e, _lw=lw, _rx=rx, _hc=hc,
                            _f=(hc == 0 and li == 0),
                            _e=(hc == NHC - 1 and li == 2):
                            e.matmul(ps_ga[:],
                                     _lw.ap()[:, _hc * 128:(_hc + 1) * 128],
                                     _rx.ap()[:, _hc * RK:(_hc + 1) * RK],
                                     start=_f, stop=_e))
        tr.op_multi("te", facs,
                    reads=[d_gT["hi", 0], d_gT["hi", 1],
                           d_gT["lo", 0], d_gT["lo", 1], da2h, da2l],
                    writes=[d_psga])
        tr.op("se", lambda e: e.activation(
            s_ga[:], ps_ga[:], AF.Copy, scale=2.0),
            reads=[d_psga], writes=[d_ga])
        tr.op("te", lambda e: e.transpose(
            ps_gat[:, :], s_ga.ap()[:, :], s_ident[:]),
            reads=[d_ga, d_ident], writes=[d_psgat])
        tr.op("ve", lambda e, _t=tb: e.tensor_copy(
            s_gat[:, _t * 128:(_t + 1) * 128], ps_gat[:, :]),
            reads=[d_psgat], writes=[d_gat])
    emit_split(tr, "ve", s_gat[:], d_gat, s_gath[:], d_gath,
               s_gatl[:], d_gatl)

    OW = 256
    NOW = DM // OW
    s_w2 = {half: nc_sb(nc, cw, "fb_w2" + half, [128, NHC * OW],
                        BF16) for half in ("hi", "lo")}
    d_w2 = {(half, g): Dep() for half in ("hi", "lo") for g in range(2)}
    s_b2 = {half: nc_sb(nc, cw, "fb_b2" + half, [RK, OW], BF16)
            for half in ("hi", "lo")}
    d_b2 = {half: Dep() for half in ("hi", "lo")}
    s_x1t = nc_sb(nc, cw, "fb_x1t", [128, OW])
    d_x1t = Dep()
    s_out = nc_sb(nc, cw, "fb_out", [128, OW])
    d_out = Dep()
    ps_o2 = nc_ps(nc, cw, "fb_pso2", [128, OW])
    d_pso2 = Dep()
    for ow in range(NOW):
        for half in ("hi", "lo"):
            tr.dma("qs", s_b2[half][:, :],
                   sc["b2T_%s" % half][:, ow * OW:(ow + 1) * OW],
                   reads=[dep["b2T"]], writes=[d_b2[half]])
            for g in range(2):
                for sub in range(4):
                    h0 = g * 43 + sub * 11
                    h1_ = min(g * 43 + (sub + 1) * 11, (g + 1) * 43, NHC)
                    if h0 >= h1_:
                        continue
                    srcap = sc["w2T_%s" % half][
                        h0 * 128:h1_ * 128,
                        ow * OW:(ow + 1) * OW].rearrange(
                        "(i p) j -> p i j", p=128)
                    dstap = s_w2[half].ap()[
                        :, h0 * OW:h1_ * OW].rearrange(
                        "p (i j) -> p i j", j=OW)
                    tr.dma("qs", dstap, srcap,
                           reads=[dep["w2T"]], writes=[d_w2[half, g]])
        for tb in range(TPC // 128):
            for g in range(2):
                h0, h1_ = g * 43, min((g + 1) * 43, NHC)
                for half in ("hi", "lo"):
                    tr.dma("qs",
                           s_gT[half][:, h0 * 128:h1_ * 128],
                           sc["gT_%s" % half][:, tb * NHC * 128 + h0 * 128:
                                              tb * NHC * 128 + h1_ * 128],
                           reads=[dep["gT"]], writes=[d_gT[half, g]])
            for g in range(2):
                h0, h1_ = g * 43, min((g + 1) * 43, NHC)
                facs = []
                for hc in range(h0, h1_):
                    facs += mm3f(ps_o2[:],
                        s_gT["hi"].ap()[:, hc * 128:(hc + 1) * 128],
                        s_gT["lo"].ap()[:, hc * 128:(hc + 1) * 128],
                        s_w2["hi"].ap()[:, hc * OW:(hc + 1) * OW],
                        s_w2["lo"].ap()[:, hc * OW:(hc + 1) * OW],
                        hc == 0, False)
                reads = [d_gT["hi", g], d_gT["lo", g], d_w2["hi", g],
                         d_w2["lo", g]]
                if g == 1:
                    combos = ((s_gath, s_b2["hi"]), (s_gath, s_b2["lo"]),
                              (s_gatl, s_b2["hi"]))
                    for li, (lw, rx) in enumerate(combos):
                        facs.append(lambda e, _lw=lw, _rx=rx, _tb=tb,
                                    _e=(li == 2):
                                    e.matmul(ps_o2[:],
                                             _lw.ap()[:, _tb * 128:
                                                      (_tb + 1) * 128],
                                             _rx.ap()[:, :],
                                             start=False, stop=_e))
                    reads += [d_gath, d_gatl, d_b2["hi"], d_b2["lo"]]
                tr.op_multi("te", facs, reads=reads, writes=[d_pso2])
            tr.dma("qs", s_x1t[:],
                   sc["x1_s"][tb * 128:(tb + 1) * 128,
                              ow * OW:(ow + 1) * OW],
                   reads=[dep["x1"]], writes=[d_x1t])
            tr.op("ve", lambda e: e.tensor_tensor(
                s_out[:], ps_o2.ap()[:, :], s_x1t.ap()[:, :], ALU.add),
                reads=[d_pso2, d_x1t], writes=[d_out])
            tr.dma("qs",
                   io["out"][tb * 128:(tb + 1) * 128, ow * OW:(ow + 1) * OW],
                   s_out[:], reads=[d_out], writes=[dep["out"]])


_NC_CACHE = [None]
_LAST_RES = [None]


def _host_consts():
    import jax
    import jax.numpy as jnp
    cpu = jax.devices("cpu")[0]
    with jax.default_device(cpu):
        inv = 1.0 / (10000.0 ** (jnp.arange(0, HD, 2, dtype=jnp.float32)
                                 / HD))
        ang = jnp.arange(SEQ, dtype=jnp.float32)[:, None] * inv[None, :]
        cos = np.ascontiguousarray(np.asarray(jnp.cos(ang)).T)
        sin = np.ascontiguousarray(np.asarray(jnp.sin(ang)).T)
    cosF = np.ascontiguousarray(np.vstack([cos, cos]))
    sinF2 = np.ascontiguousarray(np.vstack([-sin, sin]))
    mask = np.where(np.arange(896)[None, :] <= np.arange(128)[:, None] + 384,
                    np.float32(0.0), np.float32(NEGBIG)).astype(np.float32)
    ident = np.eye(128, dtype=np.float32)
    perm = np.zeros((128, 128), dtype=np.float32)
    for m in range(128):
        perm[(m + 64) % 128, m] = 1.0
    return cosF, sinF2, mask, ident, perm


def kernel(L0_attn_hidden_states, L0_attn_Wq, L0_attn_Wk, L0_attn_Wv,
           L0_attn_Wo, L0_ffn_w1, L0_ffn_w3, L0_ffn_w2, L0_ffn_lora_A1,
           L0_ffn_lora_B1, L0_ffn_lora_A3, L0_ffn_lora_B3, L0_ffn_lora_A2,
           L0_ffn_lora_B2):
    f32 = lambda a: np.ascontiguousarray(np.asarray(a, dtype=np.float32))
    x = f32(L0_attn_hidden_states).reshape(TOK, DM)
    wq, wk, wv, wo = (f32(L0_attn_Wq), f32(L0_attn_Wk), f32(L0_attn_Wv),
                      f32(L0_attn_Wo))
    w1, w3, w2 = f32(L0_ffn_w1), f32(L0_ffn_w3), f32(L0_ffn_w2)
    a1, b1 = f32(L0_ffn_lora_A1), f32(L0_ffn_lora_B1)
    a3, b3 = f32(L0_ffn_lora_A3), f32(L0_ffn_lora_B3)
    a2, b2 = f32(L0_ffn_lora_A2), f32(L0_ffn_lora_B2)
    cosF, sinF2, mask, ident, perm = _host_consts()

    if _NC_CACHE[0] is None:
        _NC_CACHE[0] = build_nc()
    nc = _NC_CACHE[0]

    in_maps = []
    for c in range(NCORE):
        hs = slice(c * HPC * HD, (c + 1) * HPC * HD)
        in_maps.append({
            "x_my": x[c * TPC:(c + 1) * TPC],
            "wq_my": wq[hs], "wk_my": wk[hs], "wv_my": wv[hs],
            "wo_my": wo[hs],
            "w1": w1, "w3": w3, "w2": w2,
            "a1": a1, "b1": b1, "a3": a3, "b3": b3, "a2": a2, "b2": b2,
            "cosF": cosF, "sinF2": sinF2, "maskS": mask, "identf": ident,
            "permF": perm,
        })
    res = run_bass_kernel_spmd(nc, in_maps, core_ids=list(range(NCORE)))
    _LAST_RES[0] = res.results
    outs = np.concatenate([np.asarray(res.results[c]["out"])
                           for c in range(NCORE)], axis=0)
    return outs.reshape(BATCH, SEQ, DM)
